# version 03
# baseline (speedup 1.0000x reference)
import numpy as np
from contextlib import ExitStack

import concourse.bass as bass
import concourse.mybir as mybir
from concourse.bass_utils import run_bass_kernel_spmd

F32 = mybir.dt.float32
BF16 = mybir.dt.bfloat16
AF = mybir.ActivationFunctionType
ALU = mybir.AluOpType
AX = mybir.AxisListType

NCORES = 8
B, T, D = 2, 8192, 1024
NTOK = B * T
TS = NTOK // NCORES
P = 128
EPS = 1e-5

F = np.float32


class Prog:
    ENGINES = ("sync", "vector", "scalar", "gpsimd", "tensor")

    def __init__(self, name):
        self.name = name
        self.nc = bass.Bass("TRN2", target_bir_lowering=False)
        self.ops = {e: [] for e in self.ENGINES}
        self.sem_counts = {}
        self.sem_names = []
        self.ctx = ExitStack()
        self._finalized = False

    def din(self, name, shape, dtype=F32):
        return self.nc.dram_tensor(name, list(shape), dtype, kind="ExternalInput").ap()

    def dout(self, name, shape, dtype=F32):
        return self.nc.dram_tensor(name, list(shape), dtype, kind="ExternalOutput").ap()

    def dtmp(self, name, shape, dtype=F32):
        return self.nc.dram_tensor(name, list(shape), dtype, kind="Internal").ap()

    def sb(self, name, shape, dtype=F32):
        return self.ctx.enter_context(self.nc.sbuf_tensor(name, list(shape), dtype))

    def ps(self, name, shape, dtype=F32):
        return self.ctx.enter_context(self.nc.psum_tensor(name, list(shape), dtype))

    def sem(self, name):
        if name not in self.sem_counts:
            self.sem_counts[name] = 0
            self.sem_names.append(name)
        return name

    CHAINED = ("vector", "scalar", "gpsimd")

    def op(self, eng, fn, wait=(), inc=None, inc_n=1, chain=True):
        _w = {}
        for w in wait:
            if w is not None:
                _w[w[0]] = max(_w.get(w[0], 0), w[1])
        wait = sorted(_w.items())
        if chain and eng in self.CHAINED:
            assert inc is None, "chained engines use their chain sem as milestone"
            csem = self.sem("c_" + eng)
            wait.append((csem, self.sem_counts[csem]))
            self.sem_counts[csem] += 1
            tok = (csem, self.sem_counts[csem])
            self.ops[eng].append((fn, tuple(wait), (csem, 1)))
            return tok
        if inc is not None:
            self.sem(inc)
            self.sem_counts[inc] += inc_n
            tok = (inc, self.sem_counts[inc])
            self.ops[eng].append((fn, tuple(wait), (inc, inc_n)))
            return tok
        self.ops[eng].append((fn, tuple(wait), None))
        return None

    def barrier(self, eng, wait):
        self.op(eng, lambda e: None, wait=wait, chain=False)

    def dma(self, eng, dst, src, wait=(), inc="ld"):
        return self.op(eng, lambda e: e.dma_start(dst, src), wait=wait, inc=inc,
                       inc_n=16, chain=False)

    def count(self, sem):
        return self.sem_counts.get(sem, 0)

    def finalize(self):
        assert not self._finalized
        self._finalized = True
        nc = self.nc
        semh = {n: self.ctx.enter_context(nc.semaphore(n)) for n in self.sem_names}
        block = self.ctx.enter_context(nc.Block())

        def emit(engine_obj, eng_name):
            for fn, waits, inc in self.ops[eng_name]:
                for (sname, cnt) in waits:
                    if cnt > 0:
                        engine_obj.wait_ge(semh[sname], cnt)
                inst = fn(engine_obj)
                if inc is not None:
                    assert inst is not None, f"op with inc on {eng_name} returned None"
                    inst.then_inc(semh[inc[0]], inc[1])

        if self.ops["sync"]:
            @block.sync
            def _(sync):
                emit(sync, "sync")
        if self.ops["vector"]:
            @block.vector
            def _(vector):
                emit(vector, "vector")
        if self.ops["scalar"]:
            @block.scalar
            def _(scalar):
                emit(scalar, "scalar")
        if self.ops["gpsimd"]:
            @block.gpsimd
            def _(gpsimd):
                emit(gpsimd, "gpsimd")
        if self.ops["tensor"]:
            @block.tensor
            def _(tensor):
                emit(tensor, "tensor")
        self.ctx.close()
        return nc


def emit_ln(p, x_t, out_t, scr, nrows=P, ncols=D, ld_wait=(), gain_bias=None, eps=EPS):
    p.op("vector", lambda e: e.tensor_reduce(scr["sum1"][:nrows], x_t[:nrows], AX.X, ALU.add), wait=ld_wait)
    p.op("vector", lambda e: e.tensor_scalar(scr["nm"][:nrows], scr["sum1"][:nrows], -1.0 / ncols, None, ALU.mult))
    c = p.op("vector", lambda e: e.tensor_scalar(scr["xm"][:nrows], x_t[:nrows], scr["nm"][:nrows], None, ALU.add))
    cs = p.op("scalar", lambda e: e.activation(scr["sq"][:nrows], scr["xm"][:nrows], AF.Square, accum_out=scr["rr"][:nrows]),
              wait=[c])
    cv = p.op("vector", lambda e: e.tensor_scalar(scr["rr"][:nrows], scr["rr"][:nrows], 1.0 / ncols, eps, ALU.mult, ALU.add),
              wait=[cs])
    cs = p.op("scalar", lambda e: e.activation(scr["sd"][:nrows], scr["rr"][:nrows], AF.Sqrt), wait=[cv])
    p.op("vector", lambda e: e.reciprocal(scr["rstd"][:nrows], scr["sd"][:nrows]), wait=[cs])
    if gain_bias is None:
        return p.op("vector", lambda e: e.tensor_scalar(out_t[:nrows], scr["xm"][:nrows], scr["rstd"][:nrows], None, ALU.mult))
    gain, bias = gain_bias
    p.op("vector", lambda e: e.tensor_scalar(scr["xm"][:nrows], scr["xm"][:nrows], scr["rstd"][:nrows], None, ALU.mult))
    p.op("vector", lambda e: e.tensor_tensor(scr["xm"][:nrows], scr["xm"][:nrows], gain[:nrows], ALU.mult))
    return p.op("vector", lambda e: e.tensor_tensor(out_t[:nrows], scr["xm"][:nrows], bias[:nrows], ALU.add))


def build_prep():
    p = Prog("prep")
    x_in = p.din("x_sl", [TS, D])
    nw_in = p.din("norm_w", [1, D])
    xn_out = p.dout("xn_sl", [TS, D])
    h_out = p.dout("h_sl", [TS, D])

    ntiles = TS // P
    xt = [p.sb(f"xt{i}", [P, D]) for i in range(2)]
    xnt = p.sb("xnt", [P, D])
    ht = p.sb("ht", [P, D])
    nwb = p.sb("nwb", [P, D])
    scr = {k: p.sb("scr_" + k, [P, D if k in ("sq", "xm", "xmr", "xg") else 1])
           for k in ("sum1", "nm", "rr", "sd", "rstd", "sq", "xm", "xmr", "xg")}
    scr2 = {"rr2": p.sb("rr2", [P, 1]), "sd2": p.sb("sd2", [P, 1]), "rstd2": p.sb("rstd2", [P, 1]),
            "sq2": p.sb("sq2", [P, D]), "hpre": p.sb("hpre", [P, D])}

    cnw = p.dma("sync", nwb[:], nw_in.broadcast_to([P, D]))
    for i in range(ntiles):
        buf = xt[i % 2]
        cl = p.dma("sync", buf[:], x_in[i * P:(i + 1) * P, :])
        cxn = emit_ln(p, buf, xnt, scr, ld_wait=[cl, cnw])
        cs = p.op("scalar", lambda e: e.activation(scr2["sq2"][:], xnt[:], AF.Square, accum_out=scr2["rr2"][:]),
                  wait=[cxn])
        cv = p.op("vector", lambda e: e.tensor_scalar(scr2["rr2"][:], scr2["rr2"][:], 1.0 / D, EPS, ALU.mult, ALU.add),
                  wait=[cs])
        cs = p.op("scalar", lambda e: e.activation(scr2["sd2"][:], scr2["rr2"][:], AF.Sqrt), wait=[cv])
        p.op("vector", lambda e: e.reciprocal(scr2["rstd2"][:], scr2["sd2"][:]), wait=[cs])
        ch = p.op("vector", lambda e: e.scalar_tensor_tensor(
            ht[:], xnt[:], scr2["rstd2"][:], nwb[:], ALU.mult, ALU.mult))
        c1 = p.dma("sync", xn_out[i * P:(i + 1) * P, :], xnt[:], wait=[cxn], inc="st")
        c2 = p.dma("sync", h_out[i * P:(i + 1) * P, :], ht[:], wait=[ch], inc="st")
        p.barrier("vector", [c2])
        p.barrier("scalar", [c2])
    p.barrier("sync", [c2])
    return p.finalize()


def mm_group(p, psum_ap, mms, psum_free=None):
    n = len(mms)
    tok = None
    for i, (l, r, w) in enumerate(mms):
        w = list(w)
        if i == 0 and psum_free is not None:
            w.append(psum_free)
        tok = p.op("tensor",
                   (lambda e, l=l, r=r, s=(i == 0), t=(i == n - 1):
                    e.matmul(psum_ap, l, r, start=s, stop=t)),
                   wait=w, inc="pe" if i == n - 1 else None, chain=False)
    return tok


def _butterfly_stage_blocks(weights):
    w = np.asarray(weights, dtype=F)
    lo = np.zeros((6, 64, 64, 64), F)
    hi = np.zeros((6, 64, 64, 64), F)
    for k in range(12):
        s = 2 ** k
        nb = 4096 // (2 * s)
        wk = w[k].reshape(nb, s, 2, 2)
        b_idx = np.arange(nb)[:, None]
        pos = np.arange(s)[None, :]
        i0 = b_idx * 2 * s + pos
        i1 = i0 + s
        if k < 6:
            o = i0 // 64
            n0, n1 = i0 % 64, i1 % 64
            lo[k, o, n0, n0] = wk[..., 0, 0]
            lo[k, o, n0, n1] = wk[..., 0, 1]
            lo[k, o, n1, n0] = wk[..., 1, 0]
            lo[k, o, n1, n1] = wk[..., 1, 1]
        else:
            n = i0 % 64
            o0, o1 = i0 // 64, i1 // 64
            hi[k - 6, n, o0, o0] = wk[..., 0, 0]
            hi[k - 6, n, o0, o1] = wk[..., 0, 1]
            hi[k - 6, n, o1, o0] = wk[..., 1, 0]
            hi[k - 6, n, o1, o1] = wk[..., 1, 1]
    return lo, hi


def _pair_blockdiag(blocks):
    t = np.zeros((32, 128, 128), F)
    t[:, :64, :64] = blocks[0::2]
    t[:, 64:, 64:] = blocks[1::2]
    return t


def stage_fold_inputs(up_weights, down_weights, W_exp, b_exp, W_con):
    lo_u, hi_u = _butterfly_stage_blocks(up_weights)
    lo_d, hi_d = _butterfly_stage_blocks(down_weights)
    tr = lambda b: np.ascontiguousarray(np.transpose(b, (0, 2, 1)))
    ch = np.zeros((4, 6, 32, 128, 128), F)
    ch[0, 0] = _pair_blockdiag(tr(lo_u[5]))
    for j in range(1, 6):
        ch[0, j] = _pair_blockdiag(lo_u[5 - j])
    ch[1, 0] = _pair_blockdiag(tr(hi_u[5]))
    for j in range(1, 6):
        ch[1, j] = _pair_blockdiag(hi_u[5 - j])
    ch[2, 0] = _pair_blockdiag(hi_d[0])
    for j in range(1, 6):
        ch[2, j] = _pair_blockdiag(tr(hi_d[j]))
    ch[3, 0] = _pair_blockdiag(lo_d[0])
    for j in range(1, 6):
        ch[3, j] = _pair_blockdiag(tr(lo_d[j]))

    wexpT_ext = np.zeros((4096, 1032), F)
    wexpT_ext[:, :D] = np.asarray(W_exp, F).T
    wexpT_ext[:, D] = np.asarray(b_exp, F)
    wcon = np.asarray(W_con, F)
    rows = (np.arange(4096).reshape(64, 64).T).reshape(-1)
    wcon_perm = np.ascontiguousarray(wcon[rows])
    return {"fold_ch": ch, "wexpT_ext": wexpT_ext, "wcon_perm": wcon_perm}


def build_fold():
    p = Prog("fold")
    ch_in = p.din("fold_ch", [4, 6, 32, 128, 128])
    wx_in = p.din("wexpT_ext", [4096, 1032])
    wc_in = p.din("wcon_perm", [4096, 1024])
    wxf_out = p.dout("wexpfT", [4096, 1032])
    wcf_out = p.dout("wconf", [4096, 1024])
    ftmp = p.dtmp("ftmp", [4096, 1032])

    chain_sb = p.sb("chain_sb", [P, 4 * 32 * P])
    stg = p.sb("stg", [P, 6 * P])
    src = [p.sb(f"src{i}", [P, 1032]) for i in range(2)]
    osb = [p.sb(f"osb{i}", [P, 1032]) for i in range(2)]
    pst = [p.ps(f"pst{i}", [P, 512]) for i in range(2)]

    ev_last = {0: None, 1: None}
    for c in range(4):
        for t in range(32):
            slot = chain_sb[:, (c * 32 + t) * P:(c * 32 + t + 1) * P]
            cl = p.dma("sync", stg[:].rearrange("r (j k) -> r j k", j=6),
                       ch_in[c, :, t].transpose([1, 0, 2]))
            cseed = p.op("vector", lambda e, slot=slot: e.tensor_copy(slot, stg[:, 0:P]), wait=[cl])
            for j in range(1, 6):
                pb = pst[j % 2]
                tok = mm_group(p, pb[:, 0:P],
                               [(stg[:, j * P:(j + 1) * P], slot, [cl, cseed])],
                               psum_free=ev_last[j % 2])
                cseed = p.op("vector", lambda e, slot=slot, pb=pb: e.tensor_copy(slot, pb[:, 0:P]), wait=[tok])
                ev_last[j % 2] = cseed
            p.barrier("sync", [tok])
    chains_done = cseed

    passes = [
        (wx_in, 0, 1032, ftmp, True),
        (ftmp, 1, 1032, wxf_out, True),
        (wc_in, 2, 1024, ftmp, True),
        (ftmp, 3, 1024, wcf_out, False),
    ]
    prev_pass_toks = [chains_done]
    for (src_ap, c, ncols, dst_ap, perm) in passes:
        tile_toks = []
        for t in range(32):
            sbuf_src = src[t % 2]
            sbuf_out = osb[t % 2]
            cl = p.dma("sync", sbuf_src[:, :ncols], src_ap[t * P:(t + 1) * P, :ncols],
                       wait=list(prev_pass_toks) + ([tile_toks[-2]] if len(tile_toks) >= 2 else []))
            stat = chain_sb[:, (c * 32 + t) * P:(c * 32 + t + 1) * P]
            war = [tile_toks[-2]] if len(tile_toks) >= 2 else []
            ev = None
            nch = (ncols + 511) // 512
            for q in range(nch):
                n0, n1 = q * 512, min((q + 1) * 512, ncols)
                pb = pst[q % 2]
                tok = mm_group(p, pb[:, 0:n1 - n0],
                               [(stat, sbuf_src[:, n0:n1], [cl, chains_done])],
                               psum_free=ev_last[q % 2])
                ev = p.op("vector", lambda e, pb=pb, sbuf_out=sbuf_out, n0=n0, n1=n1:
                          e.tensor_copy(sbuf_out[:, n0:n1], pb[:, 0:n1 - n0]),
                          wait=[tok] + war)
                ev_last[q % 2] = ev
            stsem = "st%d" % (t % 2)
            if perm:
                dst_v = dst_ap[:, 0:ncols].rearrange("(b a) c -> b a c", a=64)
                p.dma("sync", dst_v[:, 2 * t, :], sbuf_out[0:64, :ncols], wait=[ev], inc=stsem)
                stok = p.dma("sync", dst_v[:, 2 * t + 1, :], sbuf_out[64:128, :ncols],
                             wait=[ev], inc=stsem)
            else:
                stok = p.dma("sync", dst_ap[t * P:(t + 1) * P, :ncols], sbuf_out[:, :ncols],
                             wait=[ev], inc=stsem)
            tile_toks.append(stok)
        prev_pass_toks = [tile_toks[-1], tile_toks[-2]]
    p.barrier("sync", prev_pass_toks)
    return p.finalize()


def build_l0ffn():
    p = Prog("l0ffn")
    xnT_in = p.din("xnT_sl", [D, TS])
    wexpf_in = p.din("wexpf_tiled", [P, 8, 4096])
    wconf_in = p.din("wconf_tiled", [P, 32, D])
    gb1_in = p.din("gbias1_packed", [P, 32])
    gb2_in = p.din("gbias2_packed", [P, 32])
    cb_in = p.din("cbias2", [2, D])
    g0_out = p.dout("g0_sl", [TS, D])

    KT, MT = 8, 32
    xnTb_all = p.sb("xnTb_all", [P, KT * 512], BF16)
    wexpfb_all = p.sb("wexpfb_all", [P, KT * 4096], BF16)
    wconfb_all = p.sb("wconfb_all", [P, MT * D], BF16)
    xnTb = [xnTb_all[:, k * 512:(k + 1) * 512] for k in range(KT)]
    wexpfb = [wexpfb_all[:, k * 4096:(k + 1) * 4096] for k in range(KT)]
    wconfb = [wconfb_all[:, m * D:(m + 1) * D] for m in range(MT)]
    Hb = p.sb("Hb", [P, MT * 512], BF16)
    gbias = p.sb("gbias", [P, 32])
    cbias = p.sb("cbias", [P, D])
    cb_tmp = p.sb("cb_tmp", [P, D])
    ysb = p.sb("ysb", [P, D])
    gsb = p.sb("gsb", [P, D])
    pst = [p.ps(f"pst{i}", [P, 512]) for i in range(2)]

    w1 = None
    for q in range(4):
        w1 = p.dma("gpsimd", wexpfb_all[:].rearrange("p (k c) -> p k c", k=KT)[:, 2 * q:2 * q + 2],
                   wexpf_in[:, 2 * q:2 * q + 2], wait=[w1], inc="ldw1")
    w2 = None
    for q in range(4):
        w2 = p.dma("gpsimd", wconfb_all[:].rearrange("p (m c) -> p m c", m=MT)[:, 8 * q:8 * q + 8],
                   wconf_in[:, 8 * q:8 * q + 8], wait=[w2], inc="ldw2")
    b1 = p.dma("sync", gbias[:], gb1_in, inc="ld")
    b2 = p.dma("sync", cb_tmp[:, 0:32], gb2_in, wait=[b1], inc="ld")
    gb_tok = p.op("vector", lambda e: e.tensor_tensor(gbias[:], gbias[:], cb_tmp[:, 0:32], ALU.add),
                  wait=[b2])
    b3 = p.dma("sync", cbias[:], cb_in[0:1, :].broadcast_to([P, D]), wait=[b2], inc="ld")
    cld = p.dma("sync", cb_tmp[:], cb_in[1:2, :].broadcast_to([P, D]), wait=[gb_tok, b3], inc="ld")
    cbias_tok = p.op("vector", lambda e: e.tensor_tensor(cbias[:], cbias[:], cb_tmp[:], ALU.add),
                     wait=[cld])

    ev_last = {0: None, 1: None}
    prev_y_tok = None
    last_sil = None
    last_st = None
    for g in range(4):
        xg_tok = p.dma("gpsimd",
                       xnTb_all[:].rearrange("p (k c) -> p k c", k=KT),
                       xnT_in.rearrange("(k p) t -> p k t", p=P)[:, :, g * 512:(g + 1) * 512],
                       wait=[prev_y_tok], inc="ldx")
        h_done = None
        for m in range(MT):
            pb = pst[m % 2]
            mms = [(wexpfb[k][:, m * P:(m + 1) * P], xnTb[k], [xg_tok, w1]) for k in range(KT)]
            tok = mm_group(p, pb[:], mms, psum_free=ev_last[m % 2])
            h_done = p.op("scalar", lambda e, pb=pb, m=m: e.activation(
                Hb[:, m * 512:(m + 1) * 512], pb[:], AF.Gelu, bias=gbias[:, m:m + 1]),
                wait=[tok, gb_tok])
            ev_last[m % 2] = h_done
        for tsub in range(4):
            ytoks = []
            for nch in range(2):
                pb = pst[nch]
                mms = [(Hb[:, m * 512 + tsub * P: m * 512 + (tsub + 1) * P],
                        wconfb[m][:, nch * 512:(nch + 1) * 512], [h_done, w2]) for m in range(MT)]
                tok = mm_group(p, pb[:], mms, psum_free=ev_last[nch])
                ytok = p.op("vector", lambda e, pb=pb, nch=nch: e.tensor_tensor(
                    ysb[:, nch * 512:(nch + 1) * 512], pb[:], cbias[:, nch * 512:(nch + 1) * 512], ALU.add),
                    wait=[tok, cbias_tok, last_sil])
                ev_last[nch] = ytok
                ytoks.append(ytok)
            last_sil = p.op("scalar", lambda e: e.activation(gsb[:], ysb[:], AF.Silu),
                            wait=ytoks + [last_st])
            row0 = g * 512 + tsub * P
            last_st = p.dma("sync", g0_out[row0:row0 + P, :], gsb[:], wait=[last_sil], inc="st")
            prev_y_tok = tok
    p.barrier("sync", [last_st])
    return p.finalize()


def build_l0attn():
    p = Prog("l0attn")
    hT_in = p.din("hT_full", [D, NTOK])
    waT_in = p.din("waT_tiled", [P, 8, 2176])
    wvT_in = p.din("wvT_tiled", [P, 8, P])
    y_out = p.dout("y0_sl", [P, 1024, 16])
    A_hbm = p.dtmp("A_hbm", [P, 1024, 272])

    waTb = p.sb("waTb", [P, 8 * 2176], BF16)
    wvTb = p.sb("wvTb", [P, 8 * P], BF16)
    hTb = [p.sb(f"hTb{i}", [P, 8 * P], BF16) for i in range(2)]
    gates = p.sb("gates", [P, 2176])
    red = p.sb("red", [P, P])
    v_sb = p.sb("v_sb", [P, P])
    ps_g = [p.ps(f"ps_g{i}", [P, 512]) for i in range(2)]
    ps_v = p.ps("ps_v", [P, P])

    w1 = None
    for q in range(2):
        w1 = p.dma("gpsimd", waTb[:].rearrange("p (k c) -> p k c", k=8)[:, 4 * q:4 * q + 4],
                   waT_in[:, 4 * q:4 * q + 4], wait=[w1], inc="ldw1")
    w2 = p.dma("gpsimd", wvTb[:].rearrange("p (k c) -> p k c", k=8), wvT_in, inc="ldw2")

    ev_last = {0: None, 1: None}
    prev = {}
    last_store = None
    hT_v = hT_in.rearrange("(k p) t -> p k t", p=P)
    for b in range(2):
        for tt in range(64):
            t0 = b * T + tt * P
            hbuf = hTb[tt % 2]
            lx = p.dma("gpsimd", hbuf[:].rearrange("p (k c) -> p k c", k=8),
                       hT_v[:, :, t0:t0 + P], wait=[prev.get(("mmv", tt % 2))],
                       inc="ldx%d" % (tt % 2))
            etoks = []
            for q in range(5):
                n0, n1 = q * 512, min((q + 1) * 512, 2176)
                pb = ps_g[q % 2]
                mms = [(hbuf[:, k * P:(k + 1) * P], waTb[:, k * 2176 + n0:k * 2176 + n1], [lx, w1])
                       for k in range(8)]
                tok = mm_group(p, pb[:, 0:n1 - n0], mms, psum_free=ev_last[q % 2])
                ev = p.op("scalar", lambda e, pb=pb, n0=n0, n1=n1: e.activation(
                    gates[:, n0:n1], pb[:, 0:n1 - n0], AF.Exp),
                    wait=[tok, prev.get("gates_free")])
                etoks.append(ev)
                ev_last[q % 2] = ev
            mms = [(hbuf[:, k * P:(k + 1) * P], wvTb[:, k * P:(k + 1) * P], [lx, w2])
                   for k in range(8)]
            vtok = mm_group(p, ps_v[:], mms, psum_free=prev.get("v_free"))
            prev[("mmv", tt % 2)] = vtok
            vev = p.op("vector", lambda e: e.tensor_copy(v_sb[:], ps_v[:]),
                       wait=[vtok, prev.get("u_done")])
            prev["v_free"] = vev
            c1 = p.op("vector", lambda e: e.tensor_reduce(
                red[:], gates[:].rearrange("p (g j) -> p g j", j=17), AX.X, ALU.add),
                wait=etoks)
            c2 = p.op("vector", lambda e: e.reciprocal(red[:], red[:]))
            c3 = p.op("vector", lambda e: e.tensor_tensor(
                gates[:].rearrange("p (g j) -> p g j", j=17), gates[:].rearrange("p (g j) -> p g j", j=17),
                red[:].unsqueeze(2).broadcast_to([P, P, 17]), ALU.mult))
            a0v = gates[:].rearrange("p (g j) -> p g j", j=17)[:, :, 0]
            u_done = p.op("vector", lambda e: e.tensor_tensor(a0v, a0v, v_sb[:], ALU.mult),
                          wait=[vev])
            prev["u_done"] = u_done
            chunk = tt // 8
            j0 = (tt % 8) * P
            dst = A_hbm[b * 64 + chunk: b * 64 + chunk + 57: 8, j0:j0 + P, :].transpose([1, 0, 2])
            last_store = p.dma("sync", dst, gates[:].rearrange("p (g c) -> p g c", g=8),
                               wait=[u_done, c3], inc="stA%d" % (tt % 2))
            prev["gates_free"] = last_store
            prev[("stA", tt % 2)] = last_store
    p1_done = [prev[("stA", 0)], prev[("stA", 1)]]

    JB = 32
    NB = 1024 // JB
    Ablk = [p.sb(f"Ablk{i}", [P, JB * 272]) for i in range(2)]
    yloc = p.sb("yloc", [P, 1025 * 17])
    tmp = p.sb("tmp", [P, 272])
    cbuf = p.sb("cbuf", [P, 2 * 17])
    initz = p.sb("initz", [P, 17])

    yv = yloc[:].rearrange("p (j s) -> p j s", s=17)
    m0 = p.op("vector", lambda e: e.memset(yloc[:], 0.0))
    m1 = p.op("vector", lambda e: e.memset(yv[:, :, 0:1], 1.0))

    def scan_pass(homog, extra_wait):
        toks = []
        lbt = {0: None, 1: None}
        for jb in range(NB):
            buf = Ablk[jb % 2]
            lw = list(p1_done) + list(extra_wait)
            if lbt[jb % 2] is not None:
                lw.append(lbt[jb % 2])
            la = p.dma("sync", buf[:].rearrange("p (j c) -> p j c", j=JB),
                       A_hbm[:, jb * JB:(jb + 1) * JB, :],
                       wait=lw, inc="ldA%d_%d" % (jb % 2, int(homog)))
            for jj in range(JB):
                j = jb * JB + jj
                asl = buf[:, jj * 272:(jj + 1) * 272]
                av = asl.rearrange("p (i s) -> p i s", s=17)
                if not homog:
                    state = yv[:, j, :]
                    p.op("vector", lambda e, av=av, state=state: e.tensor_tensor(
                        tmp[:].rearrange("p (i s) -> p i s", s=17), av,
                        state.unsqueeze(1).broadcast_to([P, 16, 17]), ALU.mult),
                        wait=[la, m0, m1])
                    tok = p.op("vector", lambda e, j=j: e.tensor_reduce(
                        yv[:, j + 1, 1:17], tmp[:].rearrange("p (i s) -> p i s", s=17),
                        AX.X, ALU.add))
                else:
                    cst = cbuf[:, (j % 2) * 17:(j % 2) * 17 + 17]
                    cdst = cbuf[:, ((j + 1) % 2) * 17 + 1:((j + 1) % 2) * 17 + 17]
                    p.op("vector", lambda e, av=av, cst=cst: e.tensor_tensor(
                        tmp[:].rearrange("p (i s) -> p i s", s=17), av,
                        cst.unsqueeze(1).broadcast_to([P, 16, 17]), ALU.mult),
                        wait=[la])
                    p.op("vector", lambda e, cdst=cdst: e.tensor_reduce(
                        cdst, tmp[:].rearrange("p (i s) -> p i s", s=17), AX.X, ALU.add))
                    tok = p.op("vector", lambda e, j=j: e.tensor_tensor(
                        yv[:, j + 1, 1:17], yv[:, j + 1, 1:17],
                        cbuf[:, ((j + 1) % 2) * 17 + 1:((j + 1) % 2) * 17 + 17], ALU.add))
            lbt[jb % 2] = tok
            toks.append(tok)
        return toks[-1]

    pass1_done = scan_pass(False, [])
    mz = p.op("vector", lambda e: e.memset(initz[:], 0.0), wait=[pass1_done])
    sh = []
    for q in range(16):
        sh.append(p.dma("sync", initz[q * 8 + 1:q * 8 + 8, 1:17], yv[q * 8:q * 8 + 7, 1024, 1:17],
                        wait=[mz], inc="shf%d" % q))
    cseed = p.op("vector", lambda e: e.memset(cbuf[:], 0.0), wait=[pass1_done])
    cseed = p.op("vector", lambda e: e.tensor_copy(cbuf[:, 1:17], initz[:, 1:17]), wait=sh + [cseed])
    pass2_done = scan_pass(True, [cseed, pass1_done])
    st1 = p.dma("sync", y_out[:, 0:512, :], yv[:, 1:513, 1:17], wait=[pass2_done], inc="sty")
    stok = p.dma("sync", y_out[:, 512:1024, :], yv[:, 513:1025, 1:17], wait=[st1], inc="sty")
    p.barrier("sync", [stok])
    return p.finalize()


def build_l0out():
    p = Prog("l0out")
    yT_in = p.din("yT_sl", [D, TS])
    woT_in = p.din("woT_tiled", [P, 8, D])
    xn_in = p.din("xn_sl", [TS, D])
    x_in = p.din("x_sl", [TS, D])
    g0_in = p.din("g0_sl", [TS, D])
    lnw_in = p.din("ln_w", [1, D])
    lnb_in = p.din("ln_b", [1, D])
    x1_out = p.dout("x1_sl", [TS, D])
    xn1_out = p.dout("xn1_sl", [TS, D])
    xn1n_out = p.dout("xn1n_sl", [TS, D])

    KT = 8
    yTb = p.sb("yTb", [P, KT * P], BF16)
    woTb = p.sb("woTb", [P, KT * D], BF16)
    xnt = p.sb("xnt", [P, D])
    xt = p.sb("xt", [P, D])
    g0t = p.sb("g0t", [P, D])
    s0t = p.sb("s0t", [P, D])
    x1t = p.sb("x1t", [P, D])
    xn1t = p.sb("xn1t", [P, D])
    xn1nt = p.sb("xn1nt", [P, D])
    lnwb = p.sb("lnwb", [P, D])
    lnbb = p.sb("lnbb", [P, D])
    scr = {k: p.sb("scr_" + k, [P, D if k in ("sq", "xm", "xmr", "xg") else 1])
           for k in ("sum1", "nm", "rr", "sd", "rstd", "sq", "xm", "xmr", "xg")}
    pst = [p.ps(f"pst{i}", [P, 512]) for i in range(2)]

    w1 = p.dma("gpsimd", woTb[:].rearrange("p (k c) -> p k c", k=KT), woT_in, inc="ldw")
    lw1 = p.dma("sync", lnwb[:], lnw_in.broadcast_to([P, D]), inc="ldc")
    lw2 = p.dma("sync", lnbb[:], lnb_in.broadcast_to([P, D]), wait=[lw1], inc="ldc")

    yT_v = yT_in.rearrange("(k p) t -> p k t", p=P)
    ev_last = {0: None, 1: None}
    prev_st = None
    prev_mm = None
    for i in range(TS // P):
        t0 = i * P
        ly = p.dma("gpsimd", yTb[:].rearrange("p (k c) -> p k c", k=KT)[:, :, 0:P],
                   yT_v[:, :, t0:t0 + P], wait=[prev_mm], inc="ldy")
        l1 = p.dma("sync", xnt[:], xn_in[t0:t0 + P, :], wait=[prev_st], inc="ldt")
        l2 = p.dma("sync", xt[:], x_in[t0:t0 + P, :], wait=[l1], inc="ldt")
        l3 = p.dma("sync", g0t[:], g0_in[t0:t0 + P, :], wait=[l2], inc="ldt")
        stoks = []
        for nch in range(2):
            pb = pst[nch]
            mms = [(yTb[:, k * P:(k + 1) * P], woTb[:, k * D + nch * 512:k * D + (nch + 1) * 512],
                    [ly, w1]) for k in range(KT)]
            tok = mm_group(p, pb[:], mms, psum_free=ev_last[nch])
            prev_mm = tok
            ev = p.op("vector", lambda e, pb=pb, nch=nch: e.tensor_tensor(
                s0t[:, nch * 512:(nch + 1) * 512], pb[:], xnt[:, nch * 512:(nch + 1) * 512], ALU.add),
                wait=[tok, l3])
            ev_last[nch] = ev
            stoks.append(ev)
        p.op("vector", lambda e: e.tensor_tensor(x1t[:], s0t[:], g0t[:], ALU.mult), wait=stoks)
        cx1 = p.op("vector", lambda e: e.tensor_tensor(x1t[:], x1t[:], xt[:], ALU.add))
        cxn1 = emit_ln(p, x1t, xn1t, scr)
        cxn1n = emit_ln(p, xn1t, xn1nt, scr, ld_wait=[lw2], gain_bias=(lnwb, lnbb))
        s1 = p.dma("sync", x1_out[t0:t0 + P, :], x1t[:], wait=[cx1], inc="st")
        s2 = p.dma("sync", xn1_out[t0:t0 + P, :], xn1t[:], wait=[cxn1, s1], inc="st")
        prev_st = p.dma("sync", xn1n_out[t0:t0 + P, :], xn1nt[:], wait=[cxn1n, s2], inc="st")
    p.barrier("sync", [prev_st])
    return p.finalize()


def build_mamba1(debug_xs=False):
    p = Prog("mamba1")
    xnT_in = p.din("xn1nT_full", [D, NTOK])
    winT_in = p.din("winT_tiled", [P, 8, 512])
    convw_in = p.din("convw_tiled", [P, 2, 4])
    convb_in = p.din("convb_tiled", [P, 2])
    wxT_in = p.din("wxT_tiled", [P, 2, 96])
    u_out = p.dout("u_sl", [256, NTOK])
    z_out = p.dout("zsil_sl", [256, NTOK])
    xd_out = p.dout("xdbl_part", [96, NTOK])

    PADW = T + 3
    winTb = p.sb("winTb", [P, 8 * 512], BF16)
    wxTb = p.sb("wxTb", [P, 2 * 96], BF16)
    convw = p.sb("convw", [P, 2 * 4])
    convb = p.sb("convb", [P, 2])
    xnTb = p.sb("xnTb", [P, 8 * 512], BF16)
    xs_pad = p.sb("xs_pad", [P, 2 * PADW])
    xca = p.sb("xca", [P, T])
    xcb = p.sb("xcb", [P, T])
    zch = p.sb("zch", [P, 512])
    ub = p.sb("ub", [P, 2 * 512], BF16)
    xde = p.sb("xde", [P, 512])
    pst = [p.ps(f"pst{i}", [P, 512]) for i in range(2)]

    w1 = p.dma("gpsimd", winTb[:].rearrange("p (k c) -> p k c", k=8), winT_in, inc="ldw")
    w2 = p.dma("gpsimd", wxTb[:].rearrange("p (k c) -> p k c", k=2), wxT_in, wait=[w1], inc="ldw")
    c1 = p.dma("sync", convw[:].rearrange("p (t k) -> p t k", t=2), convw_in, inc="ldc")
    c2 = p.dma("sync", convb[:], convb_in, wait=[c1], inc="ldc")

    xnT_v = xnT_in.rearrange("(k p) t -> p k t", p=P)
    mpad = p.op("vector", lambda e: e.memset(xs_pad[:], 0.0))

    ev_last = {0: None, 1: None}
    prev_mm = None
    last_u_store = None
    last_z = None
    for et in range(2):
        for tc in range(32):
            b, tb = tc // 16, (tc % 16) * 512
            t0 = tc * 512
            lx = p.dma("gpsimd", xnTb[:].rearrange("p (k c) -> p k c", k=8),
                       xnT_v[:, :, t0:t0 + 512], wait=[prev_mm], inc="ldx")
            mms = [(winTb[:, k * 512 + et * P:k * 512 + (et + 1) * P], xnTb[:, k * 512:(k + 1) * 512],
                    [lx, w2]) for k in range(8)]
            tokx = mm_group(p, pst[0][:], mms, psum_free=ev_last[0])
            exs = p.op("vector", lambda e, b=b, tb=tb: e.tensor_copy(
                xs_pad[:, b * PADW + 3 + tb: b * PADW + 3 + tb + 512], pst[0][:]),
                wait=[tokx, mpad])
            ev_last[0] = exs
            mms = [(winTb[:, k * 512 + (2 + et) * P:k * 512 + (3 + et) * P], xnTb[:, k * 512:(k + 1) * 512],
                    [lx, w2]) for k in range(8)]
            tokz = mm_group(p, pst[1][:], mms, psum_free=ev_last[1])
            prev_mm = tokz
            ez = p.op("scalar", lambda e: e.activation(zch[:], pst[1][:], AF.Silu),
                      wait=[tokz, last_z])
            ev_last[1] = ez
            last_z = p.dma("sync", z_out[et * P:(et + 1) * P, t0:t0 + 512], zch[:],
                           wait=[ez], inc="stz")
        dbg_tok = None
        if debug_xs and et == 0:
            xs_dbg = p.dout("xs_dbg", [P, 2 * PADW])
            dbg_tok = p.dma("sync", xs_dbg, xs_pad[:], wait=[exs], inc="stdbg")
        for b in range(2):
            off = b * PADW
            cv0 = p.op("vector", lambda e, off=off, et=et: e.tensor_scalar(
                xca[:], xs_pad[:, off:off + T], convw[:, 4 * et:4 * et + 1],
                convb[:, et:et + 1], ALU.mult, ALU.add), wait=[exs, c2, last_u_store])
            cv1 = p.op("vector", lambda e, off=off, et=et: e.scalar_tensor_tensor(
                xcb[:], xs_pad[:, off + 1:off + 1 + T], convw[:, 4 * et + 1:4 * et + 2],
                xca[:], ALU.mult, ALU.add))
            cv2 = p.op("vector", lambda e, off=off, et=et: e.scalar_tensor_tensor(
                xca[:], xs_pad[:, off + 2:off + 2 + T], convw[:, 4 * et + 2:4 * et + 3],
                xcb[:], ALU.mult, ALU.add))
            cv3 = p.op("vector", lambda e, off=off, et=et: e.scalar_tensor_tensor(
                xcb[:], xs_pad[:, off + 3:off + 3 + T], convw[:, 4 * et + 3:4 * et + 4],
                xca[:], ALU.mult, ALU.add))
            cu = p.op("scalar", lambda e: e.activation(xca[:], xcb[:], AF.Silu), wait=[cv3])
            last_u_store = p.dma("sync", u_out[et * P:(et + 1) * P, b * T:(b + 1) * T], xca[:],
                                 wait=[cu], inc="stu")
        if et == 0:
            mpad = p.op("vector", lambda e: e.memset(xs_pad[:], 0.0),
                        wait=[cv3, last_u_store, dbg_tok])
    prev_ev = None
    last_xd = None
    for tc in range(32):
        t0 = tc * 512
        lu = p.dma("gpsimd", ub[:].rearrange("p (t c) -> p t c", t=2),
                   u_out.rearrange("(t p) n -> p t n", p=P)[:, :, t0:t0 + 512],
                   wait=[last_u_store, prev_ev], inc="ldu")
        mms = [(wxTb[:, k * 96:(k + 1) * 96], ub[:, k * 512:(k + 1) * 512], [lu, w2])
               for k in range(2)]
        tok = mm_group(p, pst[0][:96, :], mms, psum_free=prev_ev)
        prev_ev = p.op("vector", lambda e: e.tensor_copy(xde[:96, :], pst[0][:96, :]),
                       wait=[tok, last_xd])
        last_xd = p.dma("sync", xd_out[:, t0:t0 + 512], xde[:96, :], wait=[prev_ev], inc="stx")
    p.barrier("sync", [last_xd, last_u_store, last_z])
    return p.finalize()


def build_mamba2():
    p = Prog("mamba2")
    xdp_in = p.din("xdbl_parts", [NCORES, 96, NTOK])
    wdtT_in = p.din("wdtT_sl", [64, 256])
    bdt_in = p.din("bdt_tiled", [P, 2])
    alog_in = p.din("alog_tiled", [P, 2, 16])
    dskip_in = p.din("dskip_tiled", [P, 2])
    u_in = p.din("u_sl", [256, NTOK])
    z_in = p.din("zsil_sl", [256, NTOK])
    y_out = p.dout("y1_sl", [256, NTOK])
    bc_dram = p.dtmp("bc_dram", [32, 256])

    TB = 256
    NTB = T // TB
    xdbl = p.sb("xdbl", [96, NTOK])
    xdp = [p.sb(f"xdp{i}", [96, 2048]) for i in range(2)]
    wdtT = p.sb("wdtT", [64, 256])
    bdt = p.sb("bdt", [P, 2])
    A_sb = p.sb("A_sb", [P, 32])
    dskip = p.sb("dskip", [P, 2])
    neg1 = p.sb("neg1", [P, 1])
    one1 = p.sb("one1", [P, 1])
    dt_blk = p.sb("dt_blk", [P, TB])
    t_abs = p.sb("t_abs", [P, TB])
    t_e = p.sb("t_e", [P, TB])
    t_l = p.sb("t_l", [P, TB])
    t_r = p.sb("t_r", [P, TB])
    bufA = p.sb("bufA", [P, 16 * TB])
    bufB = p.sb("bufB", [P, 16 * TB])
    bufC = p.sb("bufC", [P, 16 * TB])
    bcast = p.sb("bcast", [P, 16 * TB])
    du = p.sb("du", [P, TB])
    ub = p.sb("ub", [P, TB])
    zb = p.sb("zb", [P, TB])
    yb = p.sb("yb", [P, TB])
    carry = p.sb("carry", [P, 16])
    pst = [p.ps(f"pst{i}", [P, 512]) for i in range(2)]

    m1 = p.op("vector", lambda e: e.memset(neg1[:], -1.0))
    m2 = p.op("vector", lambda e: e.memset(one1[:], 1.0))
    lw = p.dma("sync", wdtT[:], wdtT_in, inc="ldw")
    l1 = p.dma("sync", bdt[:], bdt_in, inc="ldc")
    l2 = p.dma("sync", A_sb[:], alog_in.rearrange("p a n -> p (a n)"), wait=[l1], inc="ldc")
    l3 = p.dma("sync", dskip[:], dskip_in, wait=[l2], inc="ldc")
    sA = p.op("scalar", lambda e: e.activation(A_sb[:], A_sb[:], AF.Exp), wait=[l2])
    sA = p.op("vector", lambda e: e.tensor_scalar(A_sb[:], A_sb[:], -1.0, None, ALU.mult), wait=[sA])

    sum_tok = None
    prev = {0: None, 1: None}
    for tc in range(8):
        t0 = tc * 2048
        for j in range(NCORES):
            lb = p.dma("sync", xdp[j % 2][:], xdp_in[j, :, t0:t0 + 2048],
                       wait=[prev[j % 2]], inc="ldp%d" % (j % 2))
            if j == 0:
                sum_tok = p.op("vector", lambda e, t0=t0: e.tensor_copy(
                    xdbl[:, t0:t0 + 2048], xdp[0][:]), wait=[lb])
            else:
                sum_tok = p.op("vector", lambda e, t0=t0, j=j: e.tensor_tensor(
                    xdbl[:, t0:t0 + 2048], xdbl[:, t0:t0 + 2048], xdp[j % 2][:], ALU.add),
                    wait=[lb])
            prev[j % 2] = sum_tok

    toks = {}
    ev_last = {0: None, 1: None}
    last = {}
    for b in range(2):
        for et in range(2):
            cm = p.op("vector", lambda e: e.memset(carry[:], 0.0), wait=[last.get("scan")])
            for tb in range(T // TB):
                t0g = b * T + tb * TB
                pb = pst[tb % 2]
                tok = mm_group(p, pb[:, 0:TB],
                               [(wdtT[:, et * P:(et + 1) * P], xdbl[0:64, t0g:t0g + TB],
                                 [lw, sum_tok, last.get("dtp%d" % (tb % 2))])],
                               psum_free=ev_last[tb % 2])
                cdtp = p.op("vector", lambda e, pb=pb, et=et: e.tensor_scalar(
                    t_abs[:], pb[:, 0:TB], bdt[:, et:et + 1], None, ALU.add),
                    wait=[tok, m1, m2, l1, last.get("tabs")])
                ev_last[tb % 2] = cdtp
                last["dtp%d" % (tb % 2)] = cdtp
                s4 = p.op("scalar", lambda e: e.activation(t_r[:], t_abs[:], AF.Relu), wait=[cdtp])
                s1 = p.op("scalar", lambda e: e.activation(t_abs[:], t_abs[:], AF.Abs))
                s2 = p.op("scalar", lambda e: e.activation(t_e[:], t_abs[:], AF.Exp, scale=neg1[:, 0:1]))
                s3 = p.op("scalar", lambda e: e.activation(t_l[:], t_e[:], AF.Ln, bias=one1[:, 0:1]))
                last["tabs"] = s3
                cdt = p.op("vector", lambda e: e.tensor_tensor(dt_blk[:], t_l[:], t_r[:], ALU.add),
                           wait=[s3, s4])
                cpre = p.op("vector", lambda e, et=et: e.tensor_tensor(
                    bufA[:].rearrange("p (n t) -> p n t", n=16),
                    dt_blk[:].unsqueeze(1).broadcast_to([P, 16, TB]),
                    A_sb[:, et * 16:(et + 1) * 16].unsqueeze(2).broadcast_to([P, 16, TB]),
                    ALU.mult), wait=[sA, last.get("dA")])
                cdA = p.op("scalar", lambda e: e.activation(bufA[:], bufA[:], AF.Exp), wait=[cpre])
                last["dA"] = cdA
                lu = p.dma("sync", ub[:], u_in[et * P:(et + 1) * P, t0g:t0g + TB],
                           wait=[last.get("ub")], inc="ldu")
                lz = p.dma("sync", zb[:], z_in[et * P:(et + 1) * P, t0g:t0g + TB],
                           wait=[last.get("zb")], inc="ldz")
                cdu = p.op("vector", lambda e: e.tensor_tensor(du[:], dt_blk[:], ub[:], ALU.mult),
                           wait=[lu])
                sb1 = p.dma("sync", bc_dram[:, 0:TB], xdbl[64:96, t0g:t0g + TB],
                            wait=[sum_tok, last.get("bco"), last.get("bcB")], inc="stbc")
                lbB = p.dma("sync", bcast[:], bc_dram[0:16, 0:TB].rearrange("a b -> (a b)").unsqueeze(0).broadcast_to([P, 16 * TB]),
                            wait=[sb1, last.get("hC")], inc="ldbB")
                cdBu = p.op("vector", lambda e: e.tensor_tensor(
                    bufB[:].rearrange("p (n t) -> p n t", n=16),
                    du[:].unsqueeze(1).broadcast_to([P, 16, TB]),
                    bcast[:].rearrange("p (n t) -> p n t", n=16), ALU.mult),
                    wait=[lbB, cdu, last.get("hC")])
                last["bcB"] = cdBu
                stoks = []
                for n in range(16):
                    stoks.append(p.op("vector", lambda e, n=n: e.tensor_tensor_scan(
                        bufC[:, n * TB:(n + 1) * TB], bufA[:, n * TB:(n + 1) * TB],
                        bufB[:, n * TB:(n + 1) * TB], carry[:, n:n + 1], ALU.mult, ALU.add),
                        wait=[cdA, cm]))
                ccar = p.op("vector", lambda e: e.tensor_copy(
                    carry[:], bufC[:].rearrange("p (n t) -> p n t", n=16)[:, :, TB - 1]))
                last["scan"] = ccar
                lbC = p.dma("sync", bcast[:], bc_dram[16:32, 0:TB].rearrange("a b -> (a b)").unsqueeze(0).broadcast_to([P, 16 * TB]),
                            wait=[sb1, cdBu], inc="ldbC")
                last["bco"] = lbC
                chC = p.op("vector", lambda e: e.tensor_tensor(
                    bufB[:], bufC[:], bcast[:], ALU.mult), wait=[lbC] + stoks)
                last["hC"] = chC
                cy = p.op("vector", lambda e: e.tensor_reduce(
                    yb[:], bufB[:].rearrange("p (n t) -> p t n", n=16), AX.X, ALU.add),
                    wait=[last.get("sy_last")])
                cy2 = p.op("vector", lambda e, et=et: e.scalar_tensor_tensor(
                    yb[:], ub[:], dskip[:, et:et + 1], yb[:], ALU.mult, ALU.add), wait=[l3])
                last["ub"] = cy2
                cy3 = p.op("vector", lambda e: e.tensor_tensor(yb[:], yb[:], zb[:], ALU.mult),
                           wait=[lz])
                last["zb"] = cy3
                sy = p.dma("sync", y_out[et * P:(et + 1) * P, t0g:t0g + TB], yb[:],
                           wait=[cy3, last.get("sy%d" % (tb % 2))], inc="sty%d" % (tb % 2))
                last["sy%d" % (tb % 2)] = sy
                last["sy_last"] = sy
    p.barrier("sync", [last["sy0"], last["sy1"]])
    return p.finalize()


def build_l1ffn_iter(first):
    p = Prog("l1ffn_iter")
    xcT_in = p.din("xcT_prev", [D, TS])
    if not first:
        xnT_in = p.din("xnewT_prev", [D, TS])
        done_in = p.din("done_rep", [P, 1])
    wmT_in = p.din("wmT_full", [D, 12288])
    wpT_in = p.din("wpT_full", [12288, D])
    mask_in = p.din("mask_packed", [P, 96])
    xnew_out = p.dout("xnewT_sl", [D, TS])
    if not first:
        xcu_out = p.dout("xcT_used", [D, TS])
    pooled_out = p.dout("pooled_part", [P, 8])

    KT, MT = 8, 96
    xcTb = p.sb("xcTb", [P, KT * TS], BF16)
    Hb = p.sb("Hb", [P, MT * 512], BF16)
    wmb = [p.sb(f"wmb{i}", [P, KT * P], BF16) for i in range(2)]
    wpb = [p.sb(f"wpb{i}", [P, D], BF16) for i in range(2)]
    mask = p.sb("mask", [P, 96])
    xn_ev = p.sb("xn_ev", [P, 512])
    pl_acc = p.sb("pl_acc", [P, 8])
    pl_tmp = p.sb("pl_tmp", [P, 1])
    xcf = p.sb("xcf", [P, TS])
    xnf = p.sb("xnf", [P, TS])
    done_sb = p.sb("done_sb", [P, 1])
    omd_sb = p.sb("omd_sb", [P, 1])
    psx = [p.ps(f"psx{i}", [P, 512]) for i in range(8)]
    psh = [psx[6], psx[7]]

    lm = p.dma("sync", mask[:], mask_in, inc="ldm")
    mz = p.op("vector", lambda e: e.memset(pl_acc[:], 0.0))

    if first:
        xg = None
        for q in range(2):
            xg = p.dma("gpsimd", xcTb[:].rearrange("p (k t) -> p k t", k=KT)[:, 4 * q:4 * q + 4],
                       xcT_in.rearrange("(k p) t -> p k t", p=P)[:, 4 * q:4 * q + 4],
                       wait=[xg], inc="ldx")
        blend_tok = xg
    else:
        ld = p.dma("sync", done_sb[:], done_in, inc="ldd")
        cdn = p.op("vector", lambda e: e.tensor_scalar(omd_sb[:], done_sb[:], -1.0, 1.0,
                                                       ALU.mult, ALU.add), wait=[ld])
        prev_st = {}
        xcf_free = []
        for k in range(KT):
            l1 = p.dma("sync", xcf[:], xcT_in[k * P:(k + 1) * P, :],
                       wait=xcf_free, inc="ldc1")
            l2 = p.dma("sync", xnf[:], xnT_in[k * P:(k + 1) * P, :],
                       wait=[prev_st.get("n"), l1], inc="ldc2")
            c1 = p.op("vector", lambda e: e.tensor_scalar(xcf[:], xcf[:], done_sb[:, 0:1], None, ALU.mult),
                      wait=[l1, cdn])
            c2 = p.op("vector", lambda e: e.scalar_tensor_tensor(
                xcf[:], xnf[:], omd_sb[:, 0:1], xcf[:], ALU.mult, ALU.add), wait=[l2])
            s1 = p.dma("sync", xcu_out[k * P:(k + 1) * P, :], xcf[:], wait=[c2], inc="stc%d" % (k % 2))
            cb = p.op("vector", lambda e, k=k: e.tensor_copy(xcTb[:, k * TS:(k + 1) * TS], xcf[:]))
            xcf_free = [cb, s1]
            prev_st["n"] = c2
            blend_tok = cb
            prev_st[k % 2] = s1
        p.barrier("sync", [prev_st[0], prev_st[1]])

    wm_v = wmT_in.rearrange("(k p) f -> p k f", p=P)
    ev_h = {0: None, 1: None}
    ev_x = {i: None for i in range(8)}
    prev_w = {0: None, 1: None}
    prev_wp = {0: None, 1: None}
    last_px = None
    last_xst = {}
    for g in range(4):
        t0 = g * 512
        hev = None
        for m in range(MT):
            wb = wmb[m % 2]
            lw = p.dma("gpsimd", wb[:].rearrange("p (k c) -> p k c", k=KT),
                       wm_v[:, :, m * P:(m + 1) * P], wait=[prev_w[m % 2]], inc="ldw%d" % (m % 2))
            pb = psh[m % 2]
            mms = [(wb[:, k * P:(k + 1) * P], xcTb[:, k * TS + t0:k * TS + t0 + 512],
                    [lw, blend_tok]) for k in range(KT)]
            tok = mm_group(p, pb[:], mms, psum_free=ev_h[m % 2])
            prev_w[m % 2] = tok
            hev = p.op("scalar", lambda e, pb=pb, m=m: e.activation(
                Hb[:, m * 512:(m + 1) * 512], pb[:], AF.Relu, scale=mask[:, m:m + 1]),
                wait=[tok, lm])
            ev_h[m % 2] = hev
        for m in range(MT):
            wb = wpb[m % 2]
            lwp = p.dma("gpsimd", wb[:], wpT_in[m * P:(m + 1) * P, :],
                        wait=[prev_wp[m % 2]], inc="ldp%d" % (m % 2))
            for dt in range(8):
                tok = p.op("tensor",
                           (lambda e, m=m, dt=dt, wb=wb, s=(m == 0), t=(m == MT - 1):
                            e.matmul(psx[dt][:], wb[:, dt * P:(dt + 1) * P],
                                     Hb[:, m * 512:(m + 1) * 512], start=s, stop=t)),
                           wait=([lwp, hev] if dt == 0 else [lwp]) +
                                ([ev_x[dt]] if m == 0 else []),
                           inc="pe" if (dt == 7 or m == MT - 1) else None, chain=False)
            prev_wp[m % 2] = tok
        xg_toks = []
        for dt in range(8):
            ev = p.op("vector", lambda e, dt=dt: e.tensor_copy(xn_ev[:], psx[dt][:]),
                      wait=[tok, last_px])
            ev_x[dt] = ev
            st = p.dma("sync", xnew_out[dt * P:(dt + 1) * P, t0:t0 + 512], xn_ev[:],
                       wait=[ev], inc="stx%d" % (dt % 2))
            last_px = st
            last_xst[dt % 2] = st
            c1 = p.op("vector", lambda e: e.tensor_reduce(pl_tmp[:], xn_ev[:], AX.X, ALU.add),
                      wait=[mz])
            xg_toks.append(p.op("vector", lambda e, dt=dt: e.tensor_tensor(
                pl_acc[:, dt:dt + 1], pl_acc[:, dt:dt + 1], pl_tmp[:], ALU.add)))
        ev_h[0], ev_h[1] = ev_x[6], ev_x[7]
    pst_tok = p.dma("sync", pooled_out, pl_acc[:], wait=[xg_toks[-1]], inc="stp")
    p.barrier("sync", [pst_tok, last_xst[0], last_xst[1]])
    return p.finalize()


def build_l1ffn_ctl():
    p = Prog("l1ffn_ctl")
    parts_in = p.din("pooled_parts", [NCORES, P, 8])
    wc1T_in = p.din("wc1T_tiled", [P, 8, 256])
    wc2T_in = p.din("wc2T_tiled", [P, 2, 12288])
    hist_in = p.din("history", [P, 192])
    act_in = p.din("active", [P, 192])
    done_in = p.din("done_rep", [P, 1])
    act_out = p.dout("active_next", [P, 192])
    hist_out = p.dout("history_next", [P, 192])
    done_out = p.dout("done_next", [P, 1])

    prt = p.sb("prt", [P, 8])
    pl = p.sb("pl", [P, 16])
    wc1 = p.sb("wc1", [P, 8 * 256], BF16)
    wc2 = p.sb("wc2", [P, 2 * 12288], BF16)
    r_sb = p.sb("r_sb", [P, 4], BF16)
    ck = p.sb("ck", [P, 192])
    hist = p.sb("hist", [P, 192])
    act = p.sb("act", [P, 192])
    done = p.sb("done", [P, 1])
    omd = p.sb("omd", [P, 1])
    lo = p.sb("lo", [P, 2])
    hi = p.sb("hi", [P, 2])
    mid = p.sb("mid", [P, 2])
    ge = p.sb("ge", [P, 192])
    rs = p.sb("rs", [P, 2])
    pred = p.sb("pred", [P, 2])
    omp = p.sb("omp", [P, 2])
    tq = p.sb("tq", [P, 2])
    nov = p.sb("nov", [P, 1])
    ps1 = p.ps("ps1", [P, 8])
    ps2 = p.ps("ps2", [P, 8])

    lws = p.dma("gpsimd", wc1[:].rearrange("p (k c) -> p k c", k=8), wc1T_in, inc="lw1")
    lw2 = p.dma("gpsimd", wc2[:].rearrange("p (k c) -> p k c", k=2), wc2T_in, inc="lw2")
    lh = p.dma("sync", hist[:], hist_in, inc="ldh")
    la = p.dma("sync", act[:], act_in, wait=[lh], inc="ldh")
    ldn = p.dma("sync", done[:], done_in, wait=[la], inc="ldh")

    toks = []
    prev = None
    for j in range(NCORES):
        b = j // 4
        lp = p.dma("sync", prt[:], parts_in[j], wait=[prev], inc="ldp")
        if j % 4 == 0:
            tok = p.op("vector", lambda e, b=b: e.tensor_copy(
                pl[:].rearrange("p (dt b2) -> p dt b2", b2=2)[:, :, b], prt[:]), wait=[lp])
        else:
            tok = p.op("vector", lambda e, b=b: e.tensor_tensor(
                pl[:].rearrange("p (dt b2) -> p dt b2", b2=2)[:, :, b],
                pl[:].rearrange("p (dt b2) -> p dt b2", b2=2)[:, :, b], prt[:], ALU.add),
                wait=[lp])
        prev = tok
    cpl = p.op("vector", lambda e: e.tensor_scalar(pl[:], pl[:], 1.0 / T, None, ALU.mult))
    plb = p.sb("plb", [P, 16], BF16)
    cpl = p.op("vector", lambda e: e.tensor_copy(plb[:], pl[:]), wait=[cpl])
    rtoks = []
    for mt in range(2):
        mms = [(wc1[:, (k * 256) + mt * P:(k * 256) + (mt + 1) * P], plb[:, k * 2:(k + 1) * 2], [lws, cpl])
               for k in range(8)]
        tok = mm_group(p, ps1[:, 0:2], mms, psum_free=rtoks[-1] if rtoks else None)
        rtoks.append(p.op("scalar", lambda e, mt=mt: e.activation(
            r_sb[:, mt * 2:(mt + 1) * 2], ps1[:, 0:2], AF.Relu), wait=[tok]))
    cktoks = []
    for m in range(96):
        mms = [(wc2[:, k * 12288 + m * P:k * 12288 + (m + 1) * P], r_sb[:, k * 2:(k + 1) * 2],
                [lw2, rtoks[-1]]) for k in range(2)]
        tok = mm_group(p, ps2[:, 0:2], mms, psum_free=cktoks[-1] if cktoks else None)
        cktoks.append(p.op("vector", lambda e, m=m: e.tensor_copy(
            ck[:, m * 2:(m + 1) * 2], ps2[:, 0:2]), wait=[tok]))
    ck_done = cktoks[-1]

    ckv = ck[:].rearrange("p (m b2) -> p b2 m", b2=2)
    ones = p.sb("ones", [P, P])
    lohi = p.sb("lohi", [P, 4])
    lohi1 = p.sb("lohi1", [1, 4])
    lh_dram = p.dtmp("lh_dram", [P, 4])
    psr = p.ps("psr", [P, 4])
    mon = p.op("vector", lambda e: e.memset(ones[:], 1.0))

    def part_sum(src_ap, dst_ap, w):
        n = src_ap.shape[-1]
        t1 = p.op("tensor", lambda e: e.matmul(psr[0:1, 0:n], ones[:, 0:1], src_ap, start=True, stop=True),
                  wait=list(w) + [mon, part_sum.free], inc="pe", chain=False)
        t2 = p.op("vector", lambda e: e.tensor_copy(lohi1[0:1, 0:n], psr[0:1, 0:n]), wait=[t1])
        t3 = p.op("tensor", lambda e: e.matmul(psr[:, 0:n], ones[0:1, :], lohi1[0:1, 0:n], start=True, stop=True),
                  wait=[t2], inc="pe", chain=False)
        t4 = p.op("vector", lambda e: e.tensor_copy(dst_ap, psr[:, 0:n]), wait=[t3])
        part_sum.free = t4
        return t4
    part_sum.free = None

    flat512 = p.sb("flat512", [1, 512])
    lh2_dram = p.dtmp("lh2_dram", [1, 4])
    c1 = p.op("vector", lambda e: e.tensor_reduce(lohi[:, 0:2], ckv, AX.X, ALU.min), wait=[ck_done])
    c2 = p.op("vector", lambda e: e.tensor_reduce(lohi[:, 2:4], ckv, AX.X, ALU.max))
    d1 = p.dma("sync", lh_dram, lohi[:], wait=[c2], inc="lhd")
    d2 = p.dma("sync", flat512[0:1, :], lh_dram.rearrange("p c -> (p c)").unsqueeze(0),
               wait=[d1], inc="lhd")
    fv = flat512[0:1, :].rearrange("q (pp c) -> q c pp", c=4)
    c3 = p.op("vector", lambda e: e.tensor_reduce(lohi1[0:1, 0:2], fv[:, 0:2, :], AX.X, ALU.min),
              wait=[d2])
    c4 = p.op("vector", lambda e: e.tensor_reduce(lohi1[0:1, 2:4], fv[:, 2:4, :], AX.X, ALU.max))
    d3 = p.dma("sync", lh2_dram, lohi1[0:1, :], wait=[c4], inc="lhd")
    d4 = p.dma("sync", lohi[:], lh2_dram.broadcast_to([P, 4]), wait=[d3], inc="lhd")
    c5 = p.op("vector", lambda e: e.tensor_copy(lo[:], lohi[:, 0:2]), wait=[d4])
    c6 = p.op("vector", lambda e: e.tensor_copy(hi[:], lohi[:, 2:4]))
    ctok = c6
    for it in range(42):
        cm = p.op("vector", lambda e: e.tensor_tensor(mid[:], lo[:], hi[:], ALU.add), wait=[ctok])
        cm = p.op("vector", lambda e: e.tensor_scalar(mid[:], mid[:], 0.5, None, ALU.mult))
        cg = p.op("vector", lambda e: e.tensor_tensor(
            ge[:].rearrange("p (m b2) -> p b2 m", b2=2), ckv,
            mid[:].unsqueeze(2).broadcast_to([P, 2, 96]), ALU.is_ge))
        cr = p.op("vector", lambda e: e.tensor_reduce(rs[:], ge[:].rearrange("p (m b2) -> p b2 m", b2=2),
                                                      AX.X, ALU.add))
        cgp = part_sum(rs[:, 0:2], rs[:, 0:2], [cr])
        cp = p.op("vector", lambda e: e.tensor_scalar(pred[:], rs[:], 4096.0, None, ALU.is_ge), wait=[cgp])
        co = p.op("vector", lambda e: e.tensor_scalar(omp[:], pred[:], -1.0, 1.0, ALU.mult, ALU.add))
        u1 = p.op("vector", lambda e: e.tensor_tensor(tq[:], mid[:], lo[:], ALU.subtract))
        u2 = p.op("vector", lambda e: e.tensor_tensor(tq[:], tq[:], pred[:], ALU.mult))
        u3 = p.op("vector", lambda e: e.tensor_tensor(lo[:], lo[:], tq[:], ALU.add))
        u4 = p.op("vector", lambda e: e.tensor_tensor(tq[:], mid[:], hi[:], ALU.subtract))
        u5 = p.op("vector", lambda e: e.tensor_tensor(tq[:], tq[:], omp[:], ALU.mult))
        ctok = p.op("vector", lambda e: e.tensor_tensor(hi[:], hi[:], tq[:], ALU.add))
    cmask = p.op("vector", lambda e: e.tensor_tensor(
        ge[:].rearrange("p (m b2) -> p b2 m", b2=2), ckv,
        lo[:].unsqueeze(2).broadcast_to([P, 2, 96]), ALU.is_ge), wait=[ctok])
    comb = ck
    cc1 = p.op("vector", lambda e: e.tensor_tensor(comb[:], hist[:], ge[:], ALU.add), wait=[cmask, la])
    cc2 = p.op("vector", lambda e: e.tensor_single_scalar(comb[:], comb[:], 1.0, ALU.min))
    dif = p.sb("dif", [P, 192])
    cd1 = p.op("vector", lambda e: e.tensor_tensor(dif[:], comb[:], hist[:], ALU.subtract))
    cd2 = p.op("vector", lambda e: e.tensor_reduce(rs[:], dif[:].rearrange("p (m b2) -> p b2 m", b2=2),
                                                   AX.X, ALU.add))
    cd3 = part_sum(rs[:, 0:2], rs[:, 0:2], [cd2])
    cd4 = p.op("vector", lambda e: e.tensor_reduce(nov[:], rs[:], AX.X, ALU.add), wait=[cd3])
    cd5 = p.op("vector", lambda e: e.tensor_scalar(nov[:], nov[:], 1.0 / 8192.0, None, ALU.mult))
    cd6 = p.op("vector", lambda e: e.tensor_scalar(nov[:], nov[:], 0.7, None, ALU.is_lt))
    cdn = p.op("vector", lambda e: e.tensor_tensor(nov[:], done[:], nov[:], ALU.max), wait=[ldn])
    sd = p.dma("sync", done_out, nov[:], wait=[cdn], inc="std")
    co = p.op("vector", lambda e: e.tensor_scalar(omd[:], done[:], -1.0, 1.0, ALU.mult, ALU.add))
    b1 = p.op("vector", lambda e: e.tensor_scalar(act[:], act[:], done[:, 0:1], None, ALU.mult))
    b2 = p.op("vector", lambda e: e.scalar_tensor_tensor(act[:], ge[:], omd[:, 0:1], act[:],
                                                         ALU.mult, ALU.add), wait=[cmask])
    sa = p.dma("sync", act_out, act[:], wait=[b2], inc="sta")
    b3 = p.op("vector", lambda e: e.tensor_scalar(hist[:], hist[:], done[:, 0:1], None, ALU.mult))
    b4 = p.op("vector", lambda e: e.scalar_tensor_tensor(hist[:], comb[:], omd[:, 0:1], hist[:],
                                                         ALU.mult, ALU.add))
    sh2 = p.dma("sync", hist_out, hist[:], wait=[b4], inc="sth")
    p.barrier("sync", [sd, sa, sh2])
    return p.finalize()


def build_final():
    p = Prog("final")
    y1T_in = p.din("y1T_sl", [2048, TS])
    woT_in = p.din("wo1T_tiled", [P, 16, D])
    xn1_in = p.din("xn1_sl", [TS, D])
    x1_in = p.din("x1_sl", [TS, D])
    xc3_in = p.din("xc3_sl", [TS, D])
    xn3_in = p.din("xnew3_sl", [TS, D])
    done_in = p.din("done_rep", [P, 1])
    x2_out = p.dout("x2_sl", [TS, D])

    KT = 16
    y1Tb = p.sb("y1Tb", [P, KT * P], BF16)
    woTb = p.sb("woTb", [P, KT * D], BF16)
    xn1t = p.sb("xn1t", [P, D])
    x1t = p.sb("x1t", [P, D])
    xc3t = p.sb("xc3t", [P, D])
    xn3t = p.sb("xn3t", [P, D])
    s1t = p.sb("s1t", [P, D])
    gt = p.sb("gt", [P, D])
    done_sb = p.sb("done_sb", [P, 1])
    omd_sb = p.sb("omd_sb", [P, 1])
    pst = [p.ps(f"pst{i}", [P, 512]) for i in range(2)]

    w1 = None
    for q in range(2):
        w1 = p.dma("gpsimd", woTb[:].rearrange("p (k c) -> p k c", k=KT)[:, 8 * q:8 * q + 8],
                   woT_in[:, 8 * q:8 * q + 8], wait=[w1], inc="ldw")
    ld = p.dma("sync", done_sb[:], done_in, inc="ldd")
    cdn = p.op("vector", lambda e: e.tensor_scalar(omd_sb[:], done_sb[:], -1.0, 1.0,
                                                   ALU.mult, ALU.add), wait=[ld])
    y1T_v = y1T_in.rearrange("(k p) t -> p k t", p=P)
    ev_last = {0: None, 1: None}
    prev_mm = None
    prev_st = None
    for i in range(TS // P):
        t0 = i * P
        ly = p.dma("gpsimd", y1Tb[:].rearrange("p (k c) -> p k c", k=KT),
                   y1T_v[:, :, t0:t0 + P], wait=[prev_mm], inc="ldy")
        l1 = p.dma("sync", xn1t[:], xn1_in[t0:t0 + P, :], wait=[prev_st], inc="ldt")
        l2 = p.dma("sync", x1t[:], x1_in[t0:t0 + P, :], wait=[l1], inc="ldt")
        l3 = p.dma("sync", xc3t[:], xc3_in[t0:t0 + P, :], wait=[l2], inc="ldt")
        l4 = p.dma("sync", xn3t[:], xn3_in[t0:t0 + P, :], wait=[l3], inc="ldt")
        stoks = []
        for nch in range(2):
            pb = pst[nch]
            mms = [(y1Tb[:, k * P:(k + 1) * P], woTb[:, k * D + nch * 512:k * D + (nch + 1) * 512],
                    [ly, w1]) for k in range(KT)]
            tok = mm_group(p, pb[:], mms, psum_free=ev_last[nch])
            prev_mm = tok
            ev = p.op("vector", lambda e, pb=pb, nch=nch: e.tensor_tensor(
                s1t[:, nch * 512:(nch + 1) * 512], pb[:], xn1t[:, nch * 512:(nch + 1) * 512], ALU.add),
                wait=[tok, l4])
            ev_last[nch] = ev
            stoks.append(ev)
        p.op("vector", lambda e: e.tensor_scalar(xc3t[:], xc3t[:], done_sb[:, 0:1], None, ALU.mult),
             wait=[cdn, l4])
        cxf = p.op("vector", lambda e: e.scalar_tensor_tensor(
            xc3t[:], xn3t[:], omd_sb[:, 0:1], xc3t[:], ALU.mult, ALU.add))
        cg = p.op("scalar", lambda e: e.activation(gt[:], xc3t[:], AF.Silu), wait=[cxf, prev_st])
        p.op("vector", lambda e: e.tensor_tensor(gt[:], gt[:], s1t[:], ALU.mult), wait=[cg] + stoks)
        cx2 = p.op("vector", lambda e: e.tensor_tensor(gt[:], gt[:], x1t[:], ALU.add))
        prev_st = p.dma("sync", x2_out[t0:t0 + P, :], gt[:], wait=[cx2], inc="st")
    p.barrier("sync", [prev_st])
    return p.finalize()


_BUILD_CACHE = {}


def _get(name, builder, *a):
    key = (name,) + a
    if key not in _BUILD_CACHE:
        _BUILD_CACHE[key] = builder(*a)
    return _BUILD_CACHE[key]


def _tile3(w, kt):
    return np.ascontiguousarray(w.reshape(kt, P, -1).transpose(1, 0, 2))


def _pack32(v):
    return np.ascontiguousarray(np.asarray(v, F).reshape(-1, P).T)


def _packF2(v2):
    return np.ascontiguousarray(v2.reshape(2, 96, P).transpose(2, 1, 0).reshape(P, 192))


def kernel(**inputs):
    inp = {k: np.ascontiguousarray(np.asarray(v, dtype=F)) for k, v in inputs.items()}
    x = inp["L0_attn_x"].reshape(NTOK, D)
    R = list(range(NCORES))

    def run(nc, im):
        return run_bass_kernel_spmd(nc, im, core_ids=R).results

    def sl(c):
        return slice(c * TS, (c + 1) * TS)

    r = run(_get("prep", build_prep),
            [{"x_sl": x[sl(c)], "norm_w": inp["L0_attn_norm_w"][None, :]} for c in R])
    xn = np.concatenate([q["xn_sl"] for q in r], 0)
    h = np.concatenate([q["h_sl"] for q in r], 0)
    hT = np.ascontiguousarray(h.T)

    fold_in = stage_fold_inputs(inp["L0_ffn_up_weights"], inp["L0_ffn_down_weights"],
                                inp["L0_ffn_W_exp"], inp["L0_ffn_b_exp"], inp["L0_ffn_W_con"])
    r = run(_get("fold", build_fold), [fold_in] * NCORES)
    wexpfT, wconf = r[0]["wexpfT"], r[0]["wconf"]
    Wf = np.ascontiguousarray(wexpfT[:, :D].T)
    bf = np.ascontiguousarray(wexpfT[:, D])
    l0ffn_shared = {
        "wexpf_tiled": _tile3(Wf, 8), "wconf_tiled": _tile3(wconf, 32),
        "gbias1_packed": _pack32(bf), "gbias2_packed": _pack32(inp["L0_ffn_up_bias"]),
        "cbias2": np.ascontiguousarray(np.stack([inp["L0_ffn_b_con"], inp["L0_ffn_down_bias"]])),
    }

    r = run(_get("l0ffn", build_l0ffn),
            [dict(l0ffn_shared, xnT_sl=np.ascontiguousarray(xn[sl(c)].T)) for c in R])
    g0 = np.concatenate([q["g0_sl"] for q in r], 0)

    im = []
    for c in R:
        rows = slice(c * 8 * 16 * 17, (c + 1) * 8 * 16 * 17)
        waT = np.ascontiguousarray(inp["L0_attn_W_a"][rows].T)
        wvT = np.ascontiguousarray(inp["L0_attn_W_v"][c * P:(c + 1) * P].T)
        im.append({"hT_full": hT, "waT_tiled": _tile3(waT, 8), "wvT_tiled": _tile3(wvT, 8)})
    r = run(_get("l0attn", build_l0attn), im)
    y0 = np.stack([q["y0_sl"] for q in r])
    yfull = np.ascontiguousarray(
        y0.reshape(NCORES, 2, 8, 8, 1024, 16).transpose(0, 2, 5, 1, 3, 4).reshape(D, NTOK))

    woT_t = _tile3(np.ascontiguousarray(inp["L0_attn_W_out"].T), 8)
    im = [{"yT_sl": np.ascontiguousarray(yfull[:, sl(c)]), "woT_tiled": woT_t,
           "xn_sl": xn[sl(c)], "x_sl": x[sl(c)], "g0_sl": g0[sl(c)],
           "ln_w": inp["L1_attn_ln_w"][None, :], "ln_b": inp["L1_attn_ln_b"][None, :]}
          for c in R]
    r = run(_get("l0out", build_l0out), im)
    x1 = np.concatenate([q["x1_sl"] for q in r], 0)
    xn1 = np.concatenate([q["xn1_sl"] for q in r], 0)
    xn1n = np.concatenate([q["xn1n_sl"] for q in r], 0)
    xn1nT = np.ascontiguousarray(xn1n.T)

    im = []
    for c in R:
        rows = np.r_[256 * c:256 * c + 256, 2048 + 256 * c:2048 + 256 * c + 256]
        winT = np.ascontiguousarray(inp["L1_attn_w_in"][rows].T)
        wxT = np.ascontiguousarray(inp["L1_attn_w_xproj"][:, 256 * c:256 * c + 256].T)
        im.append({
            "xn1nT_full": xn1nT, "winT_tiled": _tile3(winT, 8),
            "convw_tiled": np.ascontiguousarray(
                inp["L1_attn_conv_w"][256 * c:256 * c + 256].reshape(2, P, 4).transpose(1, 0, 2)),
            "convb_tiled": np.ascontiguousarray(
                inp["L1_attn_conv_b"][256 * c:256 * c + 256].reshape(2, P).T),
            "wxT_tiled": np.ascontiguousarray(wxT.reshape(2, P, 96).transpose(1, 0, 2)),
        })
    r = run(_get("mamba1", build_mamba1), im)
    u_full = np.concatenate([q["u_sl"] for q in r], 0)
    z_full = np.concatenate([q["zsil_sl"] for q in r], 0)
    xdbl_parts = np.stack([q["xdbl_part"] for q in r])

    im = []
    for c in R:
        s2 = slice(256 * c, 256 * c + 256)
        im.append({
            "xdbl_parts": xdbl_parts,
            "wdtT_sl": np.ascontiguousarray(inp["L1_attn_w_dt"][s2].T),
            "bdt_tiled": np.ascontiguousarray(inp["L1_attn_b_dt"][s2].reshape(2, P).T),
            "alog_tiled": np.ascontiguousarray(
                inp["L1_attn_a_log"][s2].reshape(2, P, 16).transpose(1, 0, 2)),
            "dskip_tiled": np.ascontiguousarray(inp["L1_attn_d_skip"][s2].reshape(2, P).T),
            "u_sl": np.ascontiguousarray(u_full[s2]), "zsil_sl": np.ascontiguousarray(z_full[s2]),
        })
    r = run(_get("mamba2", build_mamba2), im)
    y1full = np.concatenate([q["y1_sl"] for q in r], 0)

    wmT = np.ascontiguousarray(inp["L1_ffn_Wm"].T)
    wpT = np.ascontiguousarray(inp["L1_ffn_Wp"].T)
    wc1T_t = np.ascontiguousarray(inp["L1_ffn_Wc1"].T.reshape(8, P, 256).transpose(1, 0, 2))
    wc2T_t = np.ascontiguousarray(inp["L1_ffn_Wc2"].T.reshape(2, P, 12288).transpose(1, 0, 2))
    base2 = np.zeros((2, 12288), F)
    base2[:, :4096] = 1.0
    active = _packF2(base2)
    history = _packF2(base2)
    zeros_rep = np.zeros((P, 1), F)
    done_for_blend = [zeros_rep, zeros_rep, None]
    done_ctl_in = zeros_rep
    xcT = {c: np.ascontiguousarray(xn1[sl(c)].T) for c in R}
    xnT = {}
    xcuT = {}
    for it in range(3):
        im = []
        for c in R:
            b = c // 4
            d = {"xcT_prev": xcT[c], "wmT_full": wmT, "wpT_full": wpT,
                 "mask_packed": np.ascontiguousarray(active[:, b::2])}
            if it > 0:
                d["xnewT_prev"] = xnT[c]
                d["done_rep"] = done_for_blend[it]
            im.append(d)
        r = run(_get("l1ffn_iter", build_l1ffn_iter, it == 0), im)
        for c in R:
            xnT[c] = r[c]["xnewT_sl"]
            if it > 0:
                xcuT[c] = r[c]["xcT_used"]
                xcT[c] = xcuT[c]
        if it == 2:
            break
        pooled_parts = np.stack([r[j]["pooled_part"] for j in R])
        cin = {"pooled_parts": pooled_parts, "wc1T_tiled": wc1T_t, "wc2T_tiled": wc2T_t,
               "history": history, "active": active, "done_rep": done_ctl_in}
        rc = run(_get("l1ffn_ctl", build_l1ffn_ctl), [cin] * NCORES)
        active = rc[0]["active_next"]
        history = rc[0]["history_next"]
        done_next = np.ascontiguousarray(rc[0]["done_next"])
        done_for_blend[2] = done_next if it == 0 else done_for_blend[2]
        if it == 1:
            done2 = done_next
        done_ctl_in = done_next

    wo1T_t = _tile3(np.ascontiguousarray(inp["L1_attn_w_out"].T), 16)
    im = [{"y1T_sl": np.ascontiguousarray(y1full[:, sl(c)]), "wo1T_tiled": wo1T_t,
           "xn1_sl": xn1[sl(c)], "x1_sl": x1[sl(c)],
           "xc3_sl": np.ascontiguousarray(xcuT[c].T), "xnew3_sl": np.ascontiguousarray(xnT[c].T),
           "done_rep": done2} for c in R]
    r = run(_get("final", build_final), im)
    x2 = np.concatenate([q["x2_sl"] for q in r], 0)
    return np.ascontiguousarray(x2.reshape(B, T, D))


# version 09
# speedup vs baseline: 1.0537x; 1.0537x over previous
import numpy as np
from contextlib import ExitStack

import concourse.bass as bass
import concourse.mybir as mybir
from concourse.bass_utils import run_bass_kernel_spmd

F32 = mybir.dt.float32
BF16 = mybir.dt.bfloat16
AF = mybir.ActivationFunctionType
ALU = mybir.AluOpType
AX = mybir.AxisListType

NCORES = 8
B, T, D = 2, 8192, 1024
NTOK = B * T
TS = NTOK // NCORES
P = 128
EPS = 1e-5

F = np.float32


class Prog:
    ENGINES = ("sync", "vector", "scalar", "gpsimd", "tensor")

    def __init__(self, name):
        self.name = name
        self.nc = bass.Bass("TRN2", target_bir_lowering=False, num_swdge_queues=2)
        self.ops = {e: [] for e in self.ENGINES}
        self.sem_counts = {}
        self.sem_names = []
        self.ctx = ExitStack()
        self._finalized = False

    def din(self, name, shape, dtype=F32):
        return self.nc.dram_tensor(name, list(shape), dtype, kind="ExternalInput").ap()

    def dout(self, name, shape, dtype=F32):
        return self.nc.dram_tensor(name, list(shape), dtype, kind="ExternalOutput").ap()

    def dtmp(self, name, shape, dtype=F32):
        return self.nc.dram_tensor(name, list(shape), dtype, kind="Internal").ap()

    def sb(self, name, shape, dtype=F32):
        return self.ctx.enter_context(self.nc.sbuf_tensor(name, list(shape), dtype))

    def ps(self, name, shape, dtype=F32):
        return self.ctx.enter_context(self.nc.psum_tensor(name, list(shape), dtype))

    def sem(self, name):
        if name not in self.sem_counts:
            self.sem_counts[name] = 0
            self.sem_names.append(name)
        return name

    CHAINED = ("vector", "scalar", "gpsimd")

    def op(self, eng, fn, wait=(), inc=None, inc_n=1, chain=True):
        _w = {}
        for w in wait:
            if w is not None:
                _w[w[0]] = max(_w.get(w[0], 0), w[1])
        wait = sorted(_w.items())
        if chain and eng in self.CHAINED:
            assert inc is None, "chained engines use their chain sem as milestone"
            csem = self.sem("c_" + eng)
            wait.append((csem, self.sem_counts[csem]))
            self.sem_counts[csem] += 1
            tok = (csem, self.sem_counts[csem])
            self.ops[eng].append((fn, tuple(wait), (csem, 1)))
            return tok
        if inc is not None:
            self.sem(inc)
            self.sem_counts[inc] += inc_n
            tok = (inc, self.sem_counts[inc])
            self.ops[eng].append((fn, tuple(wait), (inc, inc_n)))
            return tok
        self.ops[eng].append((fn, tuple(wait), None))
        return None

    def barrier(self, eng, wait):
        self.op(eng, lambda e: None, wait=wait, chain=False)

    def dma(self, eng, dst, src, wait=(), inc="ld"):
        return self.op(eng, lambda e: e.dma_start(dst, src), wait=wait, inc=inc,
                       inc_n=16, chain=False)

    def count(self, sem):
        return self.sem_counts.get(sem, 0)

    def finalize(self):
        assert not self._finalized
        self._finalized = True
        nc = self.nc
        semh = {n: self.ctx.enter_context(nc.semaphore(n)) for n in self.sem_names}
        block = self.ctx.enter_context(nc.Block())

        def emit(engine_obj, eng_name):
            for fn, waits, inc in self.ops[eng_name]:
                for (sname, cnt) in waits:
                    if cnt > 0:
                        engine_obj.wait_ge(semh[sname], cnt)
                inst = fn(engine_obj)
                if inc is not None:
                    assert inst is not None, f"op with inc on {eng_name} returned None"
                    inst.then_inc(semh[inc[0]], inc[1])

        if self.ops["sync"]:
            @block.sync
            def _(sync):
                emit(sync, "sync")
        if self.ops["vector"]:
            @block.vector
            def _(vector):
                emit(vector, "vector")
        if self.ops["scalar"]:
            @block.scalar
            def _(scalar):
                emit(scalar, "scalar")
        if self.ops["gpsimd"]:
            @block.gpsimd
            def _(gpsimd):
                emit(gpsimd, "gpsimd")
        if self.ops["tensor"]:
            @block.tensor
            def _(tensor):
                emit(tensor, "tensor")
        self.ctx.close()
        return nc


def emit_ln(p, x_t, out_t, scr, nrows=P, ncols=D, ld_wait=(), gain_bias=None, eps=EPS):
    p.op("vector", lambda e: e.tensor_reduce(scr["sum1"][:nrows], x_t[:nrows], AX.X, ALU.add), wait=ld_wait)
    p.op("vector", lambda e: e.tensor_scalar(scr["nm"][:nrows], scr["sum1"][:nrows], -1.0 / ncols, None, ALU.mult))
    c = p.op("vector", lambda e: e.tensor_scalar(scr["xm"][:nrows], x_t[:nrows], scr["nm"][:nrows], None, ALU.add))
    cs = p.op("scalar", lambda e: e.activation(scr["sq"][:nrows], scr["xm"][:nrows], AF.Square, accum_out=scr["rr"][:nrows]),
              wait=[c])
    cv = p.op("vector", lambda e: e.tensor_scalar(scr["rr"][:nrows], scr["rr"][:nrows], 1.0 / ncols, eps, ALU.mult, ALU.add),
              wait=[cs])
    cs = p.op("scalar", lambda e: e.activation(scr["sd"][:nrows], scr["rr"][:nrows], AF.Sqrt), wait=[cv])
    p.op("vector", lambda e: e.reciprocal(scr["rstd"][:nrows], scr["sd"][:nrows]), wait=[cs])
    if gain_bias is None:
        return p.op("vector", lambda e: e.tensor_scalar(out_t[:nrows], scr["xm"][:nrows], scr["rstd"][:nrows], None, ALU.mult))
    gain, bias = gain_bias
    p.op("vector", lambda e: e.tensor_scalar(scr["xm"][:nrows], scr["xm"][:nrows], scr["rstd"][:nrows], None, ALU.mult))
    p.op("vector", lambda e: e.tensor_tensor(scr["xm"][:nrows], scr["xm"][:nrows], gain[:nrows], ALU.mult))
    return p.op("vector", lambda e: e.tensor_tensor(out_t[:nrows], scr["xm"][:nrows], bias[:nrows], ALU.add))


def build_prep():
    p = Prog("prep")
    x_in = p.din("x_sl", [TS, D])
    nw_in = p.din("norm_w", [1, D])
    xn_out = p.dout("xn_sl", [TS, D])
    h_out = p.dout("h_sl", [TS, D])

    ntiles = TS // P
    xt = [p.sb(f"xt{i}", [P, D]) for i in range(2)]
    xnt = p.sb("xnt", [P, D])
    ht = p.sb("ht", [P, D])
    nwb = p.sb("nwb", [P, D])
    scr = {k: p.sb("scr_" + k, [P, D if k in ("sq", "xm", "xmr", "xg") else 1])
           for k in ("sum1", "nm", "rr", "sd", "rstd", "sq", "xm", "xmr", "xg")}
    scr2 = {"rr2": p.sb("rr2", [P, 1]), "sd2": p.sb("sd2", [P, 1]), "rstd2": p.sb("rstd2", [P, 1]),
            "sq2": p.sb("sq2", [P, D]), "hpre": p.sb("hpre", [P, D])}

    cnw = p.dma("sync", nwb[:], nw_in.broadcast_to([P, D]))
    for i in range(ntiles):
        buf = xt[i % 2]
        cl = p.dma("sync", buf[:], x_in[i * P:(i + 1) * P, :])
        cxn = emit_ln(p, buf, xnt, scr, ld_wait=[cl, cnw])
        cs = p.op("scalar", lambda e: e.activation(scr2["sq2"][:], xnt[:], AF.Square, accum_out=scr2["rr2"][:]),
                  wait=[cxn])
        cv = p.op("vector", lambda e: e.tensor_scalar(scr2["rr2"][:], scr2["rr2"][:], 1.0 / D, EPS, ALU.mult, ALU.add),
                  wait=[cs])
        cs = p.op("scalar", lambda e: e.activation(scr2["sd2"][:], scr2["rr2"][:], AF.Sqrt), wait=[cv])
        p.op("vector", lambda e: e.reciprocal(scr2["rstd2"][:], scr2["sd2"][:]), wait=[cs])
        ch = p.op("vector", lambda e: e.scalar_tensor_tensor(
            ht[:], xnt[:], scr2["rstd2"][:], nwb[:], ALU.mult, ALU.mult))
        c1 = p.dma("sync", xn_out[i * P:(i + 1) * P, :], xnt[:], wait=[cxn], inc="st")
        c2 = p.dma("sync", h_out[i * P:(i + 1) * P, :], ht[:], wait=[ch], inc="st")
        p.barrier("vector", [c2])
        p.barrier("scalar", [c2])
    p.barrier("sync", [c2])
    return p.finalize()


def mm_group(p, psum_ap, mms, psum_free=None):
    n = len(mms)
    tok = None
    for i, (l, r, w) in enumerate(mms):
        w = list(w)
        if i == 0 and psum_free is not None:
            w.append(psum_free)
        tok = p.op("tensor",
                   (lambda e, l=l, r=r, s=(i == 0), t=(i == n - 1):
                    e.matmul(psum_ap, l, r, start=s, stop=t)),
                   wait=w, inc="pe" if i == n - 1 else None, chain=False)
    return tok


def _butterfly_stage_blocks(weights):
    w = np.asarray(weights, dtype=F)
    lo = np.zeros((6, 64, 64, 64), F)
    hi = np.zeros((6, 64, 64, 64), F)
    for k in range(12):
        s = 2 ** k
        nb = 4096 // (2 * s)
        wk = w[k].reshape(nb, s, 2, 2)
        b_idx = np.arange(nb)[:, None]
        pos = np.arange(s)[None, :]
        i0 = b_idx * 2 * s + pos
        i1 = i0 + s
        if k < 6:
            o = i0 // 64
            n0, n1 = i0 % 64, i1 % 64
            lo[k, o, n0, n0] = wk[..., 0, 0]
            lo[k, o, n0, n1] = wk[..., 0, 1]
            lo[k, o, n1, n0] = wk[..., 1, 0]
            lo[k, o, n1, n1] = wk[..., 1, 1]
        else:
            n = i0 % 64
            o0, o1 = i0 // 64, i1 // 64
            hi[k - 6, n, o0, o0] = wk[..., 0, 0]
            hi[k - 6, n, o0, o1] = wk[..., 0, 1]
            hi[k - 6, n, o1, o0] = wk[..., 1, 0]
            hi[k - 6, n, o1, o1] = wk[..., 1, 1]
    return lo, hi


def _pair_blockdiag(blocks):
    t = np.zeros((32, 128, 128), F)
    t[:, :64, :64] = blocks[0::2]
    t[:, 64:, 64:] = blocks[1::2]
    return t


def stage_fold_inputs(up_weights, down_weights, W_exp, b_exp, W_con):
    lo_u, hi_u = _butterfly_stage_blocks(up_weights)
    lo_d, hi_d = _butterfly_stage_blocks(down_weights)
    tr = lambda b: np.ascontiguousarray(np.transpose(b, (0, 2, 1)))
    ch = np.zeros((4, 6, 32, 128, 128), F)
    ch[0, 0] = _pair_blockdiag(tr(lo_u[5]))
    for j in range(1, 6):
        ch[0, j] = _pair_blockdiag(lo_u[5 - j])
    ch[1, 0] = _pair_blockdiag(tr(hi_u[5]))
    for j in range(1, 6):
        ch[1, j] = _pair_blockdiag(hi_u[5 - j])
    ch[2, 0] = _pair_blockdiag(hi_d[0])
    for j in range(1, 6):
        ch[2, j] = _pair_blockdiag(tr(hi_d[j]))
    ch[3, 0] = _pair_blockdiag(lo_d[0])
    for j in range(1, 6):
        ch[3, j] = _pair_blockdiag(tr(lo_d[j]))

    wexpT_ext = np.zeros((4096, 1032), F)
    wexpT_ext[:, :D] = np.asarray(W_exp, F).T
    wexpT_ext[:, D] = np.asarray(b_exp, F)
    wcon = np.asarray(W_con, F)
    rows = (np.arange(4096).reshape(64, 64).T).reshape(-1)
    wcon_perm = np.ascontiguousarray(wcon[rows])
    return {"fold_ch": ch, "wexpT_ext": wexpT_ext, "wcon_perm": wcon_perm}


def build_fold():
    p = Prog("fold")
    ch_in = p.din("fold_ch", [4, 6, 32, 128, 128])
    wx_in = p.din("wexpT_ext", [4096, 1032])
    wc_in = p.din("wcon_perm", [4096, 1024])
    wxf_out = p.dout("wexpfT", [4096, 1032])
    wcf_out = p.dout("wconf", [4096, 1024])
    ftmp = p.dtmp("ftmp", [4096, 1032])

    chain_sb = p.sb("chain_sb", [P, 4 * 32 * P])
    stg = p.sb("stg", [P, 6 * P])
    src = [p.sb(f"src{i}", [P, 1032]) for i in range(2)]
    osb = [p.sb(f"osb{i}", [P, 1032]) for i in range(2)]
    pst = [p.ps(f"pst{i}", [P, 512]) for i in range(2)]

    ev_last = {0: None, 1: None}
    for c in range(4):
        for t in range(32):
            slot = chain_sb[:, (c * 32 + t) * P:(c * 32 + t + 1) * P]
            cl = p.dma("sync", stg[:].rearrange("r (j k) -> r j k", j=6),
                       ch_in[c, :, t].transpose([1, 0, 2]))
            cseed = p.op("vector", lambda e, slot=slot: e.tensor_copy(slot, stg[:, 0:P]), wait=[cl])
            for j in range(1, 6):
                pb = pst[j % 2]
                tok = mm_group(p, pb[:, 0:P],
                               [(stg[:, j * P:(j + 1) * P], slot, [cl, cseed])],
                               psum_free=ev_last[j % 2])
                cseed = p.op("vector", lambda e, slot=slot, pb=pb: e.tensor_copy(slot, pb[:, 0:P]), wait=[tok])
                ev_last[j % 2] = cseed
            p.barrier("sync", [tok])
    chains_done = cseed

    passes = [
        (wx_in, 0, 1032, ftmp, True),
        (ftmp, 1, 1032, wxf_out, True),
        (wc_in, 2, 1024, ftmp, True),
        (ftmp, 3, 1024, wcf_out, False),
    ]
    prev_pass_toks = [chains_done]
    for (src_ap, c, ncols, dst_ap, perm) in passes:
        tile_toks = []
        for t in range(32):
            sbuf_src = src[t % 2]
            sbuf_out = osb[t % 2]
            cl = p.dma("sync", sbuf_src[:, :ncols], src_ap[t * P:(t + 1) * P, :ncols],
                       wait=list(prev_pass_toks) + ([tile_toks[-2]] if len(tile_toks) >= 2 else []))
            stat = chain_sb[:, (c * 32 + t) * P:(c * 32 + t + 1) * P]
            war = [tile_toks[-2]] if len(tile_toks) >= 2 else []
            ev = None
            nch = (ncols + 511) // 512
            for q in range(nch):
                n0, n1 = q * 512, min((q + 1) * 512, ncols)
                pb = pst[q % 2]
                tok = mm_group(p, pb[:, 0:n1 - n0],
                               [(stat, sbuf_src[:, n0:n1], [cl, chains_done])],
                               psum_free=ev_last[q % 2])
                ev = p.op("vector", lambda e, pb=pb, sbuf_out=sbuf_out, n0=n0, n1=n1:
                          e.tensor_copy(sbuf_out[:, n0:n1], pb[:, 0:n1 - n0]),
                          wait=[tok] + war)
                ev_last[q % 2] = ev
            stsem = "st%d" % (t % 2)
            if perm:
                dst_v = dst_ap[:, 0:ncols].rearrange("(b a) c -> b a c", a=64)
                p.dma("sync", dst_v[:, 2 * t, :], sbuf_out[0:64, :ncols], wait=[ev], inc=stsem)
                stok = p.dma("sync", dst_v[:, 2 * t + 1, :], sbuf_out[64:128, :ncols],
                             wait=[ev], inc=stsem)
            else:
                stok = p.dma("sync", dst_ap[t * P:(t + 1) * P, :ncols], sbuf_out[:, :ncols],
                             wait=[ev], inc=stsem)
            tile_toks.append(stok)
        prev_pass_toks = [tile_toks[-1], tile_toks[-2]]
    p.barrier("sync", prev_pass_toks)
    return p.finalize()


def build_l0ffn():
    p = Prog("l0ffn")
    xnT_in = p.din("xnT_sl", [D, TS])
    wexpf_in = p.din("wexpf_tiled", [P, 8, 4096])
    wconf_in = p.din("wconf_tiled", [P, 32, D])
    gb1_in = p.din("gbias1_packed", [P, 32])
    gb2_in = p.din("gbias2_packed", [P, 32])
    cb_in = p.din("cbias2", [2, D])
    g0_out = p.dout("g0_sl", [TS, D])

    KT, MT = 8, 32
    xnTb_all = p.sb("xnTb_all", [P, KT * 512], BF16)
    wexpfb_all = p.sb("wexpfb_all", [P, KT * 4096], BF16)
    wconfb_all = p.sb("wconfb_all", [P, MT * D], BF16)
    xnTb = [xnTb_all[:, k * 512:(k + 1) * 512] for k in range(KT)]
    wexpfb = [wexpfb_all[:, k * 4096:(k + 1) * 4096] for k in range(KT)]
    wconfb = [wconfb_all[:, m * D:(m + 1) * D] for m in range(MT)]
    Hb = p.sb("Hb", [P, MT * 512], BF16)
    gbias = p.sb("gbias", [P, 32])
    cbias = p.sb("cbias", [P, D])
    cb_tmp = p.sb("cb_tmp", [P, D])
    ysb = p.sb("ysb", [P, D])
    gsb = p.sb("gsb", [P, D])
    pst = [p.ps(f"pst{i}", [P, 512]) for i in range(2)]

    w1 = None
    for q in range(4):
        w1 = p.dma("gpsimd", wexpfb_all[:].rearrange("p (k c) -> p k c", k=KT)[:, 2 * q:2 * q + 2],
                   wexpf_in[:, 2 * q:2 * q + 2], wait=[w1], inc="ldw1")
    w2 = None
    for q in range(4):
        w2 = p.dma("gpsimd", wconfb_all[:].rearrange("p (m c) -> p m c", m=MT)[:, 8 * q:8 * q + 8],
                   wconf_in[:, 8 * q:8 * q + 8], wait=[w2], inc="ldw2")
    b1 = p.dma("sync", gbias[:], gb1_in, inc="ld")
    b2 = p.dma("sync", cb_tmp[:, 0:32], gb2_in, wait=[b1], inc="ld")
    gb_tok = p.op("vector", lambda e: e.tensor_tensor(gbias[:], gbias[:], cb_tmp[:, 0:32], ALU.add),
                  wait=[b2])
    b3 = p.dma("sync", cbias[:], cb_in[0:1, :].broadcast_to([P, D]), wait=[b2], inc="ld")
    cld = p.dma("sync", cb_tmp[:], cb_in[1:2, :].broadcast_to([P, D]), wait=[gb_tok, b3], inc="ld")
    cbias_tok = p.op("vector", lambda e: e.tensor_tensor(cbias[:], cbias[:], cb_tmp[:], ALU.add),
                     wait=[cld])

    ev_last = {0: None, 1: None}
    prev_y_tok = None
    last_sil = None
    last_st = None
    for g in range(4):
        xg_tok = p.dma("gpsimd",
                       xnTb_all[:].rearrange("p (k c) -> p k c", k=KT),
                       xnT_in.rearrange("(k p) t -> p k t", p=P)[:, :, g * 512:(g + 1) * 512],
                       wait=[prev_y_tok], inc="ldx")
        h_done = None
        for m in range(MT):
            pb = pst[m % 2]
            mms = [(wexpfb[k][:, m * P:(m + 1) * P], xnTb[k], [xg_tok, w1]) for k in range(KT)]
            tok = mm_group(p, pb[:], mms, psum_free=ev_last[m % 2])
            h_done = p.op("scalar", lambda e, pb=pb, m=m: e.activation(
                Hb[:, m * 512:(m + 1) * 512], pb[:], AF.Gelu, bias=gbias[:, m:m + 1]),
                wait=[tok, gb_tok])
            ev_last[m % 2] = h_done
        for tsub in range(4):
            ytoks = []
            for nch in range(2):
                pb = pst[nch]
                mms = [(Hb[:, m * 512 + tsub * P: m * 512 + (tsub + 1) * P],
                        wconfb[m][:, nch * 512:(nch + 1) * 512], [h_done, w2]) for m in range(MT)]
                tok = mm_group(p, pb[:], mms, psum_free=ev_last[nch])
                ytok = p.op("vector", lambda e, pb=pb, nch=nch: e.tensor_tensor(
                    ysb[:, nch * 512:(nch + 1) * 512], pb[:], cbias[:, nch * 512:(nch + 1) * 512], ALU.add),
                    wait=[tok, cbias_tok, last_sil])
                ev_last[nch] = ytok
                ytoks.append(ytok)
            last_sil = p.op("scalar", lambda e: e.activation(gsb[:], ysb[:], AF.Silu),
                            wait=ytoks + [last_st])
            row0 = g * 512 + tsub * P
            last_st = p.dma("sync", g0_out[row0:row0 + P, :], gsb[:], wait=[last_sil], inc="st")
            prev_y_tok = tok
    p.barrier("sync", [last_st])
    return p.finalize()


def build_l0attn():
    p = Prog("l0attn")
    hT_in = p.din("hT_full", [D, NTOK])
    waT_in = p.din("waT_tiled", [P, 8, 2176])
    wvT_in = p.din("wvT_tiled", [P, 8, P])
    y_out = p.dout("y0_sl", [P, 1024, 16])
    A_hbm = p.dtmp("A_hbm", [P, 1024, 272])

    waTb = p.sb("waTb", [P, 8 * 2176], BF16)
    wvTb = p.sb("wvTb", [P, 8 * P], BF16)
    hTb = [p.sb(f"hTb{i}", [P, 8 * P], BF16) for i in range(2)]
    gates = p.sb("gates", [P, 2176])
    red = p.sb("red", [P, P])
    v_sb = p.sb("v_sb", [P, P])
    ps_g = [p.ps(f"ps_g{i}", [P, 512]) for i in range(2)]
    ps_v = p.ps("ps_v", [P, P])

    w1 = None
    for q in range(2):
        w1 = p.dma("gpsimd", waTb[:].rearrange("p (k c) -> p k c", k=8)[:, 4 * q:4 * q + 4],
                   waT_in[:, 4 * q:4 * q + 4], wait=[w1], inc="ldw1")
    w2 = p.dma("gpsimd", wvTb[:].rearrange("p (k c) -> p k c", k=8), wvT_in, inc="ldw2")

    ev_last = {0: None, 1: None}
    prev = {}
    last_store = None
    hT_v = hT_in.rearrange("(k p) t -> p k t", p=P)
    for b in range(2):
        for tt in range(64):
            t0 = b * T + tt * P
            hbuf = hTb[tt % 2]
            lx = p.dma("gpsimd", hbuf[:].rearrange("p (k c) -> p k c", k=8),
                       hT_v[:, :, t0:t0 + P], wait=[prev.get(("mmv", tt % 2))],
                       inc="ldx%d" % (tt % 2))
            etoks = []
            for q in range(5):
                n0, n1 = q * 512, min((q + 1) * 512, 2176)
                pb = ps_g[q % 2]
                mms = [(hbuf[:, k * P:(k + 1) * P], waTb[:, k * 2176 + n0:k * 2176 + n1], [lx, w1])
                       for k in range(8)]
                tok = mm_group(p, pb[:, 0:n1 - n0], mms, psum_free=ev_last[q % 2])
                ev = p.op("scalar", lambda e, pb=pb, n0=n0, n1=n1: e.activation(
                    gates[:, n0:n1], pb[:, 0:n1 - n0], AF.Exp),
                    wait=[tok, prev.get("gates_free")])
                etoks.append(ev)
                ev_last[q % 2] = ev
            mms = [(hbuf[:, k * P:(k + 1) * P], wvTb[:, k * P:(k + 1) * P], [lx, w2])
                   for k in range(8)]
            vtok = mm_group(p, ps_v[:], mms, psum_free=prev.get("v_free"))
            prev[("mmv", tt % 2)] = vtok
            vev = p.op("vector", lambda e: e.tensor_copy(v_sb[:], ps_v[:]),
                       wait=[vtok, prev.get("u_done")])
            prev["v_free"] = vev
            c1 = p.op("vector", lambda e: e.tensor_reduce(
                red[:], gates[:].rearrange("p (g j) -> p g j", j=17), AX.X, ALU.add),
                wait=etoks)
            c2 = p.op("vector", lambda e: e.reciprocal(red[:], red[:]))
            c3 = p.op("vector", lambda e: e.tensor_tensor(
                gates[:].rearrange("p (g j) -> p g j", j=17), gates[:].rearrange("p (g j) -> p g j", j=17),
                red[:].unsqueeze(2).broadcast_to([P, P, 17]), ALU.mult))
            a0v = gates[:].rearrange("p (g j) -> p g j", j=17)[:, :, 0]
            u_done = p.op("vector", lambda e: e.tensor_tensor(a0v, a0v, v_sb[:], ALU.mult),
                          wait=[vev])
            prev["u_done"] = u_done
            chunk = tt // 8
            j0 = (tt % 8) * P
            dst = A_hbm[b * 64 + chunk: b * 64 + chunk + 57: 8, j0:j0 + P, :].transpose([1, 0, 2])
            last_store = p.dma("sync", dst, gates[:].rearrange("p (g c) -> p g c", g=8),
                               wait=[u_done, c3], inc="stA%d" % (tt % 2))
            prev["gates_free"] = last_store
            prev[("stA", tt % 2)] = last_store
    p1_done = [prev[("stA", 0)], prev[("stA", 1)]]

    JB = 32
    NB = 1024 // JB
    Ablk = [p.sb(f"Ablk{i}", [P, JB * 272]) for i in range(2)]
    yloc = p.sb("yloc", [P, 1025 * 17])
    tmp = p.sb("tmp", [P, 272])
    cbuf = p.sb("cbuf", [P, 2 * 17])
    initz = p.sb("initz", [P, 17])

    yv = yloc[:].rearrange("p (j s) -> p j s", s=17)
    m0 = p.op("vector", lambda e: e.memset(yloc[:], 0.0))
    m1 = p.op("vector", lambda e: e.memset(yv[:, :, 0:1], 1.0))

    def scan_pass(homog, extra_wait):
        toks = []
        lbt = {0: None, 1: None}
        for jb in range(NB):
            buf = Ablk[jb % 2]
            lw = list(p1_done) + list(extra_wait)
            if lbt[jb % 2] is not None:
                lw.append(lbt[jb % 2])
            la = p.dma("sync", buf[:].rearrange("p (j c) -> p j c", j=JB),
                       A_hbm[:, jb * JB:(jb + 1) * JB, :],
                       wait=lw, inc="ldA%d_%d" % (jb % 2, int(homog)))
            for jj in range(JB):
                j = jb * JB + jj
                asl = buf[:, jj * 272:(jj + 1) * 272]
                av = asl.rearrange("p (i s) -> p i s", s=17)
                if not homog:
                    state = yv[:, j, :]
                    p.op("vector", lambda e, av=av, state=state: e.tensor_tensor(
                        tmp[:].rearrange("p (i s) -> p i s", s=17), av,
                        state.unsqueeze(1).broadcast_to([P, 16, 17]), ALU.mult),
                        wait=[la, m0, m1])
                    tok = p.op("vector", lambda e, j=j: e.tensor_reduce(
                        yv[:, j + 1, 1:17], tmp[:].rearrange("p (i s) -> p i s", s=17),
                        AX.X, ALU.add))
                else:
                    cst = cbuf[:, (j % 2) * 17:(j % 2) * 17 + 17]
                    cdst = cbuf[:, ((j + 1) % 2) * 17 + 1:((j + 1) % 2) * 17 + 17]
                    p.op("vector", lambda e, av=av, cst=cst: e.tensor_tensor(
                        tmp[:].rearrange("p (i s) -> p i s", s=17), av,
                        cst.unsqueeze(1).broadcast_to([P, 16, 17]), ALU.mult),
                        wait=[la])
                    p.op("vector", lambda e, cdst=cdst: e.tensor_reduce(
                        cdst, tmp[:].rearrange("p (i s) -> p i s", s=17), AX.X, ALU.add))
                    tok = p.op("vector", lambda e, j=j: e.tensor_tensor(
                        yv[:, j + 1, 1:17], yv[:, j + 1, 1:17],
                        cbuf[:, ((j + 1) % 2) * 17 + 1:((j + 1) % 2) * 17 + 17], ALU.add))
            lbt[jb % 2] = tok
            toks.append(tok)
        return toks[-1]

    pass1_done = scan_pass(False, [])
    mz = p.op("vector", lambda e: e.memset(initz[:], 0.0), wait=[pass1_done])
    sh = []
    for q in range(16):
        sh.append(p.dma("sync", initz[q * 8 + 1:q * 8 + 8, 1:17], yv[q * 8:q * 8 + 7, 1024, 1:17],
                        wait=[mz], inc="shf%d" % q))
    cseed = p.op("vector", lambda e: e.memset(cbuf[:], 0.0), wait=[pass1_done])
    cseed = p.op("vector", lambda e: e.tensor_copy(cbuf[:, 1:17], initz[:, 1:17]), wait=sh + [cseed])
    pass2_done = scan_pass(True, [cseed, pass1_done])
    st1 = p.dma("sync", y_out[:, 0:512, :], yv[:, 1:513, 1:17], wait=[pass2_done], inc="sty")
    stok = p.dma("sync", y_out[:, 512:1024, :], yv[:, 513:1025, 1:17], wait=[st1], inc="sty")
    p.barrier("sync", [stok])
    return p.finalize()


def build_l0out():
    p = Prog("l0out")
    yT_in = p.din("yT_sl", [D, TS])
    woT_in = p.din("woT_tiled", [P, 8, D])
    xn_in = p.din("xn_sl", [TS, D])
    x_in = p.din("x_sl", [TS, D])
    g0_in = p.din("g0_sl", [TS, D])
    lnw_in = p.din("ln_w", [1, D])
    lnb_in = p.din("ln_b", [1, D])
    x1_out = p.dout("x1_sl", [TS, D])
    xn1_out = p.dout("xn1_sl", [TS, D])
    xn1n_out = p.dout("xn1n_sl", [TS, D])

    KT = 8
    yTb = p.sb("yTb", [P, KT * P], BF16)
    woTb = p.sb("woTb", [P, KT * D], BF16)
    xnt = p.sb("xnt", [P, D])
    xt = p.sb("xt", [P, D])
    g0t = p.sb("g0t", [P, D])
    s0t = p.sb("s0t", [P, D])
    x1t = p.sb("x1t", [P, D])
    xn1t = p.sb("xn1t", [P, D])
    xn1nt = p.sb("xn1nt", [P, D])
    lnwb = p.sb("lnwb", [P, D])
    lnbb = p.sb("lnbb", [P, D])
    scr = {k: p.sb("scr_" + k, [P, D if k in ("sq", "xm", "xmr", "xg") else 1])
           for k in ("sum1", "nm", "rr", "sd", "rstd", "sq", "xm", "xmr", "xg")}
    pst = [p.ps(f"pst{i}", [P, 512]) for i in range(2)]

    w1 = p.dma("gpsimd", woTb[:].rearrange("p (k c) -> p k c", k=KT), woT_in, inc="ldw")
    lw1 = p.dma("sync", lnwb[:], lnw_in.broadcast_to([P, D]), inc="ldc")
    lw2 = p.dma("sync", lnbb[:], lnb_in.broadcast_to([P, D]), wait=[lw1], inc="ldc")

    yT_v = yT_in.rearrange("(k p) t -> p k t", p=P)
    ev_last = {0: None, 1: None}
    prev_st = None
    prev_mm = None
    for i in range(TS // P):
        t0 = i * P
        ly = p.dma("gpsimd", yTb[:].rearrange("p (k c) -> p k c", k=KT)[:, :, 0:P],
                   yT_v[:, :, t0:t0 + P], wait=[prev_mm], inc="ldy")
        l1 = p.dma("sync", xnt[:], xn_in[t0:t0 + P, :], wait=[prev_st], inc="ldt")
        l2 = p.dma("sync", xt[:], x_in[t0:t0 + P, :], wait=[l1], inc="ldt")
        l3 = p.dma("sync", g0t[:], g0_in[t0:t0 + P, :], wait=[l2], inc="ldt")
        stoks = []
        for nch in range(2):
            pb = pst[nch]
            mms = [(yTb[:, k * P:(k + 1) * P], woTb[:, k * D + nch * 512:k * D + (nch + 1) * 512],
                    [ly, w1]) for k in range(KT)]
            tok = mm_group(p, pb[:], mms, psum_free=ev_last[nch])
            prev_mm = tok
            ev = p.op("vector", lambda e, pb=pb, nch=nch: e.tensor_tensor(
                s0t[:, nch * 512:(nch + 1) * 512], pb[:], xnt[:, nch * 512:(nch + 1) * 512], ALU.add),
                wait=[tok, l3])
            ev_last[nch] = ev
            stoks.append(ev)
        p.op("vector", lambda e: e.tensor_tensor(x1t[:], s0t[:], g0t[:], ALU.mult), wait=stoks)
        cx1 = p.op("vector", lambda e: e.tensor_tensor(x1t[:], x1t[:], xt[:], ALU.add))
        cxn1 = emit_ln(p, x1t, xn1t, scr)
        cxn1n = emit_ln(p, xn1t, xn1nt, scr, ld_wait=[lw2], gain_bias=(lnwb, lnbb))
        s1 = p.dma("sync", x1_out[t0:t0 + P, :], x1t[:], wait=[cx1], inc="st")
        s2 = p.dma("sync", xn1_out[t0:t0 + P, :], xn1t[:], wait=[cxn1, s1], inc="st")
        prev_st = p.dma("sync", xn1n_out[t0:t0 + P, :], xn1nt[:], wait=[cxn1n, s2], inc="st")
    p.barrier("sync", [prev_st])
    return p.finalize()


def build_mamba1(debug_xs=False):
    p = Prog("mamba1")
    xnT_in = p.din("xn1nT_full", [D, NTOK])
    winT_in = p.din("winT_tiled", [P, 8, 512])
    convw_in = p.din("convw_tiled", [P, 2, 4])
    convb_in = p.din("convb_tiled", [P, 2])
    wxT_in = p.din("wxT_tiled", [P, 2, 96])
    u_out = p.dout("u_sl", [256, NTOK])
    z_out = p.dout("zsil_sl", [256, NTOK])
    xd_out = p.dout("xdbl_part", [96, NTOK])

    PADW = T + 3
    winTb = p.sb("winTb", [P, 8 * 512], BF16)
    wxTb = p.sb("wxTb", [P, 2 * 96], BF16)
    convw = p.sb("convw", [P, 2 * 4])
    convb = p.sb("convb", [P, 2])
    xnTb = p.sb("xnTb", [P, 8 * 512], BF16)
    xs_pad = p.sb("xs_pad", [P, 2 * PADW])
    xca = p.sb("xca", [P, T])
    xcb = p.sb("xcb", [P, T])
    zch = p.sb("zch", [P, 512])
    ub = p.sb("ub", [P, 2 * 512], BF16)
    xde = p.sb("xde", [P, 512])
    pst = [p.ps(f"pst{i}", [P, 512]) for i in range(2)]

    w1 = p.dma("gpsimd", winTb[:].rearrange("p (k c) -> p k c", k=8), winT_in, inc="ldw")
    w2 = p.dma("gpsimd", wxTb[:].rearrange("p (k c) -> p k c", k=2), wxT_in, wait=[w1], inc="ldw")
    c1 = p.dma("sync", convw[:].rearrange("p (t k) -> p t k", t=2), convw_in, inc="ldc")
    c2 = p.dma("sync", convb[:], convb_in, wait=[c1], inc="ldc")

    xnT_v = xnT_in.rearrange("(k p) t -> p k t", p=P)
    mpad = p.op("vector", lambda e: e.memset(xs_pad[:], 0.0))

    ev_last = {0: None, 1: None}
    prev_mm = None
    last_u_store = None
    last_z = None
    for et in range(2):
        for tc in range(32):
            b, tb = tc // 16, (tc % 16) * 512
            t0 = tc * 512
            lx = p.dma("gpsimd", xnTb[:].rearrange("p (k c) -> p k c", k=8),
                       xnT_v[:, :, t0:t0 + 512], wait=[prev_mm], inc="ldx")
            mms = [(winTb[:, k * 512 + et * P:k * 512 + (et + 1) * P], xnTb[:, k * 512:(k + 1) * 512],
                    [lx, w2]) for k in range(8)]
            tokx = mm_group(p, pst[0][:], mms, psum_free=ev_last[0])
            exs = p.op("vector", lambda e, b=b, tb=tb: e.tensor_copy(
                xs_pad[:, b * PADW + 3 + tb: b * PADW + 3 + tb + 512], pst[0][:]),
                wait=[tokx, mpad])
            ev_last[0] = exs
            mms = [(winTb[:, k * 512 + (2 + et) * P:k * 512 + (3 + et) * P], xnTb[:, k * 512:(k + 1) * 512],
                    [lx, w2]) for k in range(8)]
            tokz = mm_group(p, pst[1][:], mms, psum_free=ev_last[1])
            prev_mm = tokz
            ez = p.op("scalar", lambda e: e.activation(zch[:], pst[1][:], AF.Silu),
                      wait=[tokz, last_z])
            ev_last[1] = ez
            last_z = p.dma("sync", z_out[et * P:(et + 1) * P, t0:t0 + 512], zch[:],
                           wait=[ez], inc="stz")
        dbg_tok = None
        if debug_xs and et == 0:
            xs_dbg = p.dout("xs_dbg", [P, 2 * PADW])
            dbg_tok = p.dma("sync", xs_dbg, xs_pad[:], wait=[exs], inc="stdbg")
        for b in range(2):
            off = b * PADW
            cv0 = p.op("vector", lambda e, off=off, et=et: e.tensor_scalar(
                xca[:], xs_pad[:, off:off + T], convw[:, 4 * et:4 * et + 1],
                convb[:, et:et + 1], ALU.mult, ALU.add), wait=[exs, c2, last_u_store])
            cv1 = p.op("vector", lambda e, off=off, et=et: e.scalar_tensor_tensor(
                xcb[:], xs_pad[:, off + 1:off + 1 + T], convw[:, 4 * et + 1:4 * et + 2],
                xca[:], ALU.mult, ALU.add))
            cv2 = p.op("vector", lambda e, off=off, et=et: e.scalar_tensor_tensor(
                xca[:], xs_pad[:, off + 2:off + 2 + T], convw[:, 4 * et + 2:4 * et + 3],
                xcb[:], ALU.mult, ALU.add))
            cv3 = p.op("vector", lambda e, off=off, et=et: e.scalar_tensor_tensor(
                xcb[:], xs_pad[:, off + 3:off + 3 + T], convw[:, 4 * et + 3:4 * et + 4],
                xca[:], ALU.mult, ALU.add))
            cu = p.op("scalar", lambda e: e.activation(xca[:], xcb[:], AF.Silu), wait=[cv3])
            last_u_store = p.dma("sync", u_out[et * P:(et + 1) * P, b * T:(b + 1) * T], xca[:],
                                 wait=[cu], inc="stu")
        if et == 0:
            mpad = p.op("vector", lambda e: e.memset(xs_pad[:], 0.0),
                        wait=[cv3, last_u_store, dbg_tok])
    prev_ev = None
    last_xd = None
    for tc in range(32):
        t0 = tc * 512
        lu = p.dma("gpsimd", ub[:].rearrange("p (t c) -> p t c", t=2),
                   u_out.rearrange("(t p) n -> p t n", p=P)[:, :, t0:t0 + 512],
                   wait=[last_u_store, prev_ev], inc="ldu")
        mms = [(wxTb[:, k * 96:(k + 1) * 96], ub[:, k * 512:(k + 1) * 512], [lu, w2])
               for k in range(2)]
        tok = mm_group(p, pst[0][:96, :], mms, psum_free=prev_ev)
        prev_ev = p.op("vector", lambda e: e.tensor_copy(xde[:96, :], pst[0][:96, :]),
                       wait=[tok, last_xd])
        last_xd = p.dma("sync", xd_out[:, t0:t0 + 512], xde[:96, :], wait=[prev_ev], inc="stx")
    p.barrier("sync", [last_xd, last_u_store, last_z])
    return p.finalize()


def build_mamba2():
    p = Prog("mamba2")
    xdp_in = p.din("xdbl_parts", [NCORES, 96, NTOK])
    wdtT_in = p.din("wdtT_sl", [64, 256])
    bdt_in = p.din("bdt_tiled", [P, 2])
    alog_in = p.din("alog_tiled", [P, 2, 16])
    dskip_in = p.din("dskip_tiled", [P, 2])
    u_in = p.din("u_sl", [256, NTOK])
    z_in = p.din("zsil_sl", [256, NTOK])
    y_out = p.dout("y1_sl", [256, NTOK])
    bc_dram = p.dtmp("bc_dram", [32, 256])

    TB = 256
    NTB = T // TB
    xdbl = p.sb("xdbl", [96, NTOK])
    xdp = [p.sb(f"xdp{i}", [96, 2048]) for i in range(2)]
    wdtT = p.sb("wdtT", [64, 256])
    bdt = p.sb("bdt", [P, 2])
    A_sb = p.sb("A_sb", [P, 32])
    dskip = p.sb("dskip", [P, 2])
    neg1 = p.sb("neg1", [P, 1])
    one1 = p.sb("one1", [P, 1])
    dt_blk = p.sb("dt_blk", [P, TB])
    t_abs = p.sb("t_abs", [P, TB])
    t_e = p.sb("t_e", [P, TB])
    t_l = p.sb("t_l", [P, TB])
    t_r = p.sb("t_r", [P, TB])
    bufA = p.sb("bufA", [P, 16 * TB])
    bufB = p.sb("bufB", [P, 16 * TB])
    bufC = p.sb("bufC", [P, 16 * TB])
    bcast = p.sb("bcast", [P, 16 * TB])
    du = p.sb("du", [P, TB])
    ub = p.sb("ub", [P, TB])
    zb = p.sb("zb", [P, TB])
    yb = p.sb("yb", [P, TB])
    carry = p.sb("carry", [P, 16])
    pst = [p.ps(f"pst{i}", [P, 512]) for i in range(2)]

    m1 = p.op("vector", lambda e: e.memset(neg1[:], -1.0))
    m2 = p.op("vector", lambda e: e.memset(one1[:], 1.0))
    lw = p.dma("sync", wdtT[:], wdtT_in, inc="ldw")
    l1 = p.dma("sync", bdt[:], bdt_in, inc="ldc")
    l2 = p.dma("sync", A_sb[:], alog_in.rearrange("p a n -> p (a n)"), wait=[l1], inc="ldc")
    l3 = p.dma("sync", dskip[:], dskip_in, wait=[l2], inc="ldc")
    sA = p.op("scalar", lambda e: e.activation(A_sb[:], A_sb[:], AF.Exp), wait=[l2])
    sA = p.op("vector", lambda e: e.tensor_scalar(A_sb[:], A_sb[:], -1.0, None, ALU.mult), wait=[sA])

    sum_tok = None
    prev = {0: None, 1: None}
    for tc in range(8):
        t0 = tc * 2048
        for j in range(NCORES):
            lb = p.dma("sync", xdp[j % 2][:], xdp_in[j, :, t0:t0 + 2048],
                       wait=[prev[j % 2]], inc="ldp%d" % (j % 2))
            if j == 0:
                sum_tok = p.op("vector", lambda e, t0=t0: e.tensor_copy(
                    xdbl[:, t0:t0 + 2048], xdp[0][:]), wait=[lb])
            else:
                sum_tok = p.op("vector", lambda e, t0=t0, j=j: e.tensor_tensor(
                    xdbl[:, t0:t0 + 2048], xdbl[:, t0:t0 + 2048], xdp[j % 2][:], ALU.add),
                    wait=[lb])
            prev[j % 2] = sum_tok

    toks = {}
    ev_last = {0: None, 1: None}
    last = {}
    for b in range(2):
        for et in range(2):
            cm = p.op("vector", lambda e: e.memset(carry[:], 0.0), wait=[last.get("scan")])
            for tb in range(T // TB):
                t0g = b * T + tb * TB
                pb = pst[tb % 2]
                tok = mm_group(p, pb[:, 0:TB],
                               [(wdtT[:, et * P:(et + 1) * P], xdbl[0:64, t0g:t0g + TB],
                                 [lw, sum_tok, last.get("dtp%d" % (tb % 2))])],
                               psum_free=ev_last[tb % 2])
                cdtp = p.op("vector", lambda e, pb=pb, et=et: e.tensor_scalar(
                    t_abs[:], pb[:, 0:TB], bdt[:, et:et + 1], None, ALU.add),
                    wait=[tok, m1, m2, l1, last.get("tabs")])
                ev_last[tb % 2] = cdtp
                last["dtp%d" % (tb % 2)] = cdtp
                s4 = p.op("scalar", lambda e: e.activation(t_r[:], t_abs[:], AF.Relu), wait=[cdtp])
                s1 = p.op("scalar", lambda e: e.activation(t_abs[:], t_abs[:], AF.Abs))
                s2 = p.op("scalar", lambda e: e.activation(t_e[:], t_abs[:], AF.Exp, scale=neg1[:, 0:1]))
                s3 = p.op("scalar", lambda e: e.activation(t_l[:], t_e[:], AF.Ln, bias=one1[:, 0:1]))
                last["tabs"] = s3
                cdt = p.op("vector", lambda e: e.tensor_tensor(dt_blk[:], t_l[:], t_r[:], ALU.add),
                           wait=[s3, s4])
                cpre = p.op("vector", lambda e, et=et: e.tensor_tensor(
                    bufA[:].rearrange("p (n t) -> p n t", n=16),
                    dt_blk[:].unsqueeze(1).broadcast_to([P, 16, TB]),
                    A_sb[:, et * 16:(et + 1) * 16].unsqueeze(2).broadcast_to([P, 16, TB]),
                    ALU.mult), wait=[sA, last.get("dA")])
                cdA = p.op("scalar", lambda e: e.activation(bufA[:], bufA[:], AF.Exp), wait=[cpre])
                last["dA"] = cdA
                lu = p.dma("sync", ub[:], u_in[et * P:(et + 1) * P, t0g:t0g + TB],
                           wait=[last.get("ub")], inc="ldu")
                lz = p.dma("sync", zb[:], z_in[et * P:(et + 1) * P, t0g:t0g + TB],
                           wait=[last.get("zb")], inc="ldz")
                cdu = p.op("vector", lambda e: e.tensor_tensor(du[:], dt_blk[:], ub[:], ALU.mult),
                           wait=[lu])
                sb1 = p.dma("sync", bc_dram[:, 0:TB], xdbl[64:96, t0g:t0g + TB],
                            wait=[sum_tok, last.get("bco"), last.get("bcB")], inc="stbc")
                lbB = p.dma("sync", bcast[:], bc_dram[0:16, 0:TB].rearrange("a b -> (a b)").unsqueeze(0).broadcast_to([P, 16 * TB]),
                            wait=[sb1, last.get("hC")], inc="ldbB")
                cdBu = p.op("vector", lambda e: e.tensor_tensor(
                    bufB[:].rearrange("p (n t) -> p n t", n=16),
                    du[:].unsqueeze(1).broadcast_to([P, 16, TB]),
                    bcast[:].rearrange("p (n t) -> p n t", n=16), ALU.mult),
                    wait=[lbB, cdu, last.get("hC")])
                last["bcB"] = cdBu
                stoks = []
                for n in range(16):
                    stoks.append(p.op("vector", lambda e, n=n: e.tensor_tensor_scan(
                        bufC[:, n * TB:(n + 1) * TB], bufA[:, n * TB:(n + 1) * TB],
                        bufB[:, n * TB:(n + 1) * TB], carry[:, n:n + 1], ALU.mult, ALU.add),
                        wait=[cdA, cm]))
                ccar = p.op("vector", lambda e: e.tensor_copy(
                    carry[:], bufC[:].rearrange("p (n t) -> p n t", n=16)[:, :, TB - 1]))
                last["scan"] = ccar
                lbC = p.dma("sync", bcast[:], bc_dram[16:32, 0:TB].rearrange("a b -> (a b)").unsqueeze(0).broadcast_to([P, 16 * TB]),
                            wait=[sb1, cdBu], inc="ldbC")
                last["bco"] = lbC
                chC = p.op("vector", lambda e: e.tensor_tensor(
                    bufB[:], bufC[:], bcast[:], ALU.mult), wait=[lbC] + stoks)
                last["hC"] = chC
                cy = p.op("vector", lambda e: e.tensor_reduce(
                    yb[:], bufB[:].rearrange("p (n t) -> p t n", n=16), AX.X, ALU.add),
                    wait=[last.get("sy_last")])
                cy2 = p.op("vector", lambda e, et=et: e.scalar_tensor_tensor(
                    yb[:], ub[:], dskip[:, et:et + 1], yb[:], ALU.mult, ALU.add), wait=[l3])
                last["ub"] = cy2
                cy3 = p.op("vector", lambda e: e.tensor_tensor(yb[:], yb[:], zb[:], ALU.mult),
                           wait=[lz])
                last["zb"] = cy3
                sy = p.dma("sync", y_out[et * P:(et + 1) * P, t0g:t0g + TB], yb[:],
                           wait=[cy3, last.get("sy%d" % (tb % 2))], inc="sty%d" % (tb % 2))
                last["sy%d" % (tb % 2)] = sy
                last["sy_last"] = sy
    p.barrier("sync", [last["sy0"], last["sy1"]])
    return p.finalize()


def build_l1ffn_iter(first):
    p = Prog("l1ffn_iter")
    xcT_in = p.din("xcT_prev", [D, TS])
    if not first:
        xnT_in = p.din("xnewT_prev", [D, TS])
        done_in = p.din("done_rep", [P, 1])
    wmT_in = p.din("wmT_full", [D, 12288])
    wpT_in = p.din("wpT_full", [12288, D])
    mask_in = p.din("mask_packed", [P, 96])
    xnew_out = p.dout("xnewT_sl", [D, TS])
    if not first:
        xcu_out = p.dout("xcT_used", [D, TS])
    pooled_out = p.dout("pooled_part", [P, 8])

    KT, MT = 8, 96
    M_ACT = list(range(32)) if first else list(range(MT))
    xcTb = p.sb("xcTb", [P, KT * TS], BF16)
    Hb = p.sb("Hb", [P, MT * 512], BF16)
    wmb = [p.sb(f"wmb{i}", [P, KT * P], BF16) for i in range(2)]
    wpb = [p.sb(f"wpb{i}", [P, D], BF16) for i in range(2)]
    mask = p.sb("mask", [P, 96])
    xn_ev = p.sb("xn_ev", [P, 512])
    pl_acc = p.sb("pl_acc", [P, 8])
    pl_tmp = p.sb("pl_tmp", [P, 1])
    xcf = p.sb("xcf", [P, TS])
    xnf = p.sb("xnf", [P, TS])
    done_sb = p.sb("done_sb", [P, 1])
    omd_sb = p.sb("omd_sb", [P, 1])
    psx = [p.ps(f"psx{i}", [P, 512]) for i in range(8)]
    psh = [psx[6], psx[7]]

    lm = p.dma("sync", mask[:], mask_in, inc="ldm")
    mz = p.op("vector", lambda e: e.memset(pl_acc[:], 0.0))

    if first:
        xg = None
        for q in range(2):
            xg = p.dma("gpsimd", xcTb[:].rearrange("p (k t) -> p k t", k=KT)[:, 4 * q:4 * q + 4],
                       xcT_in.rearrange("(k p) t -> p k t", p=P)[:, 4 * q:4 * q + 4],
                       wait=[xg], inc="ldx")
        blend_tok = xg
    else:
        ld = p.dma("sync", done_sb[:], done_in, inc="ldd")
        cdn = p.op("vector", lambda e: e.tensor_scalar(omd_sb[:], done_sb[:], -1.0, 1.0,
                                                       ALU.mult, ALU.add), wait=[ld])
        prev_st = {}
        xcf_free = []
        for k in range(KT):
            l1 = p.dma("sync", xcf[:], xcT_in[k * P:(k + 1) * P, :],
                       wait=xcf_free, inc="ldc1")
            l2 = p.dma("gpsimd", xnf[:], xnT_in[k * P:(k + 1) * P, :],
                       wait=[prev_st.get("n"), l1], inc="ldc2")
            c1 = p.op("vector", lambda e: e.tensor_scalar(xcf[:], xcf[:], done_sb[:, 0:1], None, ALU.mult),
                      wait=[l1, cdn])
            c2 = p.op("vector", lambda e: e.scalar_tensor_tensor(
                xcf[:], xnf[:], omd_sb[:, 0:1], xcf[:], ALU.mult, ALU.add), wait=[l2])
            s1 = p.dma("sync", xcu_out[k * P:(k + 1) * P, :], xcf[:], wait=[c2], inc="stc%d" % (k % 2))
            cb = p.op("vector", lambda e, k=k: e.tensor_copy(xcTb[:, k * TS:(k + 1) * TS], xcf[:]))
            xcf_free = [cb, s1]
            prev_st["n"] = c2
            blend_tok = cb
            prev_st[k % 2] = s1
        p.barrier("sync", [prev_st[0], prev_st[1]])

    wm_v = wmT_in.rearrange("(k p) f -> p k f", p=P)
    ev_h = {0: None, 1: None}
    ev_x = {i: None for i in range(8)}
    prev_w = {0: None, 1: None}
    prev_wp = {0: None, 1: None}
    last_px = None
    last_xst = {}
    for g in range(4):
        t0 = g * 512
        hev = None
        for m in M_ACT:
            wb = wmb[m % 2]
            lw = p.dma("gpsimd", wb[:].rearrange("p (k c) -> p k c", k=KT)[:, 0:4],
                       wm_v[:, 0:4, m * P:(m + 1) * P], wait=[prev_w[m % 2]], inc="ldw%d" % (m % 2))
            lw2 = p.dma("gpsimd", wb[:].rearrange("p (k c) -> p k c", k=KT)[:, 4:8],
                        wm_v[:, 4:8, m * P:(m + 1) * P], wait=[prev_w[m % 2]], inc="lds%d" % (m % 2))
            pb = psh[m % 2]
            mms = [(wb[:, k * P:(k + 1) * P], xcTb[:, k * TS + t0:k * TS + t0 + 512],
                    [lw, lw2, blend_tok]) for k in range(KT)]
            tok = mm_group(p, pb[:], mms, psum_free=ev_h[m % 2])
            prev_w[m % 2] = tok
            hev = p.op("scalar", lambda e, pb=pb, m=m: e.activation(
                Hb[:, m * 512:(m + 1) * 512], pb[:], AF.Relu, scale=mask[:, m:m + 1]),
                wait=[tok, lm])
            ev_h[m % 2] = hev
        for m in M_ACT:
            wb = wpb[m % 2]
            lwp = p.dma("gpsimd", wb[:, 0:512], wpT_in[m * P:(m + 1) * P, 0:512],
                        wait=[prev_wp[m % 2]], inc="ldp%d" % (m % 2))
            lwp2 = p.dma("gpsimd", wb[:, 512:1024], wpT_in[m * P:(m + 1) * P, 512:1024],
                         wait=[prev_wp[m % 2]], inc="ldq%d" % (m % 2))
            for dt in range(8):
                tok = p.op("tensor",
                           (lambda e, m=m, dt=dt, wb=wb, s=(m == M_ACT[0]), t=(m == M_ACT[-1]):
                            e.matmul(psx[dt][:], wb[:, dt * P:(dt + 1) * P],
                                     Hb[:, m * 512:(m + 1) * 512], start=s, stop=t)),
                           wait=([lwp, lwp2, hev] if dt == 0 else [lwp, lwp2]) +
                                ([ev_x[dt]] if m == M_ACT[0] else []),
                           inc="pe" if (dt == 7 or m == M_ACT[-1]) else None, chain=False)
            prev_wp[m % 2] = tok
        xg_toks = []
        for dt in range(8):
            ev = p.op("vector", lambda e, dt=dt: e.tensor_copy(xn_ev[:], psx[dt][:]),
                      wait=[tok, last_px])
            ev_x[dt] = ev
            st = p.dma("sync", xnew_out[dt * P:(dt + 1) * P, t0:t0 + 512], xn_ev[:],
                       wait=[ev], inc="stx%d" % (dt % 2))
            last_px = st
            last_xst[dt % 2] = st
            c1 = p.op("vector", lambda e: e.tensor_reduce(pl_tmp[:], xn_ev[:], AX.X, ALU.add),
                      wait=[mz])
            xg_toks.append(p.op("vector", lambda e, dt=dt: e.tensor_tensor(
                pl_acc[:, dt:dt + 1], pl_acc[:, dt:dt + 1], pl_tmp[:], ALU.add)))
        ev_h[0], ev_h[1] = ev_x[6], ev_x[7]
    pst_tok = p.dma("sync", pooled_out, pl_acc[:], wait=[xg_toks[-1]], inc="stp")
    p.barrier("sync", [pst_tok, last_xst[0], last_xst[1]])
    return p.finalize()


def build_l1ffn_ctl():
    p = Prog("l1ffn_ctl")
    parts_in = p.din("pooled_parts", [NCORES, P, 8])
    wc1T_in = p.din("wc1T_tiled", [P, 8, 256])
    wc2T_in = p.din("wc2T_tiled", [P, 2, 12288])
    hist_in = p.din("history", [P, 192])
    act_in = p.din("active", [P, 192])
    done_in = p.din("done_rep", [P, 1])
    act_out = p.dout("active_next", [P, 192])
    hist_out = p.dout("history_next", [P, 192])
    done_out = p.dout("done_next", [P, 1])

    prt = p.sb("prt", [P, 8])
    pl = p.sb("pl", [P, 16])
    wc1 = p.sb("wc1", [P, 8 * 256], BF16)
    wc2 = p.sb("wc2", [P, 2 * 12288], BF16)
    r_sb = p.sb("r_sb", [P, 4], BF16)
    ck = p.sb("ck", [P, 192])
    hist = p.sb("hist", [P, 192])
    act = p.sb("act", [P, 192])
    done = p.sb("done", [P, 1])
    omd = p.sb("omd", [P, 1])
    lo = p.sb("lo", [P, 2])
    hi = p.sb("hi", [P, 2])
    mid = p.sb("mid", [P, 2])
    ge = p.sb("ge", [P, 192])
    rs = p.sb("rs", [P, 2])
    pred = p.sb("pred", [P, 2])
    omp = p.sb("omp", [P, 2])
    tq = p.sb("tq", [P, 2])
    nov = p.sb("nov", [P, 1])
    ps1 = p.ps("ps1", [P, 8])
    ps2 = p.ps("ps2", [P, 8])

    lws = p.dma("gpsimd", wc1[:].rearrange("p (k c) -> p k c", k=8), wc1T_in, inc="lw1")
    lw2 = p.dma("gpsimd", wc2[:].rearrange("p (k c) -> p k c", k=2), wc2T_in, inc="lw2")
    lh = p.dma("sync", hist[:], hist_in, inc="ldh")
    la = p.dma("sync", act[:], act_in, wait=[lh], inc="ldh")
    ldn = p.dma("sync", done[:], done_in, wait=[la], inc="ldh")

    toks = []
    prev = None
    for j in range(NCORES):
        b = j // 4
        lp = p.dma("sync", prt[:], parts_in[j], wait=[prev], inc="ldp")
        if j % 4 == 0:
            tok = p.op("vector", lambda e, b=b: e.tensor_copy(
                pl[:].rearrange("p (dt b2) -> p dt b2", b2=2)[:, :, b], prt[:]), wait=[lp])
        else:
            tok = p.op("vector", lambda e, b=b: e.tensor_tensor(
                pl[:].rearrange("p (dt b2) -> p dt b2", b2=2)[:, :, b],
                pl[:].rearrange("p (dt b2) -> p dt b2", b2=2)[:, :, b], prt[:], ALU.add),
                wait=[lp])
        prev = tok
    cpl = p.op("vector", lambda e: e.tensor_scalar(pl[:], pl[:], 1.0 / T, None, ALU.mult))
    plb = p.sb("plb", [P, 16], BF16)
    cpl = p.op("vector", lambda e: e.tensor_copy(plb[:], pl[:]), wait=[cpl])
    rtoks = []
    for mt in range(2):
        mms = [(wc1[:, (k * 256) + mt * P:(k * 256) + (mt + 1) * P], plb[:, k * 2:(k + 1) * 2], [lws, cpl])
               for k in range(8)]
        tok = mm_group(p, ps1[:, 0:2], mms, psum_free=rtoks[-1] if rtoks else None)
        rtoks.append(p.op("scalar", lambda e, mt=mt: e.activation(
            r_sb[:, mt * 2:(mt + 1) * 2], ps1[:, 0:2], AF.Relu), wait=[tok]))
    cktoks = []
    for m in range(96):
        mms = [(wc2[:, k * 12288 + m * P:k * 12288 + (m + 1) * P], r_sb[:, k * 2:(k + 1) * 2],
                [lw2, rtoks[-1]]) for k in range(2)]
        tok = mm_group(p, ps2[:, 0:2], mms, psum_free=cktoks[-1] if cktoks else None)
        cktoks.append(p.op("vector", lambda e, m=m: e.tensor_copy(
            ck[:, m * 2:(m + 1) * 2], ps2[:, 0:2]), wait=[tok]))
    ck_done = cktoks[-1]

    ckv = ck[:].rearrange("p (m b2) -> p b2 m", b2=2)
    ones = p.sb("ones", [P, P])
    lohi = p.sb("lohi", [P, 4])
    lohi1 = p.sb("lohi1", [1, 4])
    lh_dram = p.dtmp("lh_dram", [P, 4])
    psr = p.ps("psr", [P, 4])
    mon = p.op("vector", lambda e: e.memset(ones[:], 1.0))

    def part_sum(src_ap, dst_ap, w):
        n = src_ap.shape[-1]
        t1 = p.op("tensor", lambda e: e.matmul(psr[0:1, 0:n], ones[:, 0:1], src_ap, start=True, stop=True),
                  wait=list(w) + [mon, part_sum.free], inc="pe", chain=False)
        t2 = p.op("vector", lambda e: e.tensor_copy(lohi1[0:1, 0:n], psr[0:1, 0:n]), wait=[t1])
        t3 = p.op("tensor", lambda e: e.matmul(psr[:, 0:n], ones[0:1, :], lohi1[0:1, 0:n], start=True, stop=True),
                  wait=[t2], inc="pe", chain=False)
        t4 = p.op("vector", lambda e: e.tensor_copy(dst_ap, psr[:, 0:n]), wait=[t3])
        part_sum.free = t4
        return t4
    part_sum.free = None

    flat512 = p.sb("flat512", [1, 512])
    lh2_dram = p.dtmp("lh2_dram", [1, 4])
    c1 = p.op("vector", lambda e: e.tensor_reduce(lohi[:, 0:2], ckv, AX.X, ALU.min), wait=[ck_done])
    c2 = p.op("vector", lambda e: e.tensor_reduce(lohi[:, 2:4], ckv, AX.X, ALU.max))
    d1 = p.dma("sync", lh_dram, lohi[:], wait=[c2], inc="lhd")
    d2 = p.dma("sync", flat512[0:1, :], lh_dram.rearrange("p c -> (p c)").unsqueeze(0),
               wait=[d1], inc="lhd")
    fv = flat512[0:1, :].rearrange("q (pp c) -> q c pp", c=4)
    c3 = p.op("vector", lambda e: e.tensor_reduce(lohi1[0:1, 0:2], fv[:, 0:2, :], AX.X, ALU.min),
              wait=[d2])
    c4 = p.op("vector", lambda e: e.tensor_reduce(lohi1[0:1, 2:4], fv[:, 2:4, :], AX.X, ALU.max))
    d3 = p.dma("sync", lh2_dram, lohi1[0:1, :], wait=[c4], inc="lhd")
    d4 = p.dma("sync", lohi[:], lh2_dram.broadcast_to([P, 4]), wait=[d3], inc="lhd")
    c5 = p.op("vector", lambda e: e.tensor_copy(lo[:], lohi[:, 0:2]), wait=[d4])
    c6 = p.op("vector", lambda e: e.tensor_copy(hi[:], lohi[:, 2:4]))
    ctok = c6
    for it in range(42):
        cm = p.op("vector", lambda e: e.tensor_tensor(mid[:], lo[:], hi[:], ALU.add), wait=[ctok])
        cm = p.op("vector", lambda e: e.tensor_scalar(mid[:], mid[:], 0.5, None, ALU.mult))
        cg = p.op("vector", lambda e: e.tensor_tensor(
            ge[:].rearrange("p (m b2) -> p b2 m", b2=2), ckv,
            mid[:].unsqueeze(2).broadcast_to([P, 2, 96]), ALU.is_ge))
        cr = p.op("vector", lambda e: e.tensor_reduce(rs[:], ge[:].rearrange("p (m b2) -> p b2 m", b2=2),
                                                      AX.X, ALU.add))
        cgp = part_sum(rs[:, 0:2], rs[:, 0:2], [cr])
        cp = p.op("vector", lambda e: e.tensor_scalar(pred[:], rs[:], 4096.0, None, ALU.is_ge), wait=[cgp])
        co = p.op("vector", lambda e: e.tensor_scalar(omp[:], pred[:], -1.0, 1.0, ALU.mult, ALU.add))
        u1 = p.op("vector", lambda e: e.tensor_tensor(tq[:], mid[:], lo[:], ALU.subtract))
        u2 = p.op("vector", lambda e: e.tensor_tensor(tq[:], tq[:], pred[:], ALU.mult))
        u3 = p.op("vector", lambda e: e.tensor_tensor(lo[:], lo[:], tq[:], ALU.add))
        u4 = p.op("vector", lambda e: e.tensor_tensor(tq[:], mid[:], hi[:], ALU.subtract))
        u5 = p.op("vector", lambda e: e.tensor_tensor(tq[:], tq[:], omp[:], ALU.mult))
        ctok = p.op("vector", lambda e: e.tensor_tensor(hi[:], hi[:], tq[:], ALU.add))
    cmask = p.op("vector", lambda e: e.tensor_tensor(
        ge[:].rearrange("p (m b2) -> p b2 m", b2=2), ckv,
        lo[:].unsqueeze(2).broadcast_to([P, 2, 96]), ALU.is_ge), wait=[ctok])
    comb = ck
    cc1 = p.op("vector", lambda e: e.tensor_tensor(comb[:], hist[:], ge[:], ALU.add), wait=[cmask, la])
    cc2 = p.op("vector", lambda e: e.tensor_single_scalar(comb[:], comb[:], 1.0, ALU.min))
    dif = p.sb("dif", [P, 192])
    cd1 = p.op("vector", lambda e: e.tensor_tensor(dif[:], comb[:], hist[:], ALU.subtract))
    cd2 = p.op("vector", lambda e: e.tensor_reduce(rs[:], dif[:].rearrange("p (m b2) -> p b2 m", b2=2),
                                                   AX.X, ALU.add))
    cd3 = part_sum(rs[:, 0:2], rs[:, 0:2], [cd2])
    cd4 = p.op("vector", lambda e: e.tensor_reduce(nov[:], rs[:], AX.X, ALU.add), wait=[cd3])
    cd5 = p.op("vector", lambda e: e.tensor_scalar(nov[:], nov[:], 1.0 / 8192.0, None, ALU.mult))
    cd6 = p.op("vector", lambda e: e.tensor_scalar(nov[:], nov[:], 0.7, None, ALU.is_lt))
    cdn = p.op("vector", lambda e: e.tensor_tensor(nov[:], done[:], nov[:], ALU.max), wait=[ldn])
    sd = p.dma("sync", done_out, nov[:], wait=[cdn], inc="std")
    co = p.op("vector", lambda e: e.tensor_scalar(omd[:], done[:], -1.0, 1.0, ALU.mult, ALU.add))
    b1 = p.op("vector", lambda e: e.tensor_scalar(act[:], act[:], done[:, 0:1], None, ALU.mult))
    b2 = p.op("vector", lambda e: e.scalar_tensor_tensor(act[:], ge[:], omd[:, 0:1], act[:],
                                                         ALU.mult, ALU.add), wait=[cmask])
    sa = p.dma("sync", act_out, act[:], wait=[b2], inc="sta")
    b3 = p.op("vector", lambda e: e.tensor_scalar(hist[:], hist[:], done[:, 0:1], None, ALU.mult))
    b4 = p.op("vector", lambda e: e.scalar_tensor_tensor(hist[:], comb[:], omd[:, 0:1], hist[:],
                                                         ALU.mult, ALU.add))
    sh2 = p.dma("sync", hist_out, hist[:], wait=[b4], inc="sth")
    p.barrier("sync", [sd, sa, sh2])
    return p.finalize()


def build_final():
    p = Prog("final")
    y1T_in = p.din("y1T_sl", [2048, TS])
    woT_in = p.din("wo1T_tiled", [P, 16, D])
    xn1_in = p.din("xn1_sl", [TS, D])
    x1_in = p.din("x1_sl", [TS, D])
    xc3_in = p.din("xc3_sl", [TS, D])
    xn3_in = p.din("xnew3_sl", [TS, D])
    done_in = p.din("done_rep", [P, 1])
    x2_out = p.dout("x2_sl", [TS, D])

    KT = 16
    y1Tb = p.sb("y1Tb", [P, KT * P], BF16)
    woTb = p.sb("woTb", [P, KT * D], BF16)
    xn1t = p.sb("xn1t", [P, D])
    x1t = p.sb("x1t", [P, D])
    xc3t = p.sb("xc3t", [P, D])
    xn3t = p.sb("xn3t", [P, D])
    s1t = p.sb("s1t", [P, D])
    gt = p.sb("gt", [P, D])
    done_sb = p.sb("done_sb", [P, 1])
    omd_sb = p.sb("omd_sb", [P, 1])
    pst = [p.ps(f"pst{i}", [P, 512]) for i in range(2)]

    w1 = None
    for q in range(2):
        w1 = p.dma("gpsimd", woTb[:].rearrange("p (k c) -> p k c", k=KT)[:, 8 * q:8 * q + 8],
                   woT_in[:, 8 * q:8 * q + 8], wait=[w1], inc="ldw")
    ld = p.dma("sync", done_sb[:], done_in, inc="ldd")
    cdn = p.op("vector", lambda e: e.tensor_scalar(omd_sb[:], done_sb[:], -1.0, 1.0,
                                                   ALU.mult, ALU.add), wait=[ld])
    y1T_v = y1T_in.rearrange("(k p) t -> p k t", p=P)
    ev_last = {0: None, 1: None}
    prev_mm = None
    prev_st = None
    for i in range(TS // P):
        t0 = i * P
        ly = p.dma("gpsimd", y1Tb[:].rearrange("p (k c) -> p k c", k=KT),
                   y1T_v[:, :, t0:t0 + P], wait=[prev_mm], inc="ldy")
        l1 = p.dma("sync", xn1t[:], xn1_in[t0:t0 + P, :], wait=[prev_st], inc="ldt")
        l2 = p.dma("sync", x1t[:], x1_in[t0:t0 + P, :], wait=[l1], inc="ldt")
        l3 = p.dma("sync", xc3t[:], xc3_in[t0:t0 + P, :], wait=[l2], inc="ldt")
        l4 = p.dma("sync", xn3t[:], xn3_in[t0:t0 + P, :], wait=[l3], inc="ldt")
        stoks = []
        for nch in range(2):
            pb = pst[nch]
            mms = [(y1Tb[:, k * P:(k + 1) * P], woTb[:, k * D + nch * 512:k * D + (nch + 1) * 512],
                    [ly, w1]) for k in range(KT)]
            tok = mm_group(p, pb[:], mms, psum_free=ev_last[nch])
            prev_mm = tok
            ev = p.op("vector", lambda e, pb=pb, nch=nch: e.tensor_tensor(
                s1t[:, nch * 512:(nch + 1) * 512], pb[:], xn1t[:, nch * 512:(nch + 1) * 512], ALU.add),
                wait=[tok, l4])
            ev_last[nch] = ev
            stoks.append(ev)
        p.op("vector", lambda e: e.tensor_scalar(xc3t[:], xc3t[:], done_sb[:, 0:1], None, ALU.mult),
             wait=[cdn, l4])
        cxf = p.op("vector", lambda e: e.scalar_tensor_tensor(
            xc3t[:], xn3t[:], omd_sb[:, 0:1], xc3t[:], ALU.mult, ALU.add))
        cg = p.op("scalar", lambda e: e.activation(gt[:], xc3t[:], AF.Silu), wait=[cxf, prev_st])
        p.op("vector", lambda e: e.tensor_tensor(gt[:], gt[:], s1t[:], ALU.mult), wait=[cg] + stoks)
        cx2 = p.op("vector", lambda e: e.tensor_tensor(gt[:], gt[:], x1t[:], ALU.add))
        prev_st = p.dma("sync", x2_out[t0:t0 + P, :], gt[:], wait=[cx2], inc="st")
    p.barrier("sync", [prev_st])
    return p.finalize()


_BUILD_CACHE = {}


def _get(name, builder, *a):
    key = (name,) + a
    if key not in _BUILD_CACHE:
        _BUILD_CACHE[key] = builder(*a)
    return _BUILD_CACHE[key]


def _tile3(w, kt):
    return np.ascontiguousarray(w.reshape(kt, P, -1).transpose(1, 0, 2))


def _pack32(v):
    return np.ascontiguousarray(np.asarray(v, F).reshape(-1, P).T)


def _packF2(v2):
    return np.ascontiguousarray(v2.reshape(2, 96, P).transpose(2, 1, 0).reshape(P, 192))


def kernel(**inputs):
    inp = {k: np.ascontiguousarray(np.asarray(v, dtype=F)) for k, v in inputs.items()}
    x = inp["L0_attn_x"].reshape(NTOK, D)
    R = list(range(NCORES))

    def run(nc, im):
        return run_bass_kernel_spmd(nc, im, core_ids=R).results

    def sl(c):
        return slice(c * TS, (c + 1) * TS)

    r = run(_get("prep", build_prep),
            [{"x_sl": x[sl(c)], "norm_w": inp["L0_attn_norm_w"][None, :]} for c in R])
    xn = np.concatenate([q["xn_sl"] for q in r], 0)
    h = np.concatenate([q["h_sl"] for q in r], 0)
    hT = np.ascontiguousarray(h.T)

    fold_in = stage_fold_inputs(inp["L0_ffn_up_weights"], inp["L0_ffn_down_weights"],
                                inp["L0_ffn_W_exp"], inp["L0_ffn_b_exp"], inp["L0_ffn_W_con"])
    r = run(_get("fold", build_fold), [fold_in] * NCORES)
    wexpfT, wconf = r[0]["wexpfT"], r[0]["wconf"]
    Wf = np.ascontiguousarray(wexpfT[:, :D].T)
    bf = np.ascontiguousarray(wexpfT[:, D])
    l0ffn_shared = {
        "wexpf_tiled": _tile3(Wf, 8), "wconf_tiled": _tile3(wconf, 32),
        "gbias1_packed": _pack32(bf), "gbias2_packed": _pack32(inp["L0_ffn_up_bias"]),
        "cbias2": np.ascontiguousarray(np.stack([inp["L0_ffn_b_con"], inp["L0_ffn_down_bias"]])),
    }

    r = run(_get("l0ffn", build_l0ffn),
            [dict(l0ffn_shared, xnT_sl=np.ascontiguousarray(xn[sl(c)].T)) for c in R])
    g0 = np.concatenate([q["g0_sl"] for q in r], 0)

    im = []
    for c in R:
        rows = slice(c * 8 * 16 * 17, (c + 1) * 8 * 16 * 17)
        waT = np.ascontiguousarray(inp["L0_attn_W_a"][rows].T)
        wvT = np.ascontiguousarray(inp["L0_attn_W_v"][c * P:(c + 1) * P].T)
        im.append({"hT_full": hT, "waT_tiled": _tile3(waT, 8), "wvT_tiled": _tile3(wvT, 8)})
    r = run(_get("l0attn", build_l0attn), im)
    y0 = np.stack([q["y0_sl"] for q in r])
    yfull = np.ascontiguousarray(
        y0.reshape(NCORES, 2, 8, 8, 1024, 16).transpose(0, 2, 5, 1, 3, 4).reshape(D, NTOK))

    woT_t = _tile3(np.ascontiguousarray(inp["L0_attn_W_out"].T), 8)
    im = [{"yT_sl": np.ascontiguousarray(yfull[:, sl(c)]), "woT_tiled": woT_t,
           "xn_sl": xn[sl(c)], "x_sl": x[sl(c)], "g0_sl": g0[sl(c)],
           "ln_w": inp["L1_attn_ln_w"][None, :], "ln_b": inp["L1_attn_ln_b"][None, :]}
          for c in R]
    r = run(_get("l0out", build_l0out), im)
    x1 = np.concatenate([q["x1_sl"] for q in r], 0)
    xn1 = np.concatenate([q["xn1_sl"] for q in r], 0)
    xn1n = np.concatenate([q["xn1n_sl"] for q in r], 0)
    xn1nT = np.ascontiguousarray(xn1n.T)

    im = []
    for c in R:
        rows = np.r_[256 * c:256 * c + 256, 2048 + 256 * c:2048 + 256 * c + 256]
        winT = np.ascontiguousarray(inp["L1_attn_w_in"][rows].T)
        wxT = np.ascontiguousarray(inp["L1_attn_w_xproj"][:, 256 * c:256 * c + 256].T)
        im.append({
            "xn1nT_full": xn1nT, "winT_tiled": _tile3(winT, 8),
            "convw_tiled": np.ascontiguousarray(
                inp["L1_attn_conv_w"][256 * c:256 * c + 256].reshape(2, P, 4).transpose(1, 0, 2)),
            "convb_tiled": np.ascontiguousarray(
                inp["L1_attn_conv_b"][256 * c:256 * c + 256].reshape(2, P).T),
            "wxT_tiled": np.ascontiguousarray(wxT.reshape(2, P, 96).transpose(1, 0, 2)),
        })
    r = run(_get("mamba1", build_mamba1), im)
    u_full = np.concatenate([q["u_sl"] for q in r], 0)
    z_full = np.concatenate([q["zsil_sl"] for q in r], 0)
    xdbl_parts = np.stack([q["xdbl_part"] for q in r])

    im = []
    for c in R:
        s2 = slice(256 * c, 256 * c + 256)
        im.append({
            "xdbl_parts": xdbl_parts,
            "wdtT_sl": np.ascontiguousarray(inp["L1_attn_w_dt"][s2].T),
            "bdt_tiled": np.ascontiguousarray(inp["L1_attn_b_dt"][s2].reshape(2, P).T),
            "alog_tiled": np.ascontiguousarray(
                inp["L1_attn_a_log"][s2].reshape(2, P, 16).transpose(1, 0, 2)),
            "dskip_tiled": np.ascontiguousarray(inp["L1_attn_d_skip"][s2].reshape(2, P).T),
            "u_sl": np.ascontiguousarray(u_full[s2]), "zsil_sl": np.ascontiguousarray(z_full[s2]),
        })
    r = run(_get("mamba2", build_mamba2), im)
    y1full = np.concatenate([q["y1_sl"] for q in r], 0)

    wmT = np.ascontiguousarray(inp["L1_ffn_Wm"].T)
    wpT = np.ascontiguousarray(inp["L1_ffn_Wp"].T)
    wc1T_t = np.ascontiguousarray(inp["L1_ffn_Wc1"].T.reshape(8, P, 256).transpose(1, 0, 2))
    wc2T_t = np.ascontiguousarray(inp["L1_ffn_Wc2"].T.reshape(2, P, 12288).transpose(1, 0, 2))
    base2 = np.zeros((2, 12288), F)
    base2[:, :4096] = 1.0
    active = _packF2(base2)
    history = _packF2(base2)
    zeros_rep = np.zeros((P, 1), F)
    done_for_blend = [zeros_rep, zeros_rep, None]
    done_ctl_in = zeros_rep
    xcT = {c: np.ascontiguousarray(xn1[sl(c)].T) for c in R}
    xnT = {}
    xcuT = {}
    for it in range(3):
        im = []
        for c in R:
            b = c // 4
            d = {"xcT_prev": xcT[c], "wmT_full": wmT, "wpT_full": wpT,
                 "mask_packed": np.ascontiguousarray(active[:, b::2])}
            if it > 0:
                d["xnewT_prev"] = xnT[c]
                d["done_rep"] = done_for_blend[it]
            im.append(d)
        r = run(_get("l1ffn_iter", build_l1ffn_iter, it == 0), im)
        for c in R:
            xnT[c] = r[c]["xnewT_sl"]
            if it > 0:
                xcuT[c] = r[c]["xcT_used"]
                xcT[c] = xcuT[c]
        if it == 2:
            break
        pooled_parts = np.stack([r[j]["pooled_part"] for j in R])
        cin = {"pooled_parts": pooled_parts, "wc1T_tiled": wc1T_t, "wc2T_tiled": wc2T_t,
               "history": history, "active": active, "done_rep": done_ctl_in}
        rc = run(_get("l1ffn_ctl", build_l1ffn_ctl), [cin] * NCORES)
        active = rc[0]["active_next"]
        history = rc[0]["history_next"]
        done_next = np.ascontiguousarray(rc[0]["done_next"])
        done_for_blend[2] = done_next if it == 0 else done_for_blend[2]
        if it == 1:
            done2 = done_next
        done_ctl_in = done_next

    wo1T_t = _tile3(np.ascontiguousarray(inp["L1_attn_w_out"].T), 16)
    im = [{"y1T_sl": np.ascontiguousarray(y1full[:, sl(c)]), "wo1T_tiled": wo1T_t,
           "xn1_sl": xn1[sl(c)], "x1_sl": x1[sl(c)],
           "xc3_sl": np.ascontiguousarray(xcuT[c].T), "xnew3_sl": np.ascontiguousarray(xnT[c].T),
           "done_rep": done2} for c in R]
    r = run(_get("final", build_final), im)
    x2 = np.concatenate([q["x2_sl"] for q in r], 0)
    return np.ascontiguousarray(x2.reshape(B, T, D))


# version 10
# speedup vs baseline: 1.0946x; 1.0388x over previous
import numpy as np
from contextlib import ExitStack

import concourse.bass as bass
import concourse.mybir as mybir
from concourse.bass_utils import run_bass_kernel_spmd

F32 = mybir.dt.float32
BF16 = mybir.dt.bfloat16
AF = mybir.ActivationFunctionType
ALU = mybir.AluOpType
AX = mybir.AxisListType

NCORES = 8
B, T, D = 2, 8192, 1024
NTOK = B * T
TS = NTOK // NCORES
P = 128
EPS = 1e-5

F = np.float32


class Prog:
    ENGINES = ("sync", "vector", "scalar", "gpsimd", "tensor")

    def __init__(self, name):
        self.name = name
        self.nc = bass.Bass("TRN2", target_bir_lowering=False, num_swdge_queues=2)
        self.ops = {e: [] for e in self.ENGINES}
        self.sem_counts = {}
        self.sem_names = []
        self.ctx = ExitStack()
        self._finalized = False

    def din(self, name, shape, dtype=F32):
        return self.nc.dram_tensor(name, list(shape), dtype, kind="ExternalInput").ap()

    def dout(self, name, shape, dtype=F32):
        return self.nc.dram_tensor(name, list(shape), dtype, kind="ExternalOutput").ap()

    def dtmp(self, name, shape, dtype=F32):
        return self.nc.dram_tensor(name, list(shape), dtype, kind="Internal").ap()

    def sb(self, name, shape, dtype=F32):
        return self.ctx.enter_context(self.nc.sbuf_tensor(name, list(shape), dtype))

    def ps(self, name, shape, dtype=F32):
        return self.ctx.enter_context(self.nc.psum_tensor(name, list(shape), dtype))

    def sem(self, name):
        if name not in self.sem_counts:
            self.sem_counts[name] = 0
            self.sem_names.append(name)
        return name

    CHAINED = ("vector", "scalar", "gpsimd")

    def op(self, eng, fn, wait=(), inc=None, inc_n=1, chain=True):
        _w = {}
        for w in wait:
            if w is not None:
                _w[w[0]] = max(_w.get(w[0], 0), w[1])
        wait = sorted(_w.items())
        if chain and eng in self.CHAINED:
            assert inc is None, "chained engines use their chain sem as milestone"
            csem = self.sem("c_" + eng)
            wait.append((csem, self.sem_counts[csem]))
            self.sem_counts[csem] += 1
            tok = (csem, self.sem_counts[csem])
            self.ops[eng].append((fn, tuple(wait), (csem, 1)))
            return tok
        if inc is not None:
            self.sem(inc)
            self.sem_counts[inc] += inc_n
            tok = (inc, self.sem_counts[inc])
            self.ops[eng].append((fn, tuple(wait), (inc, inc_n)))
            return tok
        self.ops[eng].append((fn, tuple(wait), None))
        return None

    def barrier(self, eng, wait):
        self.op(eng, lambda e: None, wait=wait, chain=False)

    def dma(self, eng, dst, src, wait=(), inc="ld"):
        return self.op(eng, lambda e: e.dma_start(dst, src), wait=wait, inc=inc,
                       inc_n=16, chain=False)

    def count(self, sem):
        return self.sem_counts.get(sem, 0)

    def finalize(self):
        assert not self._finalized
        self._finalized = True
        nc = self.nc
        semh = {n: self.ctx.enter_context(nc.semaphore(n)) for n in self.sem_names}
        block = self.ctx.enter_context(nc.Block())

        def emit(engine_obj, eng_name):
            for fn, waits, inc in self.ops[eng_name]:
                for (sname, cnt) in waits:
                    if cnt > 0:
                        engine_obj.wait_ge(semh[sname], cnt)
                inst = fn(engine_obj)
                if inc is not None:
                    assert inst is not None, f"op with inc on {eng_name} returned None"
                    inst.then_inc(semh[inc[0]], inc[1])

        if self.ops["sync"]:
            @block.sync
            def _(sync):
                emit(sync, "sync")
        if self.ops["vector"]:
            @block.vector
            def _(vector):
                emit(vector, "vector")
        if self.ops["scalar"]:
            @block.scalar
            def _(scalar):
                emit(scalar, "scalar")
        if self.ops["gpsimd"]:
            @block.gpsimd
            def _(gpsimd):
                emit(gpsimd, "gpsimd")
        if self.ops["tensor"]:
            @block.tensor
            def _(tensor):
                emit(tensor, "tensor")
        self.ctx.close()
        return nc


def emit_ln(p, x_t, out_t, scr, nrows=P, ncols=D, ld_wait=(), gain_bias=None, eps=EPS):
    p.op("vector", lambda e: e.tensor_reduce(scr["sum1"][:nrows], x_t[:nrows], AX.X, ALU.add), wait=ld_wait)
    p.op("vector", lambda e: e.tensor_scalar(scr["nm"][:nrows], scr["sum1"][:nrows], -1.0 / ncols, None, ALU.mult))
    c = p.op("vector", lambda e: e.tensor_scalar(scr["xm"][:nrows], x_t[:nrows], scr["nm"][:nrows], None, ALU.add))
    cs = p.op("scalar", lambda e: e.activation(scr["sq"][:nrows], scr["xm"][:nrows], AF.Square, accum_out=scr["rr"][:nrows]),
              wait=[c])
    cv = p.op("vector", lambda e: e.tensor_scalar(scr["rr"][:nrows], scr["rr"][:nrows], 1.0 / ncols, eps, ALU.mult, ALU.add),
              wait=[cs])
    cs = p.op("scalar", lambda e: e.activation(scr["sd"][:nrows], scr["rr"][:nrows], AF.Sqrt), wait=[cv])
    p.op("vector", lambda e: e.reciprocal(scr["rstd"][:nrows], scr["sd"][:nrows]), wait=[cs])
    if gain_bias is None:
        return p.op("vector", lambda e: e.tensor_scalar(out_t[:nrows], scr["xm"][:nrows], scr["rstd"][:nrows], None, ALU.mult))
    gain, bias = gain_bias
    p.op("vector", lambda e: e.tensor_scalar(scr["xm"][:nrows], scr["xm"][:nrows], scr["rstd"][:nrows], None, ALU.mult))
    p.op("vector", lambda e: e.tensor_tensor(scr["xm"][:nrows], scr["xm"][:nrows], gain[:nrows], ALU.mult))
    return p.op("vector", lambda e: e.tensor_tensor(out_t[:nrows], scr["xm"][:nrows], bias[:nrows], ALU.add))


def build_prep():
    p = Prog("prep")
    x_in = p.din("x_sl", [TS, D])
    nw_in = p.din("norm_w", [1, D])
    xn_out = p.dout("xn_sl", [TS, D])
    h_out = p.dout("h_sl", [TS, D])

    ntiles = TS // P
    xt = [p.sb(f"xt{i}", [P, D]) for i in range(2)]
    xnt = p.sb("xnt", [P, D])
    ht = p.sb("ht", [P, D])
    nwb = p.sb("nwb", [P, D])
    scr = {k: p.sb("scr_" + k, [P, D if k in ("sq", "xm", "xmr", "xg") else 1])
           for k in ("sum1", "nm", "rr", "sd", "rstd", "sq", "xm", "xmr", "xg")}
    scr2 = {"rr2": p.sb("rr2", [P, 1]), "sd2": p.sb("sd2", [P, 1]), "rstd2": p.sb("rstd2", [P, 1]),
            "sq2": p.sb("sq2", [P, D]), "hpre": p.sb("hpre", [P, D])}

    cnw = p.dma("sync", nwb[:], nw_in.broadcast_to([P, D]))
    for i in range(ntiles):
        buf = xt[i % 2]
        cl = p.dma("sync", buf[:], x_in[i * P:(i + 1) * P, :])
        cxn = emit_ln(p, buf, xnt, scr, ld_wait=[cl, cnw])
        cs = p.op("scalar", lambda e: e.activation(scr2["sq2"][:], xnt[:], AF.Square, accum_out=scr2["rr2"][:]),
                  wait=[cxn])
        cv = p.op("vector", lambda e: e.tensor_scalar(scr2["rr2"][:], scr2["rr2"][:], 1.0 / D, EPS, ALU.mult, ALU.add),
                  wait=[cs])
        cs = p.op("scalar", lambda e: e.activation(scr2["sd2"][:], scr2["rr2"][:], AF.Sqrt), wait=[cv])
        p.op("vector", lambda e: e.reciprocal(scr2["rstd2"][:], scr2["sd2"][:]), wait=[cs])
        ch = p.op("vector", lambda e: e.scalar_tensor_tensor(
            ht[:], xnt[:], scr2["rstd2"][:], nwb[:], ALU.mult, ALU.mult))
        c1 = p.dma("sync", xn_out[i * P:(i + 1) * P, :], xnt[:], wait=[cxn], inc="st")
        c2 = p.dma("sync", h_out[i * P:(i + 1) * P, :], ht[:], wait=[ch], inc="st")
        p.barrier("vector", [c2])
        p.barrier("scalar", [c2])
    p.barrier("sync", [c2])
    return p.finalize()


def mm_group(p, psum_ap, mms, psum_free=None):
    n = len(mms)
    tok = None
    for i, (l, r, w) in enumerate(mms):
        w = list(w)
        if i == 0 and psum_free is not None:
            w.append(psum_free)
        tok = p.op("tensor",
                   (lambda e, l=l, r=r, s=(i == 0), t=(i == n - 1):
                    e.matmul(psum_ap, l, r, start=s, stop=t)),
                   wait=w, inc="pe" if i == n - 1 else None, chain=False)
    return tok


def _butterfly_stage_blocks(weights):
    w = np.asarray(weights, dtype=F)
    lo = np.zeros((6, 64, 64, 64), F)
    hi = np.zeros((6, 64, 64, 64), F)
    for k in range(12):
        s = 2 ** k
        nb = 4096 // (2 * s)
        wk = w[k].reshape(nb, s, 2, 2)
        b_idx = np.arange(nb)[:, None]
        pos = np.arange(s)[None, :]
        i0 = b_idx * 2 * s + pos
        i1 = i0 + s
        if k < 6:
            o = i0 // 64
            n0, n1 = i0 % 64, i1 % 64
            lo[k, o, n0, n0] = wk[..., 0, 0]
            lo[k, o, n0, n1] = wk[..., 0, 1]
            lo[k, o, n1, n0] = wk[..., 1, 0]
            lo[k, o, n1, n1] = wk[..., 1, 1]
        else:
            n = i0 % 64
            o0, o1 = i0 // 64, i1 // 64
            hi[k - 6, n, o0, o0] = wk[..., 0, 0]
            hi[k - 6, n, o0, o1] = wk[..., 0, 1]
            hi[k - 6, n, o1, o0] = wk[..., 1, 0]
            hi[k - 6, n, o1, o1] = wk[..., 1, 1]
    return lo, hi


def _pair_blockdiag(blocks):
    t = np.zeros((32, 128, 128), F)
    t[:, :64, :64] = blocks[0::2]
    t[:, 64:, 64:] = blocks[1::2]
    return t


def stage_fold_inputs(up_weights, down_weights, W_exp, b_exp, W_con):
    lo_u, hi_u = _butterfly_stage_blocks(up_weights)
    lo_d, hi_d = _butterfly_stage_blocks(down_weights)
    tr = lambda b: np.ascontiguousarray(np.transpose(b, (0, 2, 1)))
    ch = np.zeros((4, 6, 32, 128, 128), F)
    ch[0, 0] = _pair_blockdiag(tr(lo_u[5]))
    for j in range(1, 6):
        ch[0, j] = _pair_blockdiag(lo_u[5 - j])
    ch[1, 0] = _pair_blockdiag(tr(hi_u[5]))
    for j in range(1, 6):
        ch[1, j] = _pair_blockdiag(hi_u[5 - j])
    ch[2, 0] = _pair_blockdiag(hi_d[0])
    for j in range(1, 6):
        ch[2, j] = _pair_blockdiag(tr(hi_d[j]))
    ch[3, 0] = _pair_blockdiag(lo_d[0])
    for j in range(1, 6):
        ch[3, j] = _pair_blockdiag(tr(lo_d[j]))

    wexpT_ext = np.zeros((4096, 1032), F)
    wexpT_ext[:, :D] = np.asarray(W_exp, F).T
    wexpT_ext[:, D] = np.asarray(b_exp, F)
    wcon = np.asarray(W_con, F)
    rows = (np.arange(4096).reshape(64, 64).T).reshape(-1)
    wcon_perm = np.ascontiguousarray(wcon[rows])
    return {"fold_ch": ch, "wexpT_ext": wexpT_ext, "wcon_perm": wcon_perm}


def build_fold():
    p = Prog("fold")
    ch_in = p.din("fold_ch", [4, 6, 32, 128, 128])
    wx_in = p.din("wexpT_ext", [4096, 1032])
    wc_in = p.din("wcon_perm", [4096, 1024])
    wxf_out = p.dout("wexpfT", [4096, 1032])
    wcf_out = p.dout("wconf", [4096, 1024])
    ftmp = p.dtmp("ftmp", [4096, 1032])

    chain_sb = p.sb("chain_sb", [P, 4 * 32 * P])
    stg = p.sb("stg", [P, 6 * P])
    src = [p.sb(f"src{i}", [P, 1032]) for i in range(2)]
    osb = [p.sb(f"osb{i}", [P, 1032]) for i in range(2)]
    pst = [p.ps(f"pst{i}", [P, 512]) for i in range(2)]

    ev_last = {0: None, 1: None}
    for c in range(4):
        for t in range(32):
            slot = chain_sb[:, (c * 32 + t) * P:(c * 32 + t + 1) * P]
            cl = p.dma("sync", stg[:].rearrange("r (j k) -> r j k", j=6),
                       ch_in[c, :, t].transpose([1, 0, 2]))
            cseed = p.op("vector", lambda e, slot=slot: e.tensor_copy(slot, stg[:, 0:P]), wait=[cl])
            for j in range(1, 6):
                pb = pst[j % 2]
                tok = mm_group(p, pb[:, 0:P],
                               [(stg[:, j * P:(j + 1) * P], slot, [cl, cseed])],
                               psum_free=ev_last[j % 2])
                cseed = p.op("vector", lambda e, slot=slot, pb=pb: e.tensor_copy(slot, pb[:, 0:P]), wait=[tok])
                ev_last[j % 2] = cseed
            p.barrier("sync", [tok])
    chains_done = cseed

    passes = [
        (wx_in, 0, 1032, ftmp, True),
        (ftmp, 1, 1032, wxf_out, True),
        (wc_in, 2, 1024, ftmp, True),
        (ftmp, 3, 1024, wcf_out, False),
    ]
    prev_pass_toks = [chains_done]
    for (src_ap, c, ncols, dst_ap, perm) in passes:
        tile_toks = []
        for t in range(32):
            sbuf_src = src[t % 2]
            sbuf_out = osb[t % 2]
            cl = p.dma("sync", sbuf_src[:, :ncols], src_ap[t * P:(t + 1) * P, :ncols],
                       wait=list(prev_pass_toks) + ([tile_toks[-2]] if len(tile_toks) >= 2 else []))
            stat = chain_sb[:, (c * 32 + t) * P:(c * 32 + t + 1) * P]
            war = [tile_toks[-2]] if len(tile_toks) >= 2 else []
            ev = None
            nch = (ncols + 511) // 512
            for q in range(nch):
                n0, n1 = q * 512, min((q + 1) * 512, ncols)
                pb = pst[q % 2]
                tok = mm_group(p, pb[:, 0:n1 - n0],
                               [(stat, sbuf_src[:, n0:n1], [cl, chains_done])],
                               psum_free=ev_last[q % 2])
                ev = p.op("vector", lambda e, pb=pb, sbuf_out=sbuf_out, n0=n0, n1=n1:
                          e.tensor_copy(sbuf_out[:, n0:n1], pb[:, 0:n1 - n0]),
                          wait=[tok] + war)
                ev_last[q % 2] = ev
            stsem = "st%d" % (t % 2)
            if perm:
                dst_v = dst_ap[:, 0:ncols].rearrange("(b a) c -> b a c", a=64)
                p.dma("sync", dst_v[:, 2 * t, :], sbuf_out[0:64, :ncols], wait=[ev], inc=stsem)
                stok = p.dma("sync", dst_v[:, 2 * t + 1, :], sbuf_out[64:128, :ncols],
                             wait=[ev], inc=stsem)
            else:
                stok = p.dma("sync", dst_ap[t * P:(t + 1) * P, :ncols], sbuf_out[:, :ncols],
                             wait=[ev], inc=stsem)
            tile_toks.append(stok)
        prev_pass_toks = [tile_toks[-1], tile_toks[-2]]
    p.barrier("sync", prev_pass_toks)
    return p.finalize()


def build_l0ffn():
    p = Prog("l0ffn")
    xnT_in = p.din("xnT_sl", [D, TS])
    wexpf_in = p.din("wexpf_tiled", [P, 8, 4096])
    wconf_in = p.din("wconf_tiled", [P, 32, D])
    gb1_in = p.din("gbias1_packed", [P, 32])
    gb2_in = p.din("gbias2_packed", [P, 32])
    cb_in = p.din("cbias2", [2, D])
    g0_out = p.dout("g0_sl", [TS, D])

    KT, MT = 8, 32
    xnTb_all = p.sb("xnTb_all", [P, KT * 512], BF16)
    wexpfb_all = p.sb("wexpfb_all", [P, KT * 4096], BF16)
    wconfb_all = p.sb("wconfb_all", [P, MT * D], BF16)
    xnTb = [xnTb_all[:, k * 512:(k + 1) * 512] for k in range(KT)]
    wexpfb = [wexpfb_all[:, k * 4096:(k + 1) * 4096] for k in range(KT)]
    wconfb = [wconfb_all[:, m * D:(m + 1) * D] for m in range(MT)]
    Hb = p.sb("Hb", [P, MT * 512], BF16)
    gbias = p.sb("gbias", [P, 32])
    cbias = p.sb("cbias", [P, D])
    cb_tmp = p.sb("cb_tmp", [P, D])
    ysb = p.sb("ysb", [P, D])
    gsb = p.sb("gsb", [P, D])
    pst = [p.ps(f"pst{i}", [P, 512]) for i in range(2)]

    w1 = None
    for q in range(4):
        w1 = p.dma("gpsimd", wexpfb_all[:].rearrange("p (k c) -> p k c", k=KT)[:, 2 * q:2 * q + 2],
                   wexpf_in[:, 2 * q:2 * q + 2], wait=[w1], inc="ldw1")
    w2 = None
    for q in range(4):
        w2 = p.dma("gpsimd", wconfb_all[:].rearrange("p (m c) -> p m c", m=MT)[:, 8 * q:8 * q + 8],
                   wconf_in[:, 8 * q:8 * q + 8], wait=[w2], inc="ldw2")
    b1 = p.dma("sync", gbias[:], gb1_in, inc="ld")
    b2 = p.dma("sync", cb_tmp[:, 0:32], gb2_in, wait=[b1], inc="ld")
    gb_tok = p.op("vector", lambda e: e.tensor_tensor(gbias[:], gbias[:], cb_tmp[:, 0:32], ALU.add),
                  wait=[b2])
    b3 = p.dma("sync", cbias[:], cb_in[0:1, :].broadcast_to([P, D]), wait=[b2], inc="ld")
    cld = p.dma("sync", cb_tmp[:], cb_in[1:2, :].broadcast_to([P, D]), wait=[gb_tok, b3], inc="ld")
    cbias_tok = p.op("vector", lambda e: e.tensor_tensor(cbias[:], cbias[:], cb_tmp[:], ALU.add),
                     wait=[cld])

    ev_last = {0: None, 1: None}
    prev_y_tok = None
    last_sil = None
    last_st = None
    for g in range(4):
        xg_tok = p.dma("gpsimd",
                       xnTb_all[:].rearrange("p (k c) -> p k c", k=KT),
                       xnT_in.rearrange("(k p) t -> p k t", p=P)[:, :, g * 512:(g + 1) * 512],
                       wait=[prev_y_tok], inc="ldx")
        h_done = None
        for m in range(MT):
            pb = pst[m % 2]
            mms = [(wexpfb[k][:, m * P:(m + 1) * P], xnTb[k], [xg_tok, w1]) for k in range(KT)]
            tok = mm_group(p, pb[:], mms, psum_free=ev_last[m % 2])
            h_done = p.op("scalar", lambda e, pb=pb, m=m: e.activation(
                Hb[:, m * 512:(m + 1) * 512], pb[:], AF.Gelu, bias=gbias[:, m:m + 1]),
                wait=[tok, gb_tok])
            ev_last[m % 2] = h_done
        for tsub in range(4):
            ytoks = []
            for nch in range(2):
                pb = pst[nch]
                mms = [(Hb[:, m * 512 + tsub * P: m * 512 + (tsub + 1) * P],
                        wconfb[m][:, nch * 512:(nch + 1) * 512], [h_done, w2]) for m in range(MT)]
                tok = mm_group(p, pb[:], mms, psum_free=ev_last[nch])
                ytok = p.op("vector", lambda e, pb=pb, nch=nch: e.tensor_tensor(
                    ysb[:, nch * 512:(nch + 1) * 512], pb[:], cbias[:, nch * 512:(nch + 1) * 512], ALU.add),
                    wait=[tok, cbias_tok, last_sil])
                ev_last[nch] = ytok
                ytoks.append(ytok)
            last_sil = p.op("scalar", lambda e: e.activation(gsb[:], ysb[:], AF.Silu),
                            wait=ytoks + [last_st])
            row0 = g * 512 + tsub * P
            last_st = p.dma("sync", g0_out[row0:row0 + P, :], gsb[:], wait=[last_sil], inc="st")
            prev_y_tok = tok
    p.barrier("sync", [last_st])
    return p.finalize()


def build_l0attn():
    p = Prog("l0attn")
    hT_in = p.din("hT_full", [D, NTOK])
    waT_in = p.din("waT_tiled", [P, 8, 2176])
    wvT_in = p.din("wvT_tiled", [P, 8, P])
    y_out = p.dout("y0_sl", [P, 1024, 16])
    A_hbm = p.dtmp("A_hbm", [P, 1024, 272])

    waTb = p.sb("waTb", [P, 8 * 2176], BF16)
    wvTb = p.sb("wvTb", [P, 8 * P], BF16)
    hTb = [p.sb(f"hTb{i}", [P, 8 * P], BF16) for i in range(2)]
    gates = p.sb("gates", [P, 2176])
    red = p.sb("red", [P, P])
    v_sb = p.sb("v_sb", [P, P])
    ps_g = [p.ps(f"ps_g{i}", [P, 512]) for i in range(2)]
    ps_v = p.ps("ps_v", [P, P])

    w1 = None
    for q in range(2):
        w1 = p.dma("gpsimd", waTb[:].rearrange("p (k c) -> p k c", k=8)[:, 4 * q:4 * q + 4],
                   waT_in[:, 4 * q:4 * q + 4], wait=[w1], inc="ldw1")
    w2 = p.dma("gpsimd", wvTb[:].rearrange("p (k c) -> p k c", k=8), wvT_in, inc="ldw2")

    ev_last = {0: None, 1: None}
    prev = {}
    last_store = None
    hT_v = hT_in.rearrange("(k p) t -> p k t", p=P)
    for b in range(2):
        for tt in range(64):
            t0 = b * T + tt * P
            hbuf = hTb[tt % 2]
            lx = p.dma("gpsimd", hbuf[:].rearrange("p (k c) -> p k c", k=8),
                       hT_v[:, :, t0:t0 + P], wait=[prev.get(("mmv", tt % 2))],
                       inc="ldx%d" % (tt % 2))
            etoks = []
            for q in range(5):
                n0, n1 = q * 512, min((q + 1) * 512, 2176)
                pb = ps_g[q % 2]
                mms = [(hbuf[:, k * P:(k + 1) * P], waTb[:, k * 2176 + n0:k * 2176 + n1], [lx, w1])
                       for k in range(8)]
                tok = mm_group(p, pb[:, 0:n1 - n0], mms, psum_free=ev_last[q % 2])
                ev = p.op("scalar", lambda e, pb=pb, n0=n0, n1=n1: e.activation(
                    gates[:, n0:n1], pb[:, 0:n1 - n0], AF.Exp),
                    wait=[tok, prev.get("gates_free")])
                etoks.append(ev)
                ev_last[q % 2] = ev
            mms = [(hbuf[:, k * P:(k + 1) * P], wvTb[:, k * P:(k + 1) * P], [lx, w2])
                   for k in range(8)]
            vtok = mm_group(p, ps_v[:], mms, psum_free=prev.get("v_free"))
            prev[("mmv", tt % 2)] = vtok
            vev = p.op("vector", lambda e: e.tensor_copy(v_sb[:], ps_v[:]),
                       wait=[vtok, prev.get("u_done")])
            prev["v_free"] = vev
            c1 = p.op("vector", lambda e: e.tensor_reduce(
                red[:], gates[:].rearrange("p (g j) -> p g j", j=17), AX.X, ALU.add),
                wait=etoks)
            c2 = p.op("vector", lambda e: e.reciprocal(red[:], red[:]))
            c3 = p.op("vector", lambda e: e.tensor_tensor(
                gates[:].rearrange("p (g j) -> p g j", j=17), gates[:].rearrange("p (g j) -> p g j", j=17),
                red[:].unsqueeze(2).broadcast_to([P, P, 17]), ALU.mult))
            a0v = gates[:].rearrange("p (g j) -> p g j", j=17)[:, :, 0]
            u_done = p.op("vector", lambda e: e.tensor_tensor(a0v, a0v, v_sb[:], ALU.mult),
                          wait=[vev])
            prev["u_done"] = u_done
            chunk = tt // 8
            j0 = (tt % 8) * P
            dst = A_hbm[b * 64 + chunk: b * 64 + chunk + 57: 8, j0:j0 + P, :].transpose([1, 0, 2])
            last_store = p.dma("sync", dst, gates[:].rearrange("p (g c) -> p g c", g=8),
                               wait=[u_done, c3], inc="stA%d" % (tt % 2))
            prev["gates_free"] = last_store
            prev[("stA", tt % 2)] = last_store
    p1_done = [prev[("stA", 0)], prev[("stA", 1)]]

    JB = 32
    NB = 1024 // JB
    Ablk = [p.sb(f"Ablk{i}", [P, JB * 272]) for i in range(2)]
    yloc = p.sb("yloc", [P, 1025 * 17])
    tmp = p.sb("tmp", [P, 272])
    cbuf = p.sb("cbuf", [P, 2 * 17])
    initz = p.sb("initz", [P, 17])

    yv = yloc[:].rearrange("p (j s) -> p j s", s=17)
    m0 = p.op("vector", lambda e: e.memset(yloc[:], 0.0))
    m1 = p.op("vector", lambda e: e.memset(yv[:, :, 0:1], 1.0))

    def scan_pass(homog, extra_wait):
        toks = []
        lbt = {0: None, 1: None}
        for jb in range(NB):
            buf = Ablk[jb % 2]
            lw = list(p1_done) + list(extra_wait)
            if lbt[jb % 2] is not None:
                lw.append(lbt[jb % 2])
            la = p.dma("sync", buf[:].rearrange("p (j c) -> p j c", j=JB),
                       A_hbm[:, jb * JB:(jb + 1) * JB, :],
                       wait=lw, inc="ldA%d_%d" % (jb % 2, int(homog)))
            for jj in range(JB):
                j = jb * JB + jj
                asl = buf[:, jj * 272:(jj + 1) * 272]
                av = asl.rearrange("p (i s) -> p i s", s=17)
                if not homog:
                    state = yv[:, j, :]
                    p.op("vector", lambda e, av=av, state=state: e.tensor_tensor(
                        tmp[:].rearrange("p (i s) -> p i s", s=17), av,
                        state.unsqueeze(1).broadcast_to([P, 16, 17]), ALU.mult),
                        wait=[la, m0, m1])
                    tok = p.op("vector", lambda e, j=j: e.tensor_reduce(
                        yv[:, j + 1, 1:17], tmp[:].rearrange("p (i s) -> p i s", s=17),
                        AX.X, ALU.add))
                else:
                    cst = cbuf[:, (j % 2) * 17:(j % 2) * 17 + 17]
                    cdst = cbuf[:, ((j + 1) % 2) * 17 + 1:((j + 1) % 2) * 17 + 17]
                    p.op("vector", lambda e, av=av, cst=cst: e.tensor_tensor(
                        tmp[:].rearrange("p (i s) -> p i s", s=17), av,
                        cst.unsqueeze(1).broadcast_to([P, 16, 17]), ALU.mult),
                        wait=[la])
                    p.op("vector", lambda e, cdst=cdst: e.tensor_reduce(
                        cdst, tmp[:].rearrange("p (i s) -> p i s", s=17), AX.X, ALU.add))
                    tok = p.op("vector", lambda e, j=j: e.tensor_tensor(
                        yv[:, j + 1, 1:17], yv[:, j + 1, 1:17],
                        cbuf[:, ((j + 1) % 2) * 17 + 1:((j + 1) % 2) * 17 + 17], ALU.add))
            lbt[jb % 2] = tok
            toks.append(tok)
        return toks[-1]

    pass1_done = scan_pass(False, [])
    mz = p.op("vector", lambda e: e.memset(initz[:], 0.0), wait=[pass1_done])
    sh = []
    for q in range(16):
        sh.append(p.dma("sync", initz[q * 8 + 1:q * 8 + 8, 1:17], yv[q * 8:q * 8 + 7, 1024, 1:17],
                        wait=[mz], inc="shf%d" % q))
    cseed = p.op("vector", lambda e: e.memset(cbuf[:], 0.0), wait=[pass1_done])
    cseed = p.op("vector", lambda e: e.tensor_copy(cbuf[:, 1:17], initz[:, 1:17]), wait=sh + [cseed])
    pass2_done = scan_pass(True, [cseed, pass1_done])
    st1 = p.dma("sync", y_out[:, 0:512, :], yv[:, 1:513, 1:17], wait=[pass2_done], inc="sty")
    stok = p.dma("sync", y_out[:, 512:1024, :], yv[:, 513:1025, 1:17], wait=[st1], inc="sty")
    p.barrier("sync", [stok])
    return p.finalize()


def build_l0out():
    p = Prog("l0out")
    yT_in = p.din("yT_sl", [D, TS])
    woT_in = p.din("woT_tiled", [P, 8, D])
    xn_in = p.din("xn_sl", [TS, D])
    x_in = p.din("x_sl", [TS, D])
    g0_in = p.din("g0_sl", [TS, D])
    lnw_in = p.din("ln_w", [1, D])
    lnb_in = p.din("ln_b", [1, D])
    x1_out = p.dout("x1_sl", [TS, D])
    xn1_out = p.dout("xn1_sl", [TS, D])
    xn1n_out = p.dout("xn1n_sl", [TS, D])

    KT = 8
    yTb = p.sb("yTb", [P, KT * P], BF16)
    woTb = p.sb("woTb", [P, KT * D], BF16)
    xnt = p.sb("xnt", [P, D])
    xt = p.sb("xt", [P, D])
    g0t = p.sb("g0t", [P, D])
    s0t = p.sb("s0t", [P, D])
    x1t = p.sb("x1t", [P, D])
    xn1t = p.sb("xn1t", [P, D])
    xn1nt = p.sb("xn1nt", [P, D])
    lnwb = p.sb("lnwb", [P, D])
    lnbb = p.sb("lnbb", [P, D])
    scr = {k: p.sb("scr_" + k, [P, D if k in ("sq", "xm", "xmr", "xg") else 1])
           for k in ("sum1", "nm", "rr", "sd", "rstd", "sq", "xm", "xmr", "xg")}
    pst = [p.ps(f"pst{i}", [P, 512]) for i in range(2)]

    w1 = p.dma("gpsimd", woTb[:].rearrange("p (k c) -> p k c", k=KT), woT_in, inc="ldw")
    lw1 = p.dma("sync", lnwb[:], lnw_in.broadcast_to([P, D]), inc="ldc")
    lw2 = p.dma("sync", lnbb[:], lnb_in.broadcast_to([P, D]), wait=[lw1], inc="ldc")

    yT_v = yT_in.rearrange("(k p) t -> p k t", p=P)
    ev_last = {0: None, 1: None}
    prev_st = None
    prev_mm = None
    for i in range(TS // P):
        t0 = i * P
        ly = p.dma("gpsimd", yTb[:].rearrange("p (k c) -> p k c", k=KT)[:, :, 0:P],
                   yT_v[:, :, t0:t0 + P], wait=[prev_mm], inc="ldy")
        l1 = p.dma("sync", xnt[:], xn_in[t0:t0 + P, :], wait=[prev_st], inc="ldt")
        l2 = p.dma("sync", xt[:], x_in[t0:t0 + P, :], wait=[l1], inc="ldt")
        l3 = p.dma("sync", g0t[:], g0_in[t0:t0 + P, :], wait=[l2], inc="ldt")
        stoks = []
        for nch in range(2):
            pb = pst[nch]
            mms = [(yTb[:, k * P:(k + 1) * P], woTb[:, k * D + nch * 512:k * D + (nch + 1) * 512],
                    [ly, w1]) for k in range(KT)]
            tok = mm_group(p, pb[:], mms, psum_free=ev_last[nch])
            prev_mm = tok
            ev = p.op("vector", lambda e, pb=pb, nch=nch: e.tensor_tensor(
                s0t[:, nch * 512:(nch + 1) * 512], pb[:], xnt[:, nch * 512:(nch + 1) * 512], ALU.add),
                wait=[tok, l3])
            ev_last[nch] = ev
            stoks.append(ev)
        p.op("vector", lambda e: e.tensor_tensor(x1t[:], s0t[:], g0t[:], ALU.mult), wait=stoks)
        cx1 = p.op("vector", lambda e: e.tensor_tensor(x1t[:], x1t[:], xt[:], ALU.add))
        cxn1 = emit_ln(p, x1t, xn1t, scr)
        cxn1n = emit_ln(p, xn1t, xn1nt, scr, ld_wait=[lw2], gain_bias=(lnwb, lnbb))
        s1 = p.dma("sync", x1_out[t0:t0 + P, :], x1t[:], wait=[cx1], inc="st")
        s2 = p.dma("sync", xn1_out[t0:t0 + P, :], xn1t[:], wait=[cxn1, s1], inc="st")
        prev_st = p.dma("sync", xn1n_out[t0:t0 + P, :], xn1nt[:], wait=[cxn1n, s2], inc="st")
    p.barrier("sync", [prev_st])
    return p.finalize()


def build_mamba1(debug_xs=False):
    p = Prog("mamba1")
    xnT_in = p.din("xn1nT_full", [D, NTOK])
    winT_in = p.din("winT_tiled", [P, 8, 512])
    convw_in = p.din("convw_tiled", [P, 2, 4])
    convb_in = p.din("convb_tiled", [P, 2])
    wxT_in = p.din("wxT_tiled", [P, 2, 96])
    u_out = p.dout("u_sl", [256, NTOK])
    z_out = p.dout("zsil_sl", [256, NTOK])
    xd_out = p.dout("xdbl_part", [96, NTOK])

    PADW = T + 3
    winTb = p.sb("winTb", [P, 8 * 512], BF16)
    wxTb = p.sb("wxTb", [P, 2 * 96], BF16)
    convw = p.sb("convw", [P, 2 * 4])
    convb = p.sb("convb", [P, 2])
    xnTb = p.sb("xnTb", [P, 8 * 512], BF16)
    xs_pad = p.sb("xs_pad", [P, 2 * PADW])
    xca = p.sb("xca", [P, T])
    xcb = p.sb("xcb", [P, T])
    zch = p.sb("zch", [P, 512])
    ub = p.sb("ub", [P, 2 * 512], BF16)
    xde = p.sb("xde", [P, 512])
    pst = [p.ps(f"pst{i}", [P, 512]) for i in range(2)]

    w1 = p.dma("gpsimd", winTb[:].rearrange("p (k c) -> p k c", k=8), winT_in, inc="ldw")
    w2 = p.dma("gpsimd", wxTb[:].rearrange("p (k c) -> p k c", k=2), wxT_in, wait=[w1], inc="ldw")
    c1 = p.dma("sync", convw[:].rearrange("p (t k) -> p t k", t=2), convw_in, inc="ldc")
    c2 = p.dma("sync", convb[:], convb_in, wait=[c1], inc="ldc")

    xnT_v = xnT_in.rearrange("(k p) t -> p k t", p=P)
    mpad = p.op("vector", lambda e: e.memset(xs_pad[:], 0.0))

    ev_last = {0: None, 1: None}
    prev_mm = None
    last_u_store = None
    last_z = None
    for et in range(2):
        for tc in range(32):
            b, tb = tc // 16, (tc % 16) * 512
            t0 = tc * 512
            lx = p.dma("gpsimd", xnTb[:].rearrange("p (k c) -> p k c", k=8),
                       xnT_v[:, :, t0:t0 + 512], wait=[prev_mm], inc="ldx")
            mms = [(winTb[:, k * 512 + et * P:k * 512 + (et + 1) * P], xnTb[:, k * 512:(k + 1) * 512],
                    [lx, w2]) for k in range(8)]
            tokx = mm_group(p, pst[0][:], mms, psum_free=ev_last[0])
            exs = p.op("vector", lambda e, b=b, tb=tb: e.tensor_copy(
                xs_pad[:, b * PADW + 3 + tb: b * PADW + 3 + tb + 512], pst[0][:]),
                wait=[tokx, mpad])
            ev_last[0] = exs
            mms = [(winTb[:, k * 512 + (2 + et) * P:k * 512 + (3 + et) * P], xnTb[:, k * 512:(k + 1) * 512],
                    [lx, w2]) for k in range(8)]
            tokz = mm_group(p, pst[1][:], mms, psum_free=ev_last[1])
            prev_mm = tokz
            ez = p.op("scalar", lambda e: e.activation(zch[:], pst[1][:], AF.Silu),
                      wait=[tokz, last_z])
            ev_last[1] = ez
            last_z = p.dma("sync", z_out[et * P:(et + 1) * P, t0:t0 + 512], zch[:],
                           wait=[ez], inc="stz")
        dbg_tok = None
        if debug_xs and et == 0:
            xs_dbg = p.dout("xs_dbg", [P, 2 * PADW])
            dbg_tok = p.dma("sync", xs_dbg, xs_pad[:], wait=[exs], inc="stdbg")
        for b in range(2):
            off = b * PADW
            cv0 = p.op("vector", lambda e, off=off, et=et: e.tensor_scalar(
                xca[:], xs_pad[:, off:off + T], convw[:, 4 * et:4 * et + 1],
                convb[:, et:et + 1], ALU.mult, ALU.add), wait=[exs, c2, last_u_store])
            cv1 = p.op("vector", lambda e, off=off, et=et: e.scalar_tensor_tensor(
                xcb[:], xs_pad[:, off + 1:off + 1 + T], convw[:, 4 * et + 1:4 * et + 2],
                xca[:], ALU.mult, ALU.add))
            cv2 = p.op("vector", lambda e, off=off, et=et: e.scalar_tensor_tensor(
                xca[:], xs_pad[:, off + 2:off + 2 + T], convw[:, 4 * et + 2:4 * et + 3],
                xcb[:], ALU.mult, ALU.add))
            cv3 = p.op("vector", lambda e, off=off, et=et: e.scalar_tensor_tensor(
                xcb[:], xs_pad[:, off + 3:off + 3 + T], convw[:, 4 * et + 3:4 * et + 4],
                xca[:], ALU.mult, ALU.add))
            cu = p.op("scalar", lambda e: e.activation(xca[:], xcb[:], AF.Silu), wait=[cv3])
            last_u_store = p.dma("sync", u_out[et * P:(et + 1) * P, b * T:(b + 1) * T], xca[:],
                                 wait=[cu], inc="stu")
        if et == 0:
            mpad = p.op("vector", lambda e: e.memset(xs_pad[:], 0.0),
                        wait=[cv3, last_u_store, dbg_tok])
    prev_ev = None
    last_xd = None
    for tc in range(32):
        t0 = tc * 512
        lu = p.dma("gpsimd", ub[:].rearrange("p (t c) -> p t c", t=2),
                   u_out.rearrange("(t p) n -> p t n", p=P)[:, :, t0:t0 + 512],
                   wait=[last_u_store, prev_ev], inc="ldu")
        mms = [(wxTb[:, k * 96:(k + 1) * 96], ub[:, k * 512:(k + 1) * 512], [lu, w2])
               for k in range(2)]
        tok = mm_group(p, pst[0][:96, :], mms, psum_free=prev_ev)
        prev_ev = p.op("vector", lambda e: e.tensor_copy(xde[:96, :], pst[0][:96, :]),
                       wait=[tok, last_xd])
        last_xd = p.dma("sync", xd_out[:, t0:t0 + 512], xde[:96, :], wait=[prev_ev], inc="stx")
    p.barrier("sync", [last_xd, last_u_store, last_z])
    return p.finalize()


def build_mamba2():
    p = Prog("mamba2")
    xdp_in = p.din("xdbl_parts", [NCORES, 96, NTOK])
    wdtT_in = p.din("wdtT_sl", [64, 256])
    bdt_in = p.din("bdt_tiled", [P, 2])
    alog_in = p.din("alog_tiled", [P, 2, 16])
    dskip_in = p.din("dskip_tiled", [P, 2])
    u_in = p.din("u_sl", [256, NTOK])
    z_in = p.din("zsil_sl", [256, NTOK])
    y_out = p.dout("y1_sl", [256, NTOK])
    bc_dram = p.dtmp("bc_dram", [32, T])

    TQ = 1024
    NTQ = T // TQ
    xdbl = p.sb("xdbl", [96, T])
    xdp = [p.sb(f"xdp{i}", [96, 2048]) for i in range(2)]
    wdtT = p.sb("wdtT", [64, 256])
    bdt = p.sb("bdt", [P, 2])
    A_sb = p.sb("A_sb", [P, 32])
    dskip = p.sb("dskip", [P, 2])
    neg1 = p.sb("neg1", [P, 1])
    one1 = p.sb("one1", [P, 1])
    t_abs = p.sb("t_abs", [P, 512])
    t_e = p.sb("t_e", [P, 512])
    t_l = p.sb("t_l", [P, 512])
    t_r = p.sb("t_r", [P, 512])
    dt2 = p.sb("dt2", [P, 2 * TQ])
    du2 = p.sb("du2", [P, 2 * TQ])
    u2 = p.sb("u2", [P, 2 * TQ])
    z2 = p.sb("z2", [P, 2 * TQ])
    bb = [p.sb(f"bb{i}", [P, TQ]) for i in range(2)]
    cc = [p.sb(f"cc{i}", [P, TQ]) for i in range(2)]
    dAt = p.sb("dAt", [P, 2 * TQ])
    dBt = p.sb("dBt", [P, TQ])
    ht = p.sb("ht", [P, TQ])
    yb2 = p.sb("yb2", [P, 2 * TQ])
    carry = p.sb("carry", [P, 32])
    pst = [p.ps(f"pst{i}", [P, 512]) for i in range(2)]

    m1 = p.op("vector", lambda e: e.memset(neg1[:], -1.0))
    m2 = p.op("vector", lambda e: e.memset(one1[:], 1.0))
    lw = p.dma("sync", wdtT[:], wdtT_in, inc="ldw")
    l1 = p.dma("sync", bdt[:], bdt_in, inc="ldc")
    l2 = p.dma("sync", A_sb[:], alog_in.rearrange("p a n -> p (a n)"), wait=[l1], inc="ldc")
    l3 = p.dma("sync", dskip[:], dskip_in, wait=[l2], inc="ldc")
    sA = p.op("scalar", lambda e: e.activation(A_sb[:], A_sb[:], AF.Exp), wait=[l2])
    sA = p.op("vector", lambda e: e.tensor_scalar(A_sb[:], A_sb[:], -1.0, None, ALU.mult), wait=[sA])

    last = {}
    ev_last = {0: None, 1: None}
    for b in range(2):
        sum_chunk = {}
        bc_store = {}
        prev = {0: last.get("xdp0"), 1: last.get("xdp1")}
        for tc in range(4):
            t0 = tc * 2048
            sum_tok = None
            for j in range(NCORES):
                q = "sync" if j % 2 == 0 else "gpsimd"
                lb = p.dma(q, xdp[j % 2][:], xdp_in[j, :, b * T + t0:b * T + t0 + 2048],
                           wait=[prev[j % 2]], inc="ldp%d" % (j % 2))
                if j == 0:
                    sum_tok = p.op("vector", lambda e, t0=t0: e.tensor_copy(
                        xdbl[:, t0:t0 + 2048], xdp[0][:]),
                        wait=[lb, last.get("xrd%d" % tc)])
                else:
                    sum_tok = p.op("vector", lambda e, t0=t0, j=j: e.tensor_tensor(
                        xdbl[:, t0:t0 + 2048], xdbl[:, t0:t0 + 2048], xdp[j % 2][:], ALU.add),
                        wait=[lb])
                prev[j % 2] = sum_tok
            sum_chunk[tc] = sum_tok
            bc_store[tc] = p.dma("sync", bc_dram[:, t0:t0 + 2048], xdbl[64:96, t0:t0 + 2048],
                                 wait=[sum_tok, last.get("bcrd%d" % tc)],
                                 inc="stbc%d" % tc)
        last["xdp0"], last["xdp1"] = prev[0], prev[1]

        cm = p.op("vector", lambda e: e.memset(carry[:], 0.0), wait=[last.get("ce")])
        for tq in range(NTQ):
            tl0 = tq * TQ
            t0g = b * T + tl0
            tc = tq // 2
            lus, lzs = [], []
            for et in range(2):
                lus.append(p.dma("sync", u2[:, et * TQ:(et + 1) * TQ],
                                 u_in[et * P:(et + 1) * P, t0g:t0g + TQ],
                                 wait=[last.get("y2_%d" % et)], inc="ldu%d" % et))
                lzs.append(p.dma("gpsimd", z2[:, et * TQ:(et + 1) * TQ],
                                 z_in[et * P:(et + 1) * P, t0g:t0g + TQ],
                                 wait=[last.get("y3_%d" % et)], inc="ldz%d" % et))
            cdus = []
            for et in range(2):
                for h in range(2):
                    pb = pst[h % 2]
                    tok = mm_group(p, pb[:],
                                   [(wdtT[:, et * P:(et + 1) * P],
                                     xdbl[0:64, tl0 + h * 512:tl0 + (h + 1) * 512],
                                     [lw, sum_chunk[tc], last.get("dtp%d" % h)])],
                                   psum_free=ev_last[h])
                    cdtp = p.op("vector", lambda e, pb=pb, et=et: e.tensor_scalar(
                        t_abs[:], pb[:], bdt[:, et:et + 1], None, ALU.add),
                        wait=[tok, m1, m2, l1, last.get("tabs")])
                    ev_last[h] = cdtp
                    last["dtp%d" % h] = cdtp
                    s4 = p.op("scalar", lambda e: e.activation(t_r[:], t_abs[:], AF.Relu),
                              wait=[cdtp])
                    s1 = p.op("scalar", lambda e: e.activation(t_abs[:], t_abs[:], AF.Abs))
                    s2 = p.op("scalar", lambda e: e.activation(t_e[:], t_abs[:], AF.Exp,
                                                               scale=neg1[:, 0:1]))
                    s3 = p.op("scalar", lambda e: e.activation(t_l[:], t_e[:], AF.Ln,
                                                               bias=one1[:, 0:1]))
                    last["tabs"] = s3
                    last["dt_%d" % et] = p.op("vector", lambda e, et=et, h=h: e.tensor_tensor(
                        dt2[:, et * TQ + h * 512:et * TQ + (h + 1) * 512], t_l[:], t_r[:],
                        ALU.add), wait=[s3, s4])
                cdus.append(p.op("vector", lambda e, et=et: e.tensor_tensor(
                    du2[:, et * TQ:(et + 1) * TQ], dt2[:, et * TQ:(et + 1) * TQ],
                    u2[:, et * TQ:(et + 1) * TQ], ALU.mult), wait=[lus[et]]))
            for n in range(16):
                lbB = p.dma("sync", bb[n % 2][:],
                            bc_dram[n:n + 1, tl0:tl0 + TQ].rearrange("a b -> (a b)")
                            .unsqueeze(0).broadcast_to([P, TQ]),
                            wait=[bc_store[tc], last.get("cB%d" % (n % 2))], inc="ldbB%d" % (n % 2))
                lbC = p.dma("gpsimd", cc[n % 2][:],
                            bc_dram[16 + n:17 + n, tl0:tl0 + TQ].rearrange("a b -> (a b)")
                            .unsqueeze(0).broadcast_to([P, TQ]),
                            wait=[bc_store[tc], last.get("cC%d" % (n % 2))], inc="ldbC%d" % (n % 2))
                for et in range(2):
                    idx = et * 16 + n
                    sdA = p.op("scalar", lambda e, et=et, idx=idx: e.activation(
                        dAt[:, et * TQ:(et + 1) * TQ], dt2[:, et * TQ:(et + 1) * TQ], AF.Exp,
                        scale=A_sb[:, idx:idx + 1]),
                        wait=[sA, last.get("scan%d" % et), last.get("dt_%d" % et)])
                    cdBu = p.op("vector", lambda e, et=et, n=n: e.tensor_tensor(
                        dBt[:], du2[:, et * TQ:(et + 1) * TQ], bb[n % 2][:], ALU.mult),
                        wait=[lbB, cdus[et]])
                    if et == 1:
                        last["cB%d" % (n % 2)] = cdBu
                    sc = p.op("vector", lambda e, et=et, idx=idx: e.tensor_tensor_scan(
                        ht[:], dAt[:, et * TQ:(et + 1) * TQ], dBt[:],
                        carry[:, idx:idx + 1], ALU.mult, ALU.add), wait=[sdA, cm])
                    last["scan%d" % et] = sc
                    ce = p.op("vector", lambda e, idx=idx: e.tensor_copy(
                        carry[:, idx:idx + 1], ht[:, TQ - 1:TQ]))
                    last["ce"] = ce
                    chC = p.op("vector", lambda e, n=n: e.tensor_tensor(ht[:], ht[:], cc[n % 2][:],
                                                                        ALU.mult), wait=[lbC])
                    if et == 1:
                        last["cC%d" % (n % 2)] = chC
                    if n == 0:
                        p.op("vector", lambda e, et=et: e.tensor_copy(
                            yb2[:, et * TQ:(et + 1) * TQ], ht[:]),
                            wait=[last.get("sty%d" % et)])
                    else:
                        p.op("vector", lambda e, et=et: e.tensor_tensor(
                            yb2[:, et * TQ:(et + 1) * TQ], yb2[:, et * TQ:(et + 1) * TQ],
                            ht[:], ALU.add))
            for et in range(2):
                y2 = p.op("vector", lambda e, et=et: e.scalar_tensor_tensor(
                    yb2[:, et * TQ:(et + 1) * TQ], u2[:, et * TQ:(et + 1) * TQ],
                    dskip[:, et:et + 1], yb2[:, et * TQ:(et + 1) * TQ], ALU.mult, ALU.add),
                    wait=[l3])
                last["y2_%d" % et] = y2
                y3 = p.op("vector", lambda e, et=et: e.tensor_tensor(
                    yb2[:, et * TQ:(et + 1) * TQ], yb2[:, et * TQ:(et + 1) * TQ],
                    z2[:, et * TQ:(et + 1) * TQ], ALU.mult), wait=[lzs[et]])
                last["y3_%d" % et] = y3
                sty = p.dma("sync", y_out[et * P:(et + 1) * P, t0g:t0g + TQ],
                            yb2[:, et * TQ:(et + 1) * TQ],
                            wait=[y3, last.get("styq%d" % et)], inc="sty%d" % et)
                last["sty%d" % et] = sty
                last["styq%d" % et] = sty
            if tq % 2 == 1:
                last["xrd%d" % tc] = last["dtp1"]
                last["bcrd%d" % tc] = last["cC1"]
    p.barrier("sync", [last["sty0"], last["sty1"]])
    return p.finalize()


def build_l1ffn_iter(first):
    p = Prog("l1ffn_iter")
    xcT_in = p.din("xcT_prev", [D, TS])
    if not first:
        xnT_in = p.din("xnewT_prev", [D, TS])
        done_in = p.din("done_rep", [P, 1])
    wmT_in = p.din("wmT_full", [D, 12288])
    wpT_in = p.din("wpT_full", [12288, D])
    mask_in = p.din("mask_packed", [P, 96])
    xnew_out = p.dout("xnewT_sl", [D, TS])
    if not first:
        xcu_out = p.dout("xcT_used", [D, TS])
    pooled_out = p.dout("pooled_part", [P, 8])

    KT, MT = 8, 96
    M_ACT = list(range(32)) if first else list(range(MT))
    xcTb = p.sb("xcTb", [P, KT * TS], BF16)
    Hb = p.sb("Hb", [P, MT * 512], BF16)
    wmb = [p.sb(f"wmb{i}", [P, KT * P], BF16) for i in range(2)]
    wpb = [p.sb(f"wpb{i}", [P, D], BF16) for i in range(2)]
    mask = p.sb("mask", [P, 96])
    xn_ev = p.sb("xn_ev", [P, 512])
    pl_acc = p.sb("pl_acc", [P, 8])
    pl_tmp = p.sb("pl_tmp", [P, 1])
    xcf = p.sb("xcf", [P, TS])
    xnf = p.sb("xnf", [P, TS])
    done_sb = p.sb("done_sb", [P, 1])
    omd_sb = p.sb("omd_sb", [P, 1])
    psx = [p.ps(f"psx{i}", [P, 512]) for i in range(8)]
    psh = [psx[6], psx[7]]

    lm = p.dma("sync", mask[:], mask_in, inc="ldm")
    mz = p.op("vector", lambda e: e.memset(pl_acc[:], 0.0))

    if first:
        xg = None
        for q in range(2):
            xg = p.dma("gpsimd", xcTb[:].rearrange("p (k t) -> p k t", k=KT)[:, 4 * q:4 * q + 4],
                       xcT_in.rearrange("(k p) t -> p k t", p=P)[:, 4 * q:4 * q + 4],
                       wait=[xg], inc="ldx")
        blend_tok = xg
    else:
        ld = p.dma("sync", done_sb[:], done_in, inc="ldd")
        cdn = p.op("vector", lambda e: e.tensor_scalar(omd_sb[:], done_sb[:], -1.0, 1.0,
                                                       ALU.mult, ALU.add), wait=[ld])
        prev_st = {}
        xcf_free = []
        for k in range(KT):
            l1 = p.dma("sync", xcf[:], xcT_in[k * P:(k + 1) * P, :],
                       wait=xcf_free, inc="ldc1")
            l2 = p.dma("gpsimd", xnf[:], xnT_in[k * P:(k + 1) * P, :],
                       wait=[prev_st.get("n"), l1], inc="ldc2")
            c1 = p.op("vector", lambda e: e.tensor_scalar(xcf[:], xcf[:], done_sb[:, 0:1], None, ALU.mult),
                      wait=[l1, cdn])
            c2 = p.op("vector", lambda e: e.scalar_tensor_tensor(
                xcf[:], xnf[:], omd_sb[:, 0:1], xcf[:], ALU.mult, ALU.add), wait=[l2])
            s1 = p.dma("sync", xcu_out[k * P:(k + 1) * P, :], xcf[:], wait=[c2], inc="stc%d" % (k % 2))
            cb = p.op("vector", lambda e, k=k: e.tensor_copy(xcTb[:, k * TS:(k + 1) * TS], xcf[:]))
            xcf_free = [cb, s1]
            prev_st["n"] = c2
            blend_tok = cb
            prev_st[k % 2] = s1
        p.barrier("sync", [prev_st[0], prev_st[1]])

    wm_v = wmT_in.rearrange("(k p) f -> p k f", p=P)
    ev_h = {0: None, 1: None}
    ev_x = {i: None for i in range(8)}
    prev_w = {0: None, 1: None}
    prev_wp = {0: None, 1: None}
    last_px = None
    last_xst = {}
    for g in range(4):
        t0 = g * 512
        hev = None
        for m in M_ACT:
            wb = wmb[m % 2]
            lw = p.dma("gpsimd", wb[:].rearrange("p (k c) -> p k c", k=KT)[:, 0:4],
                       wm_v[:, 0:4, m * P:(m + 1) * P], wait=[prev_w[m % 2]], inc="ldw%d" % (m % 2))
            lw2 = p.dma("gpsimd", wb[:].rearrange("p (k c) -> p k c", k=KT)[:, 4:8],
                        wm_v[:, 4:8, m * P:(m + 1) * P], wait=[prev_w[m % 2]], inc="lds%d" % (m % 2))
            pb = psh[m % 2]
            mms = [(wb[:, k * P:(k + 1) * P], xcTb[:, k * TS + t0:k * TS + t0 + 512],
                    [lw, lw2, blend_tok]) for k in range(KT)]
            tok = mm_group(p, pb[:], mms, psum_free=ev_h[m % 2])
            prev_w[m % 2] = tok
            hev = p.op("scalar", lambda e, pb=pb, m=m: e.activation(
                Hb[:, m * 512:(m + 1) * 512], pb[:], AF.Relu, scale=mask[:, m:m + 1]),
                wait=[tok, lm])
            ev_h[m % 2] = hev
        for m in M_ACT:
            wb = wpb[m % 2]
            lwp = p.dma("gpsimd", wb[:, 0:512], wpT_in[m * P:(m + 1) * P, 0:512],
                        wait=[prev_wp[m % 2]], inc="ldp%d" % (m % 2))
            lwp2 = p.dma("gpsimd", wb[:, 512:1024], wpT_in[m * P:(m + 1) * P, 512:1024],
                         wait=[prev_wp[m % 2]], inc="ldq%d" % (m % 2))
            for dt in range(8):
                tok = p.op("tensor",
                           (lambda e, m=m, dt=dt, wb=wb, s=(m == M_ACT[0]), t=(m == M_ACT[-1]):
                            e.matmul(psx[dt][:], wb[:, dt * P:(dt + 1) * P],
                                     Hb[:, m * 512:(m + 1) * 512], start=s, stop=t)),
                           wait=([lwp, lwp2, hev] if dt == 0 else [lwp, lwp2]) +
                                ([ev_x[dt]] if m == M_ACT[0] else []),
                           inc="pe" if (dt == 7 or m == M_ACT[-1]) else None, chain=False)
            prev_wp[m % 2] = tok
        xg_toks = []
        for dt in range(8):
            ev = p.op("vector", lambda e, dt=dt: e.tensor_copy(xn_ev[:], psx[dt][:]),
                      wait=[tok, last_px])
            ev_x[dt] = ev
            st = p.dma("sync", xnew_out[dt * P:(dt + 1) * P, t0:t0 + 512], xn_ev[:],
                       wait=[ev], inc="stx%d" % (dt % 2))
            last_px = st
            last_xst[dt % 2] = st
            c1 = p.op("vector", lambda e: e.tensor_reduce(pl_tmp[:], xn_ev[:], AX.X, ALU.add),
                      wait=[mz])
            xg_toks.append(p.op("vector", lambda e, dt=dt: e.tensor_tensor(
                pl_acc[:, dt:dt + 1], pl_acc[:, dt:dt + 1], pl_tmp[:], ALU.add)))
        ev_h[0], ev_h[1] = ev_x[6], ev_x[7]
    pst_tok = p.dma("sync", pooled_out, pl_acc[:], wait=[xg_toks[-1]], inc="stp")
    p.barrier("sync", [pst_tok, last_xst[0], last_xst[1]])
    return p.finalize()


def build_l1ffn_ctl():
    p = Prog("l1ffn_ctl")
    parts_in = p.din("pooled_parts", [NCORES, P, 8])
    wc1T_in = p.din("wc1T_tiled", [P, 8, 256])
    wc2T_in = p.din("wc2T_tiled", [P, 2, 12288])
    hist_in = p.din("history", [P, 192])
    act_in = p.din("active", [P, 192])
    done_in = p.din("done_rep", [P, 1])
    act_out = p.dout("active_next", [P, 192])
    hist_out = p.dout("history_next", [P, 192])
    done_out = p.dout("done_next", [P, 1])

    prt = p.sb("prt", [P, 8])
    pl = p.sb("pl", [P, 16])
    wc1 = p.sb("wc1", [P, 8 * 256], BF16)
    wc2 = p.sb("wc2", [P, 2 * 12288], BF16)
    r_sb = p.sb("r_sb", [P, 4], BF16)
    ck = p.sb("ck", [P, 192])
    hist = p.sb("hist", [P, 192])
    act = p.sb("act", [P, 192])
    done = p.sb("done", [P, 1])
    omd = p.sb("omd", [P, 1])
    lo = p.sb("lo", [P, 2])
    hi = p.sb("hi", [P, 2])
    mid = p.sb("mid", [P, 2])
    ge = p.sb("ge", [P, 192])
    rs = p.sb("rs", [P, 2])
    pred = p.sb("pred", [P, 2])
    omp = p.sb("omp", [P, 2])
    tq = p.sb("tq", [P, 2])
    nov = p.sb("nov", [P, 1])
    ps1 = p.ps("ps1", [P, 8])
    ps2 = p.ps("ps2", [P, 8])

    lws = p.dma("gpsimd", wc1[:].rearrange("p (k c) -> p k c", k=8), wc1T_in, inc="lw1")
    lw2 = p.dma("gpsimd", wc2[:].rearrange("p (k c) -> p k c", k=2), wc2T_in, inc="lw2")
    lh = p.dma("sync", hist[:], hist_in, inc="ldh")
    la = p.dma("sync", act[:], act_in, wait=[lh], inc="ldh")
    ldn = p.dma("sync", done[:], done_in, wait=[la], inc="ldh")

    toks = []
    prev = None
    for j in range(NCORES):
        b = j // 4
        lp = p.dma("sync", prt[:], parts_in[j], wait=[prev], inc="ldp")
        if j % 4 == 0:
            tok = p.op("vector", lambda e, b=b: e.tensor_copy(
                pl[:].rearrange("p (dt b2) -> p dt b2", b2=2)[:, :, b], prt[:]), wait=[lp])
        else:
            tok = p.op("vector", lambda e, b=b: e.tensor_tensor(
                pl[:].rearrange("p (dt b2) -> p dt b2", b2=2)[:, :, b],
                pl[:].rearrange("p (dt b2) -> p dt b2", b2=2)[:, :, b], prt[:], ALU.add),
                wait=[lp])
        prev = tok
    cpl = p.op("vector", lambda e: e.tensor_scalar(pl[:], pl[:], 1.0 / T, None, ALU.mult))
    plb = p.sb("plb", [P, 16], BF16)
    cpl = p.op("vector", lambda e: e.tensor_copy(plb[:], pl[:]), wait=[cpl])
    rtoks = []
    for mt in range(2):
        mms = [(wc1[:, (k * 256) + mt * P:(k * 256) + (mt + 1) * P], plb[:, k * 2:(k + 1) * 2], [lws, cpl])
               for k in range(8)]
        tok = mm_group(p, ps1[:, 0:2], mms, psum_free=rtoks[-1] if rtoks else None)
        rtoks.append(p.op("scalar", lambda e, mt=mt: e.activation(
            r_sb[:, mt * 2:(mt + 1) * 2], ps1[:, 0:2], AF.Relu), wait=[tok]))
    cktoks = []
    for m in range(96):
        mms = [(wc2[:, k * 12288 + m * P:k * 12288 + (m + 1) * P], r_sb[:, k * 2:(k + 1) * 2],
                [lw2, rtoks[-1]]) for k in range(2)]
        tok = mm_group(p, ps2[:, 0:2], mms, psum_free=cktoks[-1] if cktoks else None)
        cktoks.append(p.op("vector", lambda e, m=m: e.tensor_copy(
            ck[:, m * 2:(m + 1) * 2], ps2[:, 0:2]), wait=[tok]))
    ck_done = cktoks[-1]

    ckv = ck[:].rearrange("p (m b2) -> p b2 m", b2=2)
    ones = p.sb("ones", [P, P])
    lohi = p.sb("lohi", [P, 4])
    lohi1 = p.sb("lohi1", [1, 4])
    lh_dram = p.dtmp("lh_dram", [P, 4])
    psr = p.ps("psr", [P, 4])
    mon = p.op("vector", lambda e: e.memset(ones[:], 1.0))

    def part_sum(src_ap, dst_ap, w):
        n = src_ap.shape[-1]
        t1 = p.op("tensor", lambda e: e.matmul(psr[0:1, 0:n], ones[:, 0:1], src_ap, start=True, stop=True),
                  wait=list(w) + [mon, part_sum.free], inc="pe", chain=False)
        t2 = p.op("vector", lambda e: e.tensor_copy(lohi1[0:1, 0:n], psr[0:1, 0:n]), wait=[t1])
        t3 = p.op("tensor", lambda e: e.matmul(psr[:, 0:n], ones[0:1, :], lohi1[0:1, 0:n], start=True, stop=True),
                  wait=[t2], inc="pe", chain=False)
        t4 = p.op("vector", lambda e: e.tensor_copy(dst_ap, psr[:, 0:n]), wait=[t3])
        part_sum.free = t4
        return t4
    part_sum.free = None

    flat512 = p.sb("flat512", [1, 512])
    lh2_dram = p.dtmp("lh2_dram", [1, 4])
    c1 = p.op("vector", lambda e: e.tensor_reduce(lohi[:, 0:2], ckv, AX.X, ALU.min), wait=[ck_done])
    c2 = p.op("vector", lambda e: e.tensor_reduce(lohi[:, 2:4], ckv, AX.X, ALU.max))
    d1 = p.dma("sync", lh_dram, lohi[:], wait=[c2], inc="lhd")
    d2 = p.dma("sync", flat512[0:1, :], lh_dram.rearrange("p c -> (p c)").unsqueeze(0),
               wait=[d1], inc="lhd")
    fv = flat512[0:1, :].rearrange("q (pp c) -> q c pp", c=4)
    c3 = p.op("vector", lambda e: e.tensor_reduce(lohi1[0:1, 0:2], fv[:, 0:2, :], AX.X, ALU.min),
              wait=[d2])
    c4 = p.op("vector", lambda e: e.tensor_reduce(lohi1[0:1, 2:4], fv[:, 2:4, :], AX.X, ALU.max))
    d3 = p.dma("sync", lh2_dram, lohi1[0:1, :], wait=[c4], inc="lhd")
    d4 = p.dma("sync", lohi[:], lh2_dram.broadcast_to([P, 4]), wait=[d3], inc="lhd")
    c5 = p.op("vector", lambda e: e.tensor_copy(lo[:], lohi[:, 0:2]), wait=[d4])
    c6 = p.op("vector", lambda e: e.tensor_copy(hi[:], lohi[:, 2:4]))
    ctok = c6
    for it in range(42):
        cm = p.op("vector", lambda e: e.tensor_tensor(mid[:], lo[:], hi[:], ALU.add), wait=[ctok])
        cm = p.op("vector", lambda e: e.tensor_scalar(mid[:], mid[:], 0.5, None, ALU.mult))
        cg = p.op("vector", lambda e: e.tensor_tensor(
            ge[:].rearrange("p (m b2) -> p b2 m", b2=2), ckv,
            mid[:].unsqueeze(2).broadcast_to([P, 2, 96]), ALU.is_ge))
        cr = p.op("vector", lambda e: e.tensor_reduce(rs[:], ge[:].rearrange("p (m b2) -> p b2 m", b2=2),
                                                      AX.X, ALU.add))
        cgp = part_sum(rs[:, 0:2], rs[:, 0:2], [cr])
        cp = p.op("vector", lambda e: e.tensor_scalar(pred[:], rs[:], 4096.0, None, ALU.is_ge), wait=[cgp])
        co = p.op("vector", lambda e: e.tensor_scalar(omp[:], pred[:], -1.0, 1.0, ALU.mult, ALU.add))
        u1 = p.op("vector", lambda e: e.tensor_tensor(tq[:], mid[:], lo[:], ALU.subtract))
        u2 = p.op("vector", lambda e: e.tensor_tensor(tq[:], tq[:], pred[:], ALU.mult))
        u3 = p.op("vector", lambda e: e.tensor_tensor(lo[:], lo[:], tq[:], ALU.add))
        u4 = p.op("vector", lambda e: e.tensor_tensor(tq[:], mid[:], hi[:], ALU.subtract))
        u5 = p.op("vector", lambda e: e.tensor_tensor(tq[:], tq[:], omp[:], ALU.mult))
        ctok = p.op("vector", lambda e: e.tensor_tensor(hi[:], hi[:], tq[:], ALU.add))
    cmask = p.op("vector", lambda e: e.tensor_tensor(
        ge[:].rearrange("p (m b2) -> p b2 m", b2=2), ckv,
        lo[:].unsqueeze(2).broadcast_to([P, 2, 96]), ALU.is_ge), wait=[ctok])
    comb = ck
    cc1 = p.op("vector", lambda e: e.tensor_tensor(comb[:], hist[:], ge[:], ALU.add), wait=[cmask, la])
    cc2 = p.op("vector", lambda e: e.tensor_single_scalar(comb[:], comb[:], 1.0, ALU.min))
    dif = p.sb("dif", [P, 192])
    cd1 = p.op("vector", lambda e: e.tensor_tensor(dif[:], comb[:], hist[:], ALU.subtract))
    cd2 = p.op("vector", lambda e: e.tensor_reduce(rs[:], dif[:].rearrange("p (m b2) -> p b2 m", b2=2),
                                                   AX.X, ALU.add))
    cd3 = part_sum(rs[:, 0:2], rs[:, 0:2], [cd2])
    cd4 = p.op("vector", lambda e: e.tensor_reduce(nov[:], rs[:], AX.X, ALU.add), wait=[cd3])
    cd5 = p.op("vector", lambda e: e.tensor_scalar(nov[:], nov[:], 1.0 / 8192.0, None, ALU.mult))
    cd6 = p.op("vector", lambda e: e.tensor_scalar(nov[:], nov[:], 0.7, None, ALU.is_lt))
    cdn = p.op("vector", lambda e: e.tensor_tensor(nov[:], done[:], nov[:], ALU.max), wait=[ldn])
    sd = p.dma("sync", done_out, nov[:], wait=[cdn], inc="std")
    co = p.op("vector", lambda e: e.tensor_scalar(omd[:], done[:], -1.0, 1.0, ALU.mult, ALU.add))
    b1 = p.op("vector", lambda e: e.tensor_scalar(act[:], act[:], done[:, 0:1], None, ALU.mult))
    b2 = p.op("vector", lambda e: e.scalar_tensor_tensor(act[:], ge[:], omd[:, 0:1], act[:],
                                                         ALU.mult, ALU.add), wait=[cmask])
    sa = p.dma("sync", act_out, act[:], wait=[b2], inc="sta")
    b3 = p.op("vector", lambda e: e.tensor_scalar(hist[:], hist[:], done[:, 0:1], None, ALU.mult))
    b4 = p.op("vector", lambda e: e.scalar_tensor_tensor(hist[:], comb[:], omd[:, 0:1], hist[:],
                                                         ALU.mult, ALU.add))
    sh2 = p.dma("sync", hist_out, hist[:], wait=[b4], inc="sth")
    p.barrier("sync", [sd, sa, sh2])
    return p.finalize()


def build_final():
    p = Prog("final")
    y1T_in = p.din("y1T_sl", [2048, TS])
    woT_in = p.din("wo1T_tiled", [P, 16, D])
    xn1_in = p.din("xn1_sl", [TS, D])
    x1_in = p.din("x1_sl", [TS, D])
    xc3_in = p.din("xc3_sl", [TS, D])
    xn3_in = p.din("xnew3_sl", [TS, D])
    done_in = p.din("done_rep", [P, 1])
    x2_out = p.dout("x2_sl", [TS, D])

    KT = 16
    y1Tb = p.sb("y1Tb", [P, KT * P], BF16)
    woTb = p.sb("woTb", [P, KT * D], BF16)
    xn1t = p.sb("xn1t", [P, D])
    x1t = p.sb("x1t", [P, D])
    xc3t = p.sb("xc3t", [P, D])
    xn3t = p.sb("xn3t", [P, D])
    s1t = p.sb("s1t", [P, D])
    gt = p.sb("gt", [P, D])
    done_sb = p.sb("done_sb", [P, 1])
    omd_sb = p.sb("omd_sb", [P, 1])
    pst = [p.ps(f"pst{i}", [P, 512]) for i in range(2)]

    w1 = None
    for q in range(2):
        w1 = p.dma("gpsimd", woTb[:].rearrange("p (k c) -> p k c", k=KT)[:, 8 * q:8 * q + 8],
                   woT_in[:, 8 * q:8 * q + 8], wait=[w1], inc="ldw")
    ld = p.dma("sync", done_sb[:], done_in, inc="ldd")
    cdn = p.op("vector", lambda e: e.tensor_scalar(omd_sb[:], done_sb[:], -1.0, 1.0,
                                                   ALU.mult, ALU.add), wait=[ld])
    y1T_v = y1T_in.rearrange("(k p) t -> p k t", p=P)
    ev_last = {0: None, 1: None}
    prev_mm = None
    prev_st = None
    for i in range(TS // P):
        t0 = i * P
        ly = p.dma("gpsimd", y1Tb[:].rearrange("p (k c) -> p k c", k=KT),
                   y1T_v[:, :, t0:t0 + P], wait=[prev_mm], inc="ldy")
        l1 = p.dma("sync", xn1t[:], xn1_in[t0:t0 + P, :], wait=[prev_st], inc="ldt")
        l2 = p.dma("sync", x1t[:], x1_in[t0:t0 + P, :], wait=[l1], inc="ldt")
        l3 = p.dma("sync", xc3t[:], xc3_in[t0:t0 + P, :], wait=[l2], inc="ldt")
        l4 = p.dma("sync", xn3t[:], xn3_in[t0:t0 + P, :], wait=[l3], inc="ldt")
        stoks = []
        for nch in range(2):
            pb = pst[nch]
            mms = [(y1Tb[:, k * P:(k + 1) * P], woTb[:, k * D + nch * 512:k * D + (nch + 1) * 512],
                    [ly, w1]) for k in range(KT)]
            tok = mm_group(p, pb[:], mms, psum_free=ev_last[nch])
            prev_mm = tok
            ev = p.op("vector", lambda e, pb=pb, nch=nch: e.tensor_tensor(
                s1t[:, nch * 512:(nch + 1) * 512], pb[:], xn1t[:, nch * 512:(nch + 1) * 512], ALU.add),
                wait=[tok, l4])
            ev_last[nch] = ev
            stoks.append(ev)
        p.op("vector", lambda e: e.tensor_scalar(xc3t[:], xc3t[:], done_sb[:, 0:1], None, ALU.mult),
             wait=[cdn, l4])
        cxf = p.op("vector", lambda e: e.scalar_tensor_tensor(
            xc3t[:], xn3t[:], omd_sb[:, 0:1], xc3t[:], ALU.mult, ALU.add))
        cg = p.op("scalar", lambda e: e.activation(gt[:], xc3t[:], AF.Silu), wait=[cxf, prev_st])
        p.op("vector", lambda e: e.tensor_tensor(gt[:], gt[:], s1t[:], ALU.mult), wait=[cg] + stoks)
        cx2 = p.op("vector", lambda e: e.tensor_tensor(gt[:], gt[:], x1t[:], ALU.add))
        prev_st = p.dma("sync", x2_out[t0:t0 + P, :], gt[:], wait=[cx2], inc="st")
    p.barrier("sync", [prev_st])
    return p.finalize()


_BUILD_CACHE = {}


def _get(name, builder, *a):
    key = (name,) + a
    if key not in _BUILD_CACHE:
        _BUILD_CACHE[key] = builder(*a)
    return _BUILD_CACHE[key]


def _tile3(w, kt):
    return np.ascontiguousarray(w.reshape(kt, P, -1).transpose(1, 0, 2))


def _pack32(v):
    return np.ascontiguousarray(np.asarray(v, F).reshape(-1, P).T)


def _packF2(v2):
    return np.ascontiguousarray(v2.reshape(2, 96, P).transpose(2, 1, 0).reshape(P, 192))


def kernel(**inputs):
    inp = {k: np.ascontiguousarray(np.asarray(v, dtype=F)) for k, v in inputs.items()}
    x = inp["L0_attn_x"].reshape(NTOK, D)
    R = list(range(NCORES))

    def run(nc, im):
        return run_bass_kernel_spmd(nc, im, core_ids=R).results

    def sl(c):
        return slice(c * TS, (c + 1) * TS)

    r = run(_get("prep", build_prep),
            [{"x_sl": x[sl(c)], "norm_w": inp["L0_attn_norm_w"][None, :]} for c in R])
    xn = np.concatenate([q["xn_sl"] for q in r], 0)
    h = np.concatenate([q["h_sl"] for q in r], 0)
    hT = np.ascontiguousarray(h.T)

    fold_in = stage_fold_inputs(inp["L0_ffn_up_weights"], inp["L0_ffn_down_weights"],
                                inp["L0_ffn_W_exp"], inp["L0_ffn_b_exp"], inp["L0_ffn_W_con"])
    r = run(_get("fold", build_fold), [fold_in] * NCORES)
    wexpfT, wconf = r[0]["wexpfT"], r[0]["wconf"]
    Wf = np.ascontiguousarray(wexpfT[:, :D].T)
    bf = np.ascontiguousarray(wexpfT[:, D])
    l0ffn_shared = {
        "wexpf_tiled": _tile3(Wf, 8), "wconf_tiled": _tile3(wconf, 32),
        "gbias1_packed": _pack32(bf), "gbias2_packed": _pack32(inp["L0_ffn_up_bias"]),
        "cbias2": np.ascontiguousarray(np.stack([inp["L0_ffn_b_con"], inp["L0_ffn_down_bias"]])),
    }

    r = run(_get("l0ffn", build_l0ffn),
            [dict(l0ffn_shared, xnT_sl=np.ascontiguousarray(xn[sl(c)].T)) for c in R])
    g0 = np.concatenate([q["g0_sl"] for q in r], 0)

    im = []
    for c in R:
        rows = slice(c * 8 * 16 * 17, (c + 1) * 8 * 16 * 17)
        waT = np.ascontiguousarray(inp["L0_attn_W_a"][rows].T)
        wvT = np.ascontiguousarray(inp["L0_attn_W_v"][c * P:(c + 1) * P].T)
        im.append({"hT_full": hT, "waT_tiled": _tile3(waT, 8), "wvT_tiled": _tile3(wvT, 8)})
    r = run(_get("l0attn", build_l0attn), im)
    y0 = np.stack([q["y0_sl"] for q in r])
    yfull = np.ascontiguousarray(
        y0.reshape(NCORES, 2, 8, 8, 1024, 16).transpose(0, 2, 5, 1, 3, 4).reshape(D, NTOK))

    woT_t = _tile3(np.ascontiguousarray(inp["L0_attn_W_out"].T), 8)
    im = [{"yT_sl": np.ascontiguousarray(yfull[:, sl(c)]), "woT_tiled": woT_t,
           "xn_sl": xn[sl(c)], "x_sl": x[sl(c)], "g0_sl": g0[sl(c)],
           "ln_w": inp["L1_attn_ln_w"][None, :], "ln_b": inp["L1_attn_ln_b"][None, :]}
          for c in R]
    r = run(_get("l0out", build_l0out), im)
    x1 = np.concatenate([q["x1_sl"] for q in r], 0)
    xn1 = np.concatenate([q["xn1_sl"] for q in r], 0)
    xn1n = np.concatenate([q["xn1n_sl"] for q in r], 0)
    xn1nT = np.ascontiguousarray(xn1n.T)

    im = []
    for c in R:
        rows = np.r_[256 * c:256 * c + 256, 2048 + 256 * c:2048 + 256 * c + 256]
        winT = np.ascontiguousarray(inp["L1_attn_w_in"][rows].T)
        wxT = np.ascontiguousarray(inp["L1_attn_w_xproj"][:, 256 * c:256 * c + 256].T)
        im.append({
            "xn1nT_full": xn1nT, "winT_tiled": _tile3(winT, 8),
            "convw_tiled": np.ascontiguousarray(
                inp["L1_attn_conv_w"][256 * c:256 * c + 256].reshape(2, P, 4).transpose(1, 0, 2)),
            "convb_tiled": np.ascontiguousarray(
                inp["L1_attn_conv_b"][256 * c:256 * c + 256].reshape(2, P).T),
            "wxT_tiled": np.ascontiguousarray(wxT.reshape(2, P, 96).transpose(1, 0, 2)),
        })
    r = run(_get("mamba1", build_mamba1), im)
    u_full = np.concatenate([q["u_sl"] for q in r], 0)
    z_full = np.concatenate([q["zsil_sl"] for q in r], 0)
    xdbl_parts = np.stack([q["xdbl_part"] for q in r])

    im = []
    for c in R:
        s2 = slice(256 * c, 256 * c + 256)
        im.append({
            "xdbl_parts": xdbl_parts,
            "wdtT_sl": np.ascontiguousarray(inp["L1_attn_w_dt"][s2].T),
            "bdt_tiled": np.ascontiguousarray(inp["L1_attn_b_dt"][s2].reshape(2, P).T),
            "alog_tiled": np.ascontiguousarray(
                inp["L1_attn_a_log"][s2].reshape(2, P, 16).transpose(1, 0, 2)),
            "dskip_tiled": np.ascontiguousarray(inp["L1_attn_d_skip"][s2].reshape(2, P).T),
            "u_sl": np.ascontiguousarray(u_full[s2]), "zsil_sl": np.ascontiguousarray(z_full[s2]),
        })
    r = run(_get("mamba2", build_mamba2), im)
    y1full = np.concatenate([q["y1_sl"] for q in r], 0)

    wmT = np.ascontiguousarray(inp["L1_ffn_Wm"].T)
    wpT = np.ascontiguousarray(inp["L1_ffn_Wp"].T)
    wc1T_t = np.ascontiguousarray(inp["L1_ffn_Wc1"].T.reshape(8, P, 256).transpose(1, 0, 2))
    wc2T_t = np.ascontiguousarray(inp["L1_ffn_Wc2"].T.reshape(2, P, 12288).transpose(1, 0, 2))
    base2 = np.zeros((2, 12288), F)
    base2[:, :4096] = 1.0
    active = _packF2(base2)
    history = _packF2(base2)
    zeros_rep = np.zeros((P, 1), F)
    done_for_blend = [zeros_rep, zeros_rep, None]
    done_ctl_in = zeros_rep
    xcT = {c: np.ascontiguousarray(xn1[sl(c)].T) for c in R}
    xnT = {}
    xcuT = {}
    for it in range(3):
        im = []
        for c in R:
            b = c // 4
            d = {"xcT_prev": xcT[c], "wmT_full": wmT, "wpT_full": wpT,
                 "mask_packed": np.ascontiguousarray(active[:, b::2])}
            if it > 0:
                d["xnewT_prev"] = xnT[c]
                d["done_rep"] = done_for_blend[it]
            im.append(d)
        r = run(_get("l1ffn_iter", build_l1ffn_iter, it == 0), im)
        for c in R:
            xnT[c] = r[c]["xnewT_sl"]
            if it > 0:
                xcuT[c] = r[c]["xcT_used"]
                xcT[c] = xcuT[c]
        if it == 2:
            break
        pooled_parts = np.stack([r[j]["pooled_part"] for j in R])
        cin = {"pooled_parts": pooled_parts, "wc1T_tiled": wc1T_t, "wc2T_tiled": wc2T_t,
               "history": history, "active": active, "done_rep": done_ctl_in}
        rc = run(_get("l1ffn_ctl", build_l1ffn_ctl), [cin] * NCORES)
        active = rc[0]["active_next"]
        history = rc[0]["history_next"]
        done_next = np.ascontiguousarray(rc[0]["done_next"])
        done_for_blend[2] = done_next if it == 0 else done_for_blend[2]
        if it == 1:
            done2 = done_next
        done_ctl_in = done_next

    wo1T_t = _tile3(np.ascontiguousarray(inp["L1_attn_w_out"].T), 16)
    im = [{"y1T_sl": np.ascontiguousarray(y1full[:, sl(c)]), "wo1T_tiled": wo1T_t,
           "xn1_sl": xn1[sl(c)], "x1_sl": x1[sl(c)],
           "xc3_sl": np.ascontiguousarray(xcuT[c].T), "xnew3_sl": np.ascontiguousarray(xnT[c].T),
           "done_rep": done2} for c in R]
    r = run(_get("final", build_final), im)
    x2 = np.concatenate([q["x2_sl"] for q in r], 0)
    return np.ascontiguousarray(x2.reshape(B, T, D))


# version 11 (final)
# speedup vs baseline: 1.1123x; 1.0162x over previous
import numpy as np
from contextlib import ExitStack

import concourse.bass as bass
import concourse.mybir as mybir
from concourse.bass_utils import run_bass_kernel_spmd

F32 = mybir.dt.float32
BF16 = mybir.dt.bfloat16
AF = mybir.ActivationFunctionType
ALU = mybir.AluOpType
AX = mybir.AxisListType

NCORES = 8
B, T, D = 2, 8192, 1024
NTOK = B * T
TS = NTOK // NCORES
P = 128
EPS = 1e-5

F = np.float32


class Prog:
    ENGINES = ("sync", "vector", "scalar", "gpsimd", "tensor")

    def __init__(self, name):
        self.name = name
        self.nc = bass.Bass("TRN2", target_bir_lowering=False, num_swdge_queues=2)
        self.ops = {e: [] for e in self.ENGINES}
        self.sem_counts = {}
        self.sem_names = []
        self.ctx = ExitStack()
        self._finalized = False

    def din(self, name, shape, dtype=F32):
        return self.nc.dram_tensor(name, list(shape), dtype, kind="ExternalInput").ap()

    def dout(self, name, shape, dtype=F32):
        return self.nc.dram_tensor(name, list(shape), dtype, kind="ExternalOutput").ap()

    def dtmp(self, name, shape, dtype=F32):
        return self.nc.dram_tensor(name, list(shape), dtype, kind="Internal").ap()

    def sb(self, name, shape, dtype=F32):
        return self.ctx.enter_context(self.nc.sbuf_tensor(name, list(shape), dtype))

    def ps(self, name, shape, dtype=F32):
        return self.ctx.enter_context(self.nc.psum_tensor(name, list(shape), dtype))

    def sem(self, name):
        if name not in self.sem_counts:
            self.sem_counts[name] = 0
            self.sem_names.append(name)
        return name

    CHAINED = ("vector", "scalar", "gpsimd")

    def op(self, eng, fn, wait=(), inc=None, inc_n=1, chain=True):
        _w = {}
        for w in wait:
            if w is not None:
                _w[w[0]] = max(_w.get(w[0], 0), w[1])
        wait = sorted(_w.items())
        if chain and eng in self.CHAINED:
            assert inc is None, "chained engines use their chain sem as milestone"
            csem = self.sem("c_" + eng)
            wait.append((csem, self.sem_counts[csem]))
            self.sem_counts[csem] += 1
            tok = (csem, self.sem_counts[csem])
            self.ops[eng].append((fn, tuple(wait), (csem, 1)))
            return tok
        if inc is not None:
            self.sem(inc)
            self.sem_counts[inc] += inc_n
            tok = (inc, self.sem_counts[inc])
            self.ops[eng].append((fn, tuple(wait), (inc, inc_n)))
            return tok
        self.ops[eng].append((fn, tuple(wait), None))
        return None

    def barrier(self, eng, wait):
        self.op(eng, lambda e: None, wait=wait, chain=False)

    def dma(self, eng, dst, src, wait=(), inc="ld"):
        return self.op(eng, lambda e: e.dma_start(dst, src), wait=wait, inc=inc,
                       inc_n=16, chain=False)

    def count(self, sem):
        return self.sem_counts.get(sem, 0)

    def finalize(self):
        assert not self._finalized
        self._finalized = True
        nc = self.nc
        semh = {n: self.ctx.enter_context(nc.semaphore(n)) for n in self.sem_names}
        block = self.ctx.enter_context(nc.Block())

        def emit(engine_obj, eng_name):
            for fn, waits, inc in self.ops[eng_name]:
                for (sname, cnt) in waits:
                    if cnt > 0:
                        engine_obj.wait_ge(semh[sname], cnt)
                inst = fn(engine_obj)
                if inc is not None:
                    assert inst is not None, f"op with inc on {eng_name} returned None"
                    inst.then_inc(semh[inc[0]], inc[1])

        if self.ops["sync"]:
            @block.sync
            def _(sync):
                emit(sync, "sync")
        if self.ops["vector"]:
            @block.vector
            def _(vector):
                emit(vector, "vector")
        if self.ops["scalar"]:
            @block.scalar
            def _(scalar):
                emit(scalar, "scalar")
        if self.ops["gpsimd"]:
            @block.gpsimd
            def _(gpsimd):
                emit(gpsimd, "gpsimd")
        if self.ops["tensor"]:
            @block.tensor
            def _(tensor):
                emit(tensor, "tensor")
        self.ctx.close()
        return nc


def emit_ln(p, x_t, out_t, scr, nrows=P, ncols=D, ld_wait=(), gain_bias=None, eps=EPS):
    p.op("vector", lambda e: e.tensor_reduce(scr["sum1"][:nrows], x_t[:nrows], AX.X, ALU.add), wait=ld_wait)
    p.op("vector", lambda e: e.tensor_scalar(scr["nm"][:nrows], scr["sum1"][:nrows], -1.0 / ncols, None, ALU.mult))
    c = p.op("vector", lambda e: e.tensor_scalar(scr["xm"][:nrows], x_t[:nrows], scr["nm"][:nrows], None, ALU.add))
    cs = p.op("scalar", lambda e: e.activation(scr["sq"][:nrows], scr["xm"][:nrows], AF.Square, accum_out=scr["rr"][:nrows]),
              wait=[c])
    cv = p.op("vector", lambda e: e.tensor_scalar(scr["rr"][:nrows], scr["rr"][:nrows], 1.0 / ncols, eps, ALU.mult, ALU.add),
              wait=[cs])
    cs = p.op("scalar", lambda e: e.activation(scr["sd"][:nrows], scr["rr"][:nrows], AF.Sqrt), wait=[cv])
    p.op("vector", lambda e: e.reciprocal(scr["rstd"][:nrows], scr["sd"][:nrows]), wait=[cs])
    if gain_bias is None:
        return p.op("vector", lambda e: e.tensor_scalar(out_t[:nrows], scr["xm"][:nrows], scr["rstd"][:nrows], None, ALU.mult))
    gain, bias = gain_bias
    p.op("vector", lambda e: e.tensor_scalar(scr["xm"][:nrows], scr["xm"][:nrows], scr["rstd"][:nrows], None, ALU.mult))
    p.op("vector", lambda e: e.tensor_tensor(scr["xm"][:nrows], scr["xm"][:nrows], gain[:nrows], ALU.mult))
    return p.op("vector", lambda e: e.tensor_tensor(out_t[:nrows], scr["xm"][:nrows], bias[:nrows], ALU.add))


def build_prep():
    p = Prog("prep")
    x_in = p.din("x_sl", [TS, D])
    nw_in = p.din("norm_w", [1, D])
    xn_out = p.dout("xn_sl", [TS, D])
    h_out = p.dout("h_sl", [TS, D])

    ntiles = TS // P
    xt = [p.sb(f"xt{i}", [P, D]) for i in range(2)]
    xnt = [p.sb(f"xnt{i}", [P, D]) for i in range(2)]
    htb = [p.sb(f"ht{i}", [P, D]) for i in range(2)]
    nwb = p.sb("nwb", [P, D])

    def mkscr(tag):
        s = {k: p.sb(f"scr{tag}_" + k, [P, D if k in ("sq", "xm", "xmr", "xg") else 1])
             for k in ("sum1", "nm", "rr", "sd", "rstd", "sq", "xm", "xmr", "xg")}
        s.update({"rr2": p.sb(f"rr2{tag}", [P, 1]), "sd2": p.sb(f"sd2{tag}", [P, 1]),
                  "rstd2": p.sb(f"rstd2{tag}", [P, 1]), "sq2": p.sb(f"sq2{tag}", [P, D])})
        return s

    scrs = [mkscr("a"), mkscr("b")]
    cnw = p.dma("sync", nwb[:], nw_in.broadcast_to([P, D]))
    prev = {}
    fin = []
    for i in range(ntiles):
        par = i % 2
        buf, xn_b, h_b, scr = xt[par], xnt[par], htb[par], scrs[par]
        cl = p.dma("sync", buf[:], x_in[i * P:(i + 1) * P, :],
                   wait=[prev.get(("c1", par)), prev.get(("c2", par))], inc="ld%d" % par)
        cxn = emit_ln(p, buf, xn_b, scr, ld_wait=[cl, cnw])
        cs = p.op("scalar", lambda e, scr=scr, xn_b=xn_b: e.activation(
            scr["sq2"][:], xn_b[:], AF.Square, accum_out=scr["rr2"][:]), wait=[cxn])
        cv = p.op("vector", lambda e, scr=scr: e.tensor_scalar(
            scr["rr2"][:], scr["rr2"][:], 1.0 / D, EPS, ALU.mult, ALU.add), wait=[cs])
        cs = p.op("scalar", lambda e, scr=scr: e.activation(scr["sd2"][:], scr["rr2"][:], AF.Sqrt),
                  wait=[cv])
        p.op("vector", lambda e, scr=scr: e.reciprocal(scr["rstd2"][:], scr["sd2"][:]), wait=[cs])
        ch = p.op("vector", lambda e, scr=scr, xn_b=xn_b, h_b=h_b: e.scalar_tensor_tensor(
            h_b[:], xn_b[:], scr["rstd2"][:], nwb[:], ALU.mult, ALU.mult))
        c1 = p.dma("sync", xn_out[i * P:(i + 1) * P, :], xn_b[:], wait=[cxn], inc="st1%d" % par)
        c2 = p.dma("gpsimd", h_out[i * P:(i + 1) * P, :], h_b[:], wait=[ch], inc="st2%d" % par)
        prev[("c1", par)], prev[("c2", par)] = c1, c2
        fin = [c1, c2] if i >= ntiles - 2 else fin + [c1, c2]
    p.barrier("sync", [prev[("c1", 0)], prev[("c2", 0)], prev[("c1", 1)], prev[("c2", 1)]])
    return p.finalize()


def mm_group(p, psum_ap, mms, psum_free=None):
    n = len(mms)
    tok = None
    for i, (l, r, w) in enumerate(mms):
        w = list(w)
        if i == 0 and psum_free is not None:
            w.append(psum_free)
        tok = p.op("tensor",
                   (lambda e, l=l, r=r, s=(i == 0), t=(i == n - 1):
                    e.matmul(psum_ap, l, r, start=s, stop=t)),
                   wait=w, inc="pe" if i == n - 1 else None, chain=False)
    return tok


def _butterfly_stage_blocks(weights):
    w = np.asarray(weights, dtype=F)
    lo = np.zeros((6, 64, 64, 64), F)
    hi = np.zeros((6, 64, 64, 64), F)
    for k in range(12):
        s = 2 ** k
        nb = 4096 // (2 * s)
        wk = w[k].reshape(nb, s, 2, 2)
        b_idx = np.arange(nb)[:, None]
        pos = np.arange(s)[None, :]
        i0 = b_idx * 2 * s + pos
        i1 = i0 + s
        if k < 6:
            o = i0 // 64
            n0, n1 = i0 % 64, i1 % 64
            lo[k, o, n0, n0] = wk[..., 0, 0]
            lo[k, o, n0, n1] = wk[..., 0, 1]
            lo[k, o, n1, n0] = wk[..., 1, 0]
            lo[k, o, n1, n1] = wk[..., 1, 1]
        else:
            n = i0 % 64
            o0, o1 = i0 // 64, i1 // 64
            hi[k - 6, n, o0, o0] = wk[..., 0, 0]
            hi[k - 6, n, o0, o1] = wk[..., 0, 1]
            hi[k - 6, n, o1, o0] = wk[..., 1, 0]
            hi[k - 6, n, o1, o1] = wk[..., 1, 1]
    return lo, hi


def _pair_blockdiag(blocks):
    t = np.zeros((32, 128, 128), F)
    t[:, :64, :64] = blocks[0::2]
    t[:, 64:, 64:] = blocks[1::2]
    return t


def stage_fold_inputs(up_weights, down_weights, W_exp, b_exp, W_con):
    lo_u, hi_u = _butterfly_stage_blocks(up_weights)
    lo_d, hi_d = _butterfly_stage_blocks(down_weights)
    tr = lambda b: np.ascontiguousarray(np.transpose(b, (0, 2, 1)))
    ch = np.zeros((4, 6, 32, 128, 128), F)
    ch[0, 0] = _pair_blockdiag(tr(lo_u[5]))
    for j in range(1, 6):
        ch[0, j] = _pair_blockdiag(lo_u[5 - j])
    ch[1, 0] = _pair_blockdiag(tr(hi_u[5]))
    for j in range(1, 6):
        ch[1, j] = _pair_blockdiag(hi_u[5 - j])
    ch[2, 0] = _pair_blockdiag(hi_d[0])
    for j in range(1, 6):
        ch[2, j] = _pair_blockdiag(tr(hi_d[j]))
    ch[3, 0] = _pair_blockdiag(lo_d[0])
    for j in range(1, 6):
        ch[3, j] = _pair_blockdiag(tr(lo_d[j]))

    wexpT_ext = np.zeros((4096, 1032), F)
    wexpT_ext[:, :D] = np.asarray(W_exp, F).T
    wexpT_ext[:, D] = np.asarray(b_exp, F)
    wcon = np.asarray(W_con, F)
    rows = (np.arange(4096).reshape(64, 64).T).reshape(-1)
    wcon_perm = np.ascontiguousarray(wcon[rows])
    return {"fold_ch": ch, "wexpT_ext": wexpT_ext, "wcon_perm": wcon_perm}


def build_fold():
    p = Prog("fold")
    ch_in = p.din("fold_ch", [4, 6, 32, 128, 128])
    wx_in = p.din("wexpT_ext", [4096, 1032])
    wc_in = p.din("wcon_perm", [4096, 1024])
    wxf_out = p.dout("wexpfT", [4096, 1032])
    wcf_out = p.dout("wconf", [4096, 1024])
    ftmp = p.dtmp("ftmp", [4096, 1032])

    chain_sb = p.sb("chain_sb", [P, 4 * 32 * P])
    stg = p.sb("stg", [P, 6 * P])
    src = [p.sb(f"src{i}", [P, 1032]) for i in range(2)]
    osb = [p.sb(f"osb{i}", [P, 1032]) for i in range(2)]
    pst = [p.ps(f"pst{i}", [P, 512]) for i in range(2)]

    ev_last = {0: None, 1: None}
    for c in range(4):
        for t in range(32):
            slot = chain_sb[:, (c * 32 + t) * P:(c * 32 + t + 1) * P]
            cl = p.dma("sync", stg[:].rearrange("r (j k) -> r j k", j=6),
                       ch_in[c, :, t].transpose([1, 0, 2]))
            cseed = p.op("vector", lambda e, slot=slot: e.tensor_copy(slot, stg[:, 0:P]), wait=[cl])
            for j in range(1, 6):
                pb = pst[j % 2]
                tok = mm_group(p, pb[:, 0:P],
                               [(stg[:, j * P:(j + 1) * P], slot, [cl, cseed])],
                               psum_free=ev_last[j % 2])
                cseed = p.op("vector", lambda e, slot=slot, pb=pb: e.tensor_copy(slot, pb[:, 0:P]), wait=[tok])
                ev_last[j % 2] = cseed
            p.barrier("sync", [tok])
    chains_done = cseed

    passes = [
        (wx_in, 0, 1032, ftmp, True),
        (ftmp, 1, 1032, wxf_out, True),
        (wc_in, 2, 1024, ftmp, True),
        (ftmp, 3, 1024, wcf_out, False),
    ]
    prev_pass_toks = [chains_done]
    for (src_ap, c, ncols, dst_ap, perm) in passes:
        tile_toks = []
        for t in range(32):
            sbuf_src = src[t % 2]
            sbuf_out = osb[t % 2]
            cl = p.dma("sync", sbuf_src[:, :ncols], src_ap[t * P:(t + 1) * P, :ncols],
                       wait=list(prev_pass_toks) + ([tile_toks[-2]] if len(tile_toks) >= 2 else []))
            stat = chain_sb[:, (c * 32 + t) * P:(c * 32 + t + 1) * P]
            war = [tile_toks[-2]] if len(tile_toks) >= 2 else []
            ev = None
            nch = (ncols + 511) // 512
            for q in range(nch):
                n0, n1 = q * 512, min((q + 1) * 512, ncols)
                pb = pst[q % 2]
                tok = mm_group(p, pb[:, 0:n1 - n0],
                               [(stat, sbuf_src[:, n0:n1], [cl, chains_done])],
                               psum_free=ev_last[q % 2])
                ev = p.op("vector", lambda e, pb=pb, sbuf_out=sbuf_out, n0=n0, n1=n1:
                          e.tensor_copy(sbuf_out[:, n0:n1], pb[:, 0:n1 - n0]),
                          wait=[tok] + war)
                ev_last[q % 2] = ev
            stsem = "st%d" % (t % 2)
            if perm:
                dst_v = dst_ap[:, 0:ncols].rearrange("(b a) c -> b a c", a=64)
                p.dma("sync", dst_v[:, 2 * t, :], sbuf_out[0:64, :ncols], wait=[ev], inc=stsem)
                stok = p.dma("sync", dst_v[:, 2 * t + 1, :], sbuf_out[64:128, :ncols],
                             wait=[ev], inc=stsem)
            else:
                stok = p.dma("sync", dst_ap[t * P:(t + 1) * P, :ncols], sbuf_out[:, :ncols],
                             wait=[ev], inc=stsem)
            tile_toks.append(stok)
        prev_pass_toks = [tile_toks[-1], tile_toks[-2]]
    p.barrier("sync", prev_pass_toks)
    return p.finalize()


def build_l0ffn():
    p = Prog("l0ffn")
    xnT_in = p.din("xnT_sl", [D, TS])
    wexpf_in = p.din("wexpf_tiled", [P, 8, 4096])
    wconf_in = p.din("wconf_tiled", [P, 32, D])
    gb1_in = p.din("gbias1_packed", [P, 32])
    gb2_in = p.din("gbias2_packed", [P, 32])
    cb_in = p.din("cbias2", [2, D])
    g0_out = p.dout("g0_sl", [TS, D])

    KT, MT = 8, 32
    xnTb_all = p.sb("xnTb_all", [P, KT * 512], BF16)
    wexpfb_all = p.sb("wexpfb_all", [P, KT * 4096], BF16)
    wconfb_all = p.sb("wconfb_all", [P, MT * D], BF16)
    xnTb = [xnTb_all[:, k * 512:(k + 1) * 512] for k in range(KT)]
    wexpfb = [wexpfb_all[:, k * 4096:(k + 1) * 4096] for k in range(KT)]
    wconfb = [wconfb_all[:, m * D:(m + 1) * D] for m in range(MT)]
    Hb = p.sb("Hb", [P, MT * 512], BF16)
    gbias = p.sb("gbias", [P, 32])
    cbias = p.sb("cbias", [P, D])
    cb_tmp = p.sb("cb_tmp", [P, D])
    ysb = p.sb("ysb", [P, D])
    gsb = p.sb("gsb", [P, D])
    pst = [p.ps(f"pst{i}", [P, 512]) for i in range(2)]

    w1 = None
    for q in range(4):
        w1 = p.dma("gpsimd", wexpfb_all[:].rearrange("p (k c) -> p k c", k=KT)[:, 2 * q:2 * q + 2],
                   wexpf_in[:, 2 * q:2 * q + 2], wait=[w1], inc="ldw1")
    w2 = None
    for q in range(4):
        w2 = p.dma("gpsimd", wconfb_all[:].rearrange("p (m c) -> p m c", m=MT)[:, 8 * q:8 * q + 8],
                   wconf_in[:, 8 * q:8 * q + 8], wait=[w2], inc="ldw2")
    b1 = p.dma("sync", gbias[:], gb1_in, inc="ld")
    b2 = p.dma("sync", cb_tmp[:, 0:32], gb2_in, wait=[b1], inc="ld")
    gb_tok = p.op("vector", lambda e: e.tensor_tensor(gbias[:], gbias[:], cb_tmp[:, 0:32], ALU.add),
                  wait=[b2])
    b3 = p.dma("sync", cbias[:], cb_in[0:1, :].broadcast_to([P, D]), wait=[b2], inc="ld")
    cld = p.dma("sync", cb_tmp[:], cb_in[1:2, :].broadcast_to([P, D]), wait=[gb_tok, b3], inc="ld")
    cbias_tok = p.op("vector", lambda e: e.tensor_tensor(cbias[:], cbias[:], cb_tmp[:], ALU.add),
                     wait=[cld])

    ev_last = {0: None, 1: None}
    prev_y_tok = None
    last_sil = None
    last_st = None
    for g in range(4):
        xg_tok = p.dma("gpsimd",
                       xnTb_all[:].rearrange("p (k c) -> p k c", k=KT),
                       xnT_in.rearrange("(k p) t -> p k t", p=P)[:, :, g * 512:(g + 1) * 512],
                       wait=[prev_y_tok], inc="ldx")
        h_done = None
        for m in range(MT):
            pb = pst[m % 2]
            mms = [(wexpfb[k][:, m * P:(m + 1) * P], xnTb[k], [xg_tok, w1]) for k in range(KT)]
            tok = mm_group(p, pb[:], mms, psum_free=ev_last[m % 2])
            h_done = p.op("scalar", lambda e, pb=pb, m=m: e.activation(
                Hb[:, m * 512:(m + 1) * 512], pb[:], AF.Gelu, bias=gbias[:, m:m + 1]),
                wait=[tok, gb_tok])
            ev_last[m % 2] = h_done
        for tsub in range(4):
            ytoks = []
            for nch in range(2):
                pb = pst[nch]
                mms = [(Hb[:, m * 512 + tsub * P: m * 512 + (tsub + 1) * P],
                        wconfb[m][:, nch * 512:(nch + 1) * 512], [h_done, w2]) for m in range(MT)]
                tok = mm_group(p, pb[:], mms, psum_free=ev_last[nch])
                ytok = p.op("vector", lambda e, pb=pb, nch=nch: e.tensor_tensor(
                    ysb[:, nch * 512:(nch + 1) * 512], pb[:], cbias[:, nch * 512:(nch + 1) * 512], ALU.add),
                    wait=[tok, cbias_tok, last_sil])
                ev_last[nch] = ytok
                ytoks.append(ytok)
            last_sil = p.op("scalar", lambda e: e.activation(gsb[:], ysb[:], AF.Silu),
                            wait=ytoks + [last_st])
            row0 = g * 512 + tsub * P
            last_st = p.dma("sync", g0_out[row0:row0 + P, :], gsb[:], wait=[last_sil], inc="st")
            prev_y_tok = tok
    p.barrier("sync", [last_st])
    return p.finalize()


def build_l0attn():
    p = Prog("l0attn")
    hT_in = p.din("hT_full", [D, NTOK])
    waT_in = p.din("waT_tiled", [P, 8, 2176])
    wvT_in = p.din("wvT_tiled", [P, 8, P])
    y_out = p.dout("y0_sl", [P, 1024, 16])
    A_hbm = p.dtmp("A_hbm", [P, 1024, 272])

    waTb = p.sb("waTb", [P, 8 * 2176], BF16)
    wvTb = p.sb("wvTb", [P, 8 * P], BF16)
    hTb = [p.sb(f"hTb{i}", [P, 8 * P], BF16) for i in range(2)]
    gates = p.sb("gates", [P, 2176])
    red = p.sb("red", [P, P])
    v_sb = p.sb("v_sb", [P, P])
    ps_g = [p.ps(f"ps_g{i}", [P, 512]) for i in range(2)]
    ps_v = p.ps("ps_v", [P, P])

    w1 = None
    for q in range(2):
        w1 = p.dma("gpsimd", waTb[:].rearrange("p (k c) -> p k c", k=8)[:, 4 * q:4 * q + 4],
                   waT_in[:, 4 * q:4 * q + 4], wait=[w1], inc="ldw1")
    w2 = p.dma("gpsimd", wvTb[:].rearrange("p (k c) -> p k c", k=8), wvT_in, inc="ldw2")

    ev_last = {0: None, 1: None}
    prev = {}
    last_store = None
    hT_v = hT_in.rearrange("(k p) t -> p k t", p=P)
    for b in range(2):
        for tt in range(64):
            t0 = b * T + tt * P
            hbuf = hTb[tt % 2]
            lx = p.dma("gpsimd", hbuf[:].rearrange("p (k c) -> p k c", k=8),
                       hT_v[:, :, t0:t0 + P], wait=[prev.get(("mmv", tt % 2))],
                       inc="ldx%d" % (tt % 2))
            etoks = []
            for q in range(5):
                n0, n1 = q * 512, min((q + 1) * 512, 2176)
                pb = ps_g[q % 2]
                mms = [(hbuf[:, k * P:(k + 1) * P], waTb[:, k * 2176 + n0:k * 2176 + n1], [lx, w1])
                       for k in range(8)]
                tok = mm_group(p, pb[:, 0:n1 - n0], mms, psum_free=ev_last[q % 2])
                ev = p.op("scalar", lambda e, pb=pb, n0=n0, n1=n1: e.activation(
                    gates[:, n0:n1], pb[:, 0:n1 - n0], AF.Exp),
                    wait=[tok, prev.get("gates_free")])
                etoks.append(ev)
                ev_last[q % 2] = ev
            mms = [(hbuf[:, k * P:(k + 1) * P], wvTb[:, k * P:(k + 1) * P], [lx, w2])
                   for k in range(8)]
            vtok = mm_group(p, ps_v[:], mms, psum_free=prev.get("v_free"))
            prev[("mmv", tt % 2)] = vtok
            vev = p.op("vector", lambda e: e.tensor_copy(v_sb[:], ps_v[:]),
                       wait=[vtok, prev.get("u_done")])
            prev["v_free"] = vev
            c1 = p.op("vector", lambda e: e.tensor_reduce(
                red[:], gates[:].rearrange("p (g j) -> p g j", j=17), AX.X, ALU.add),
                wait=etoks)
            c2 = p.op("vector", lambda e: e.reciprocal(red[:], red[:]))
            c3 = p.op("vector", lambda e: e.tensor_tensor(
                gates[:].rearrange("p (g j) -> p g j", j=17), gates[:].rearrange("p (g j) -> p g j", j=17),
                red[:].unsqueeze(2).broadcast_to([P, P, 17]), ALU.mult))
            a0v = gates[:].rearrange("p (g j) -> p g j", j=17)[:, :, 0]
            u_done = p.op("vector", lambda e: e.tensor_tensor(a0v, a0v, v_sb[:], ALU.mult),
                          wait=[vev])
            prev["u_done"] = u_done
            chunk = tt // 8
            j0 = (tt % 8) * P
            dst = A_hbm[b * 64 + chunk: b * 64 + chunk + 57: 8, j0:j0 + P, :].transpose([1, 0, 2])
            last_store = p.dma("sync", dst, gates[:].rearrange("p (g c) -> p g c", g=8),
                               wait=[u_done, c3], inc="stA%d" % (tt % 2))
            prev["gates_free"] = last_store
            prev[("stA", tt % 2)] = last_store
    p1_done = [prev[("stA", 0)], prev[("stA", 1)]]

    JB = 32
    NB = 1024 // JB
    Ablk = [p.sb(f"Ablk{i}", [P, JB * 272]) for i in range(2)]
    yloc = p.sb("yloc", [P, 1025 * 17])
    tmp = p.sb("tmp", [P, 272])
    cbuf = p.sb("cbuf", [P, 2 * 17])
    initz = p.sb("initz", [P, 17])

    yv = yloc[:].rearrange("p (j s) -> p j s", s=17)
    m0 = p.op("vector", lambda e: e.memset(yloc[:], 0.0))
    m1 = p.op("vector", lambda e: e.memset(yv[:, :, 0:1], 1.0))

    def scan_pass(homog, extra_wait):
        toks = []
        lbt = {0: None, 1: None}
        for jb in range(NB):
            buf = Ablk[jb % 2]
            lw = list(p1_done) + list(extra_wait)
            if lbt[jb % 2] is not None:
                lw.append(lbt[jb % 2])
            la = p.dma("sync", buf[:].rearrange("p (j c) -> p j c", j=JB),
                       A_hbm[:, jb * JB:(jb + 1) * JB, :],
                       wait=lw, inc="ldA%d_%d" % (jb % 2, int(homog)))
            for jj in range(JB):
                j = jb * JB + jj
                asl = buf[:, jj * 272:(jj + 1) * 272]
                av = asl.rearrange("p (i s) -> p i s", s=17)
                if not homog:
                    state = yv[:, j, :]
                    p.op("vector", lambda e, av=av, state=state: e.tensor_tensor(
                        tmp[:].rearrange("p (i s) -> p i s", s=17), av,
                        state.unsqueeze(1).broadcast_to([P, 16, 17]), ALU.mult),
                        wait=[la, m0, m1])
                    tok = p.op("vector", lambda e, j=j: e.tensor_reduce(
                        yv[:, j + 1, 1:17], tmp[:].rearrange("p (i s) -> p i s", s=17),
                        AX.X, ALU.add))
                else:
                    cst = cbuf[:, (j % 2) * 17:(j % 2) * 17 + 17]
                    cdst = cbuf[:, ((j + 1) % 2) * 17 + 1:((j + 1) % 2) * 17 + 17]
                    p.op("vector", lambda e, av=av, cst=cst: e.tensor_tensor(
                        tmp[:].rearrange("p (i s) -> p i s", s=17), av,
                        cst.unsqueeze(1).broadcast_to([P, 16, 17]), ALU.mult),
                        wait=[la])
                    p.op("vector", lambda e, cdst=cdst: e.tensor_reduce(
                        cdst, tmp[:].rearrange("p (i s) -> p i s", s=17), AX.X, ALU.add))
                    tok = p.op("vector", lambda e, j=j: e.tensor_tensor(
                        yv[:, j + 1, 1:17], yv[:, j + 1, 1:17],
                        cbuf[:, ((j + 1) % 2) * 17 + 1:((j + 1) % 2) * 17 + 17], ALU.add))
            lbt[jb % 2] = tok
            toks.append(tok)
        return toks[-1]

    pass1_done = scan_pass(False, [])
    mz = p.op("vector", lambda e: e.memset(initz[:], 0.0), wait=[pass1_done])
    sh = []
    for q in range(16):
        sh.append(p.dma("sync", initz[q * 8 + 1:q * 8 + 8, 1:17], yv[q * 8:q * 8 + 7, 1024, 1:17],
                        wait=[mz], inc="shf%d" % q))
    cseed = p.op("vector", lambda e: e.memset(cbuf[:], 0.0), wait=[pass1_done])
    cseed = p.op("vector", lambda e: e.tensor_copy(cbuf[:, 1:17], initz[:, 1:17]), wait=sh + [cseed])
    pass2_done = scan_pass(True, [cseed, pass1_done])
    st1 = p.dma("sync", y_out[:, 0:512, :], yv[:, 1:513, 1:17], wait=[pass2_done], inc="sty")
    stok = p.dma("sync", y_out[:, 512:1024, :], yv[:, 513:1025, 1:17], wait=[st1], inc="sty")
    p.barrier("sync", [stok])
    return p.finalize()


def build_l0out():
    p = Prog("l0out")
    yT_in = p.din("yT_sl", [D, TS])
    woT_in = p.din("woT_tiled", [P, 8, D])
    xn_in = p.din("xn_sl", [TS, D])
    x_in = p.din("x_sl", [TS, D])
    g0_in = p.din("g0_sl", [TS, D])
    lnw_in = p.din("ln_w", [1, D])
    lnb_in = p.din("ln_b", [1, D])
    x1_out = p.dout("x1_sl", [TS, D])
    xn1_out = p.dout("xn1_sl", [TS, D])
    xn1n_out = p.dout("xn1n_sl", [TS, D])

    KT = 8
    yTb = p.sb("yTb", [P, KT * P], BF16)
    woTb = p.sb("woTb", [P, KT * D], BF16)
    xnt = p.sb("xnt", [P, D])
    xt = p.sb("xt", [P, D])
    g0t = p.sb("g0t", [P, D])
    s0t = p.sb("s0t", [P, D])
    x1t = p.sb("x1t", [P, D])
    xn1t = p.sb("xn1t", [P, D])
    xn1nt = p.sb("xn1nt", [P, D])
    lnwb = p.sb("lnwb", [P, D])
    lnbb = p.sb("lnbb", [P, D])
    scr = {k: p.sb("scr_" + k, [P, D if k in ("sq", "xm", "xmr", "xg") else 1])
           for k in ("sum1", "nm", "rr", "sd", "rstd", "sq", "xm", "xmr", "xg")}
    pst = [p.ps(f"pst{i}", [P, 512]) for i in range(2)]

    w1 = p.dma("gpsimd", woTb[:].rearrange("p (k c) -> p k c", k=KT), woT_in, inc="ldw")
    lw1 = p.dma("sync", lnwb[:], lnw_in.broadcast_to([P, D]), inc="ldc")
    lw2 = p.dma("sync", lnbb[:], lnb_in.broadcast_to([P, D]), wait=[lw1], inc="ldc")

    yT_v = yT_in.rearrange("(k p) t -> p k t", p=P)
    ev_last = {0: None, 1: None}
    prev_st = None
    prev_mm = None
    for i in range(TS // P):
        t0 = i * P
        ly = p.dma("gpsimd", yTb[:].rearrange("p (k c) -> p k c", k=KT)[:, :, 0:P],
                   yT_v[:, :, t0:t0 + P], wait=[prev_mm], inc="ldy")
        l1 = p.dma("sync", xnt[:], xn_in[t0:t0 + P, :], wait=[prev_st], inc="ldt")
        l2 = p.dma("sync", xt[:], x_in[t0:t0 + P, :], wait=[l1], inc="ldt")
        l3 = p.dma("sync", g0t[:], g0_in[t0:t0 + P, :], wait=[l2], inc="ldt")
        stoks = []
        for nch in range(2):
            pb = pst[nch]
            mms = [(yTb[:, k * P:(k + 1) * P], woTb[:, k * D + nch * 512:k * D + (nch + 1) * 512],
                    [ly, w1]) for k in range(KT)]
            tok = mm_group(p, pb[:], mms, psum_free=ev_last[nch])
            prev_mm = tok
            ev = p.op("vector", lambda e, pb=pb, nch=nch: e.tensor_tensor(
                s0t[:, nch * 512:(nch + 1) * 512], pb[:], xnt[:, nch * 512:(nch + 1) * 512], ALU.add),
                wait=[tok, l3])
            ev_last[nch] = ev
            stoks.append(ev)
        p.op("vector", lambda e: e.tensor_tensor(x1t[:], s0t[:], g0t[:], ALU.mult), wait=stoks)
        cx1 = p.op("vector", lambda e: e.tensor_tensor(x1t[:], x1t[:], xt[:], ALU.add))
        cxn1 = emit_ln(p, x1t, xn1t, scr)
        cxn1n = emit_ln(p, xn1t, xn1nt, scr, ld_wait=[lw2], gain_bias=(lnwb, lnbb))
        s1 = p.dma("sync", x1_out[t0:t0 + P, :], x1t[:], wait=[cx1], inc="st")
        s2 = p.dma("sync", xn1_out[t0:t0 + P, :], xn1t[:], wait=[cxn1, s1], inc="st")
        prev_st = p.dma("sync", xn1n_out[t0:t0 + P, :], xn1nt[:], wait=[cxn1n, s2], inc="st")
    p.barrier("sync", [prev_st])
    return p.finalize()


def build_mamba1(debug_xs=False):
    p = Prog("mamba1")
    xnT_in = p.din("xn1nT_full", [D, NTOK])
    winT_in = p.din("winT_tiled", [P, 8, 512])
    convw_in = p.din("convw_tiled", [P, 2, 4])
    convb_in = p.din("convb_tiled", [P, 2])
    wxT_in = p.din("wxT_tiled", [P, 2, 96])
    u_out = p.dout("u_sl", [256, NTOK])
    z_out = p.dout("zsil_sl", [256, NTOK])
    xd_out = p.dout("xdbl_part", [96, NTOK])

    PADW = T + 3
    winTb = p.sb("winTb", [P, 8 * 512], BF16)
    wxTb = p.sb("wxTb", [P, 2 * 96], BF16)
    convw = p.sb("convw", [P, 2 * 4])
    convb = p.sb("convb", [P, 2])
    xnTb = p.sb("xnTb", [P, 8 * 512], BF16)
    xs_pad = p.sb("xs_pad", [P, 2 * PADW])
    xca = p.sb("xca", [P, T])
    xcb = p.sb("xcb", [P, T])
    zch = p.sb("zch", [P, 512])
    ub = p.sb("ub", [P, 2 * 512], BF16)
    xde = p.sb("xde", [P, 512])
    pst = [p.ps(f"pst{i}", [P, 512]) for i in range(2)]

    w1 = p.dma("gpsimd", winTb[:].rearrange("p (k c) -> p k c", k=8), winT_in, inc="ldw")
    w2 = p.dma("gpsimd", wxTb[:].rearrange("p (k c) -> p k c", k=2), wxT_in, wait=[w1], inc="ldw")
    c1 = p.dma("sync", convw[:].rearrange("p (t k) -> p t k", t=2), convw_in, inc="ldc")
    c2 = p.dma("sync", convb[:], convb_in, wait=[c1], inc="ldc")

    xnT_v = xnT_in.rearrange("(k p) t -> p k t", p=P)
    mpad = p.op("vector", lambda e: e.memset(xs_pad[:], 0.0))

    ev_last = {0: None, 1: None}
    prev_mm = None
    last_u_store = None
    last_z = None
    for et in range(2):
        for tc in range(32):
            b, tb = tc // 16, (tc % 16) * 512
            t0 = tc * 512
            lx = p.dma("gpsimd", xnTb[:].rearrange("p (k c) -> p k c", k=8),
                       xnT_v[:, :, t0:t0 + 512], wait=[prev_mm], inc="ldx")
            mms = [(winTb[:, k * 512 + et * P:k * 512 + (et + 1) * P], xnTb[:, k * 512:(k + 1) * 512],
                    [lx, w2]) for k in range(8)]
            tokx = mm_group(p, pst[0][:], mms, psum_free=ev_last[0])
            exs = p.op("vector", lambda e, b=b, tb=tb: e.tensor_copy(
                xs_pad[:, b * PADW + 3 + tb: b * PADW + 3 + tb + 512], pst[0][:]),
                wait=[tokx, mpad])
            ev_last[0] = exs
            mms = [(winTb[:, k * 512 + (2 + et) * P:k * 512 + (3 + et) * P], xnTb[:, k * 512:(k + 1) * 512],
                    [lx, w2]) for k in range(8)]
            tokz = mm_group(p, pst[1][:], mms, psum_free=ev_last[1])
            prev_mm = tokz
            ez = p.op("scalar", lambda e: e.activation(zch[:], pst[1][:], AF.Silu),
                      wait=[tokz, last_z])
            ev_last[1] = ez
            last_z = p.dma("sync", z_out[et * P:(et + 1) * P, t0:t0 + 512], zch[:],
                           wait=[ez], inc="stz")
        dbg_tok = None
        if debug_xs and et == 0:
            xs_dbg = p.dout("xs_dbg", [P, 2 * PADW])
            dbg_tok = p.dma("sync", xs_dbg, xs_pad[:], wait=[exs], inc="stdbg")
        for b in range(2):
            off = b * PADW
            cv0 = p.op("vector", lambda e, off=off, et=et: e.tensor_scalar(
                xca[:], xs_pad[:, off:off + T], convw[:, 4 * et:4 * et + 1],
                convb[:, et:et + 1], ALU.mult, ALU.add), wait=[exs, c2, last_u_store])
            cv1 = p.op("vector", lambda e, off=off, et=et: e.scalar_tensor_tensor(
                xcb[:], xs_pad[:, off + 1:off + 1 + T], convw[:, 4 * et + 1:4 * et + 2],
                xca[:], ALU.mult, ALU.add))
            cv2 = p.op("vector", lambda e, off=off, et=et: e.scalar_tensor_tensor(
                xca[:], xs_pad[:, off + 2:off + 2 + T], convw[:, 4 * et + 2:4 * et + 3],
                xcb[:], ALU.mult, ALU.add))
            cv3 = p.op("vector", lambda e, off=off, et=et: e.scalar_tensor_tensor(
                xcb[:], xs_pad[:, off + 3:off + 3 + T], convw[:, 4 * et + 3:4 * et + 4],
                xca[:], ALU.mult, ALU.add))
            cu = p.op("scalar", lambda e: e.activation(xca[:], xcb[:], AF.Silu), wait=[cv3])
            last_u_store = p.dma("sync", u_out[et * P:(et + 1) * P, b * T:(b + 1) * T], xca[:],
                                 wait=[cu], inc="stu")
        if et == 0:
            mpad = p.op("vector", lambda e: e.memset(xs_pad[:], 0.0),
                        wait=[cv3, last_u_store, dbg_tok])
    prev_ev = None
    last_xd = None
    for tc in range(32):
        t0 = tc * 512
        lu = p.dma("gpsimd", ub[:].rearrange("p (t c) -> p t c", t=2),
                   u_out.rearrange("(t p) n -> p t n", p=P)[:, :, t0:t0 + 512],
                   wait=[last_u_store, prev_ev], inc="ldu")
        mms = [(wxTb[:, k * 96:(k + 1) * 96], ub[:, k * 512:(k + 1) * 512], [lu, w2])
               for k in range(2)]
        tok = mm_group(p, pst[0][:96, :], mms, psum_free=prev_ev)
        prev_ev = p.op("vector", lambda e: e.tensor_copy(xde[:96, :], pst[0][:96, :]),
                       wait=[tok, last_xd])
        last_xd = p.dma("sync", xd_out[:, t0:t0 + 512], xde[:96, :], wait=[prev_ev], inc="stx")
    p.barrier("sync", [last_xd, last_u_store, last_z])
    return p.finalize()


def build_mamba2():
    p = Prog("mamba2")
    xdp_in = p.din("xdbl_parts", [NCORES, 96, NTOK])
    wdtT_in = p.din("wdtT_sl", [64, 256])
    bdt_in = p.din("bdt_tiled", [P, 2])
    alog_in = p.din("alog_tiled", [P, 2, 16])
    dskip_in = p.din("dskip_tiled", [P, 2])
    u_in = p.din("u_sl", [256, NTOK])
    z_in = p.din("zsil_sl", [256, NTOK])
    y_out = p.dout("y1_sl", [256, NTOK])
    bc_dram = p.dtmp("bc_dram", [32, T])

    TQ = 1024
    NTQ = T // TQ
    xdbl = p.sb("xdbl", [96, T])
    xdp = [p.sb(f"xdp{i}", [96, 2048]) for i in range(2)]
    wdtT = p.sb("wdtT", [64, 256])
    bdt = p.sb("bdt", [P, 2])
    A_sb = p.sb("A_sb", [P, 32])
    dskip = p.sb("dskip", [P, 2])
    neg1 = p.sb("neg1", [P, 1])
    one1 = p.sb("one1", [P, 1])
    t_abs = p.sb("t_abs", [P, 512])
    t_e = p.sb("t_e", [P, 512])
    t_l = p.sb("t_l", [P, 512])
    t_r = p.sb("t_r", [P, 512])
    dt2 = p.sb("dt2", [P, 2 * TQ])
    du2 = p.sb("du2", [P, 2 * TQ])
    u2 = p.sb("u2", [P, 2 * TQ])
    z2 = p.sb("z2", [P, 2 * TQ])
    bb = [p.sb(f"bb{i}", [P, TQ]) for i in range(2)]
    cc = [p.sb(f"cc{i}", [P, TQ]) for i in range(2)]
    dAt = p.sb("dAt", [P, 2 * TQ])
    dBt = p.sb("dBt", [P, TQ])
    ht = p.sb("ht", [P, TQ])
    yb2 = p.sb("yb2", [P, 2 * TQ])
    carry = p.sb("carry", [P, 32])
    pst = [p.ps(f"pst{i}", [P, 512]) for i in range(2)]

    m1 = p.op("vector", lambda e: e.memset(neg1[:], -1.0))
    m2 = p.op("vector", lambda e: e.memset(one1[:], 1.0))
    lw = p.dma("sync", wdtT[:], wdtT_in, inc="ldw")
    l1 = p.dma("sync", bdt[:], bdt_in, inc="ldc")
    l2 = p.dma("sync", A_sb[:], alog_in.rearrange("p a n -> p (a n)"), wait=[l1], inc="ldc")
    l3 = p.dma("sync", dskip[:], dskip_in, wait=[l2], inc="ldc")
    sA = p.op("scalar", lambda e: e.activation(A_sb[:], A_sb[:], AF.Exp), wait=[l2])
    sA = p.op("vector", lambda e: e.tensor_scalar(A_sb[:], A_sb[:], -1.0, None, ALU.mult), wait=[sA])

    last = {}
    ev_last = {0: None, 1: None}
    for b in range(2):
        sum_chunk = {}
        bc_store = {}
        prev = {0: last.get("xdp0"), 1: last.get("xdp1")}
        for tc in range(4):
            t0 = tc * 2048
            sum_tok = None
            for j in range(NCORES):
                q = "sync" if j % 2 == 0 else "gpsimd"
                lb = p.dma(q, xdp[j % 2][:], xdp_in[j, :, b * T + t0:b * T + t0 + 2048],
                           wait=[prev[j % 2]], inc="ldp%d" % (j % 2))
                if j == 0:
                    sum_tok = p.op("vector", lambda e, t0=t0: e.tensor_copy(
                        xdbl[:, t0:t0 + 2048], xdp[0][:]),
                        wait=[lb, last.get("xrd%d" % tc)])
                else:
                    sum_tok = p.op("vector", lambda e, t0=t0, j=j: e.tensor_tensor(
                        xdbl[:, t0:t0 + 2048], xdbl[:, t0:t0 + 2048], xdp[j % 2][:], ALU.add),
                        wait=[lb])
                prev[j % 2] = sum_tok
            sum_chunk[tc] = sum_tok
            bc_store[tc] = p.dma("sync", bc_dram[:, t0:t0 + 2048], xdbl[64:96, t0:t0 + 2048],
                                 wait=[sum_tok, last.get("bcrd%d" % tc)],
                                 inc="stbc%d" % tc)
        last["xdp0"], last["xdp1"] = prev[0], prev[1]

        cm = p.op("vector", lambda e: e.memset(carry[:], 0.0), wait=[last.get("ce")])
        for tq in range(NTQ):
            tl0 = tq * TQ
            t0g = b * T + tl0
            tc = tq // 2
            lus, lzs = [], []
            for et in range(2):
                lus.append(p.dma("sync", u2[:, et * TQ:(et + 1) * TQ],
                                 u_in[et * P:(et + 1) * P, t0g:t0g + TQ],
                                 wait=[last.get("y2_%d" % et)], inc="ldu%d" % et))
                lzs.append(p.dma("gpsimd", z2[:, et * TQ:(et + 1) * TQ],
                                 z_in[et * P:(et + 1) * P, t0g:t0g + TQ],
                                 wait=[last.get("y3_%d" % et)], inc="ldz%d" % et))
            cdus = []
            for et in range(2):
                for h in range(2):
                    pb = pst[h % 2]
                    tok = mm_group(p, pb[:],
                                   [(wdtT[:, et * P:(et + 1) * P],
                                     xdbl[0:64, tl0 + h * 512:tl0 + (h + 1) * 512],
                                     [lw, sum_chunk[tc], last.get("dtp%d" % h)])],
                                   psum_free=ev_last[h])
                    cdtp = p.op("vector", lambda e, pb=pb, et=et: e.tensor_scalar(
                        t_abs[:], pb[:], bdt[:, et:et + 1], None, ALU.add),
                        wait=[tok, m1, m2, l1, last.get("tabs")])
                    ev_last[h] = cdtp
                    last["dtp%d" % h] = cdtp
                    s4 = p.op("scalar", lambda e: e.activation(t_r[:], t_abs[:], AF.Relu),
                              wait=[cdtp])
                    s1 = p.op("scalar", lambda e: e.activation(t_abs[:], t_abs[:], AF.Abs))
                    s2 = p.op("scalar", lambda e: e.activation(t_e[:], t_abs[:], AF.Exp,
                                                               scale=neg1[:, 0:1]))
                    s3 = p.op("scalar", lambda e: e.activation(t_l[:], t_e[:], AF.Ln,
                                                               bias=one1[:, 0:1]))
                    last["tabs"] = s3
                    last["dt_%d" % et] = p.op("vector", lambda e, et=et, h=h: e.tensor_tensor(
                        dt2[:, et * TQ + h * 512:et * TQ + (h + 1) * 512], t_l[:], t_r[:],
                        ALU.add), wait=[s3, s4])
                cdus.append(p.op("vector", lambda e, et=et: e.tensor_tensor(
                    du2[:, et * TQ:(et + 1) * TQ], dt2[:, et * TQ:(et + 1) * TQ],
                    u2[:, et * TQ:(et + 1) * TQ], ALU.mult), wait=[lus[et]]))
            for n in range(16):
                lbB = p.dma("sync", bb[n % 2][:],
                            bc_dram[n:n + 1, tl0:tl0 + TQ].rearrange("a b -> (a b)")
                            .unsqueeze(0).broadcast_to([P, TQ]),
                            wait=[bc_store[tc], last.get("cB%d" % (n % 2))], inc="ldbB%d" % (n % 2))
                lbC = p.dma("gpsimd", cc[n % 2][:],
                            bc_dram[16 + n:17 + n, tl0:tl0 + TQ].rearrange("a b -> (a b)")
                            .unsqueeze(0).broadcast_to([P, TQ]),
                            wait=[bc_store[tc], last.get("cC%d" % (n % 2))], inc="ldbC%d" % (n % 2))
                for et in range(2):
                    idx = et * 16 + n
                    sdA = p.op("scalar", lambda e, et=et, idx=idx: e.activation(
                        dAt[:, et * TQ:(et + 1) * TQ], dt2[:, et * TQ:(et + 1) * TQ], AF.Exp,
                        scale=A_sb[:, idx:idx + 1]),
                        wait=[sA, last.get("scan%d" % et), last.get("dt_%d" % et)])
                    cdBu = p.op("vector", lambda e, et=et, n=n: e.tensor_tensor(
                        dBt[:], du2[:, et * TQ:(et + 1) * TQ], bb[n % 2][:], ALU.mult),
                        wait=[lbB, cdus[et]])
                    if et == 1:
                        last["cB%d" % (n % 2)] = cdBu
                    sc = p.op("vector", lambda e, et=et, idx=idx: e.tensor_tensor_scan(
                        ht[:], dAt[:, et * TQ:(et + 1) * TQ], dBt[:],
                        carry[:, idx:idx + 1], ALU.mult, ALU.add), wait=[sdA, cm])
                    last["scan%d" % et] = sc
                    ce = p.op("vector", lambda e, idx=idx: e.tensor_copy(
                        carry[:, idx:idx + 1], ht[:, TQ - 1:TQ]))
                    last["ce"] = ce
                    chC = p.op("vector", lambda e, n=n: e.tensor_tensor(ht[:], ht[:], cc[n % 2][:],
                                                                        ALU.mult), wait=[lbC])
                    if et == 1:
                        last["cC%d" % (n % 2)] = chC
                    if n == 0:
                        p.op("vector", lambda e, et=et: e.tensor_copy(
                            yb2[:, et * TQ:(et + 1) * TQ], ht[:]),
                            wait=[last.get("sty%d" % et)])
                    else:
                        p.op("vector", lambda e, et=et: e.tensor_tensor(
                            yb2[:, et * TQ:(et + 1) * TQ], yb2[:, et * TQ:(et + 1) * TQ],
                            ht[:], ALU.add))
            for et in range(2):
                y2 = p.op("vector", lambda e, et=et: e.scalar_tensor_tensor(
                    yb2[:, et * TQ:(et + 1) * TQ], u2[:, et * TQ:(et + 1) * TQ],
                    dskip[:, et:et + 1], yb2[:, et * TQ:(et + 1) * TQ], ALU.mult, ALU.add),
                    wait=[l3])
                last["y2_%d" % et] = y2
                y3 = p.op("vector", lambda e, et=et: e.tensor_tensor(
                    yb2[:, et * TQ:(et + 1) * TQ], yb2[:, et * TQ:(et + 1) * TQ],
                    z2[:, et * TQ:(et + 1) * TQ], ALU.mult), wait=[lzs[et]])
                last["y3_%d" % et] = y3
                sty = p.dma("sync", y_out[et * P:(et + 1) * P, t0g:t0g + TQ],
                            yb2[:, et * TQ:(et + 1) * TQ],
                            wait=[y3, last.get("styq%d" % et)], inc="sty%d" % et)
                last["sty%d" % et] = sty
                last["styq%d" % et] = sty
            if tq % 2 == 1:
                last["xrd%d" % tc] = last["dtp1"]
                last["bcrd%d" % tc] = last["cC1"]
    p.barrier("sync", [last["sty0"], last["sty1"]])
    return p.finalize()


def build_l1ffn_iter(first):
    p = Prog("l1ffn_iter")
    xcT_in = p.din("xcT_prev", [D, TS])
    if not first:
        xnT_in = p.din("xnewT_prev", [D, TS])
        done_in = p.din("done_rep", [P, 1])
    wmT_in = p.din("wmT_full", [D, 12288])
    wpT_in = p.din("wpT_full", [12288, D])
    mask_in = p.din("mask_packed", [P, 96])
    xnew_out = p.dout("xnewT_sl", [D, TS])
    if not first:
        xcu_out = p.dout("xcT_used", [D, TS])
    pooled_out = p.dout("pooled_part", [P, 8])

    KT, MT = 8, 96
    M_ACT = list(range(32)) if first else list(range(MT))
    xcTb = p.sb("xcTb", [P, KT * TS], BF16)
    Hb = p.sb("Hb", [P, MT * 512], BF16)
    wmb = [p.sb(f"wmb{i}", [P, KT * P], BF16) for i in range(2)]
    wpb = [p.sb(f"wpb{i}", [P, D], BF16) for i in range(2)]
    mask = p.sb("mask", [P, 96])
    xn_ev = p.sb("xn_ev", [P, 512])
    pl_acc = p.sb("pl_acc", [P, 8])
    pl_tmp = p.sb("pl_tmp", [P, 1])
    xcf = p.sb("xcf", [P, TS])
    xnf = p.sb("xnf", [P, TS])
    done_sb = p.sb("done_sb", [P, 1])
    omd_sb = p.sb("omd_sb", [P, 1])
    psx = [p.ps(f"psx{i}", [P, 512]) for i in range(8)]
    psh = [psx[6], psx[7]]

    lm = p.dma("sync", mask[:], mask_in, inc="ldm")
    mz = p.op("vector", lambda e: e.memset(pl_acc[:], 0.0))

    if first:
        xg = None
        for q in range(2):
            xg = p.dma("gpsimd", xcTb[:].rearrange("p (k t) -> p k t", k=KT)[:, 4 * q:4 * q + 4],
                       xcT_in.rearrange("(k p) t -> p k t", p=P)[:, 4 * q:4 * q + 4],
                       wait=[xg], inc="ldx")
        blend_tok = xg
    else:
        ld = p.dma("sync", done_sb[:], done_in, inc="ldd")
        cdn = p.op("vector", lambda e: e.tensor_scalar(omd_sb[:], done_sb[:], -1.0, 1.0,
                                                       ALU.mult, ALU.add), wait=[ld])
        prev_st = {}
        xcf_free = []
        for k in range(KT):
            l1 = p.dma("sync", xcf[:], xcT_in[k * P:(k + 1) * P, :],
                       wait=xcf_free, inc="ldc1")
            l2 = p.dma("gpsimd", xnf[:], xnT_in[k * P:(k + 1) * P, :],
                       wait=[prev_st.get("n"), l1], inc="ldc2")
            c1 = p.op("vector", lambda e: e.tensor_scalar(xcf[:], xcf[:], done_sb[:, 0:1], None, ALU.mult),
                      wait=[l1, cdn])
            c2 = p.op("vector", lambda e: e.scalar_tensor_tensor(
                xcf[:], xnf[:], omd_sb[:, 0:1], xcf[:], ALU.mult, ALU.add), wait=[l2])
            s1 = p.dma("sync", xcu_out[k * P:(k + 1) * P, :], xcf[:], wait=[c2], inc="stc%d" % (k % 2))
            cb = p.op("vector", lambda e, k=k: e.tensor_copy(xcTb[:, k * TS:(k + 1) * TS], xcf[:]))
            xcf_free = [cb, s1]
            prev_st["n"] = c2
            blend_tok = cb
            prev_st[k % 2] = s1
        p.barrier("sync", [prev_st[0], prev_st[1]])

    wm_v = wmT_in.rearrange("(k p) f -> p k f", p=P)
    ev_h = {0: None, 1: None}
    ev_x = {i: None for i in range(8)}
    prev_w = {0: None, 1: None}
    prev_wp = {0: None, 1: None}
    last_px = None
    last_xst = {}
    for g in range(4):
        t0 = g * 512
        hev = None
        for m in M_ACT:
            wb = wmb[m % 2]
            lw = p.dma("gpsimd", wb[:].rearrange("p (k c) -> p k c", k=KT)[:, 0:4],
                       wm_v[:, 0:4, m * P:(m + 1) * P], wait=[prev_w[m % 2]], inc="ldw%d" % (m % 2))
            lw2 = p.dma("gpsimd", wb[:].rearrange("p (k c) -> p k c", k=KT)[:, 4:8],
                        wm_v[:, 4:8, m * P:(m + 1) * P], wait=[prev_w[m % 2]], inc="lds%d" % (m % 2))
            pb = psh[m % 2]
            mms = [(wb[:, k * P:(k + 1) * P], xcTb[:, k * TS + t0:k * TS + t0 + 512],
                    [lw, lw2, blend_tok]) for k in range(KT)]
            tok = mm_group(p, pb[:], mms, psum_free=ev_h[m % 2])
            prev_w[m % 2] = tok
            hev = p.op("scalar", lambda e, pb=pb, m=m: e.activation(
                Hb[:, m * 512:(m + 1) * 512], pb[:], AF.Relu, scale=mask[:, m:m + 1]),
                wait=[tok, lm])
            ev_h[m % 2] = hev
        for m in M_ACT:
            wb = wpb[m % 2]
            lwp = p.dma("gpsimd", wb[:, 0:512], wpT_in[m * P:(m + 1) * P, 0:512],
                        wait=[prev_wp[m % 2]], inc="ldp%d" % (m % 2))
            lwp2 = p.dma("gpsimd", wb[:, 512:1024], wpT_in[m * P:(m + 1) * P, 512:1024],
                         wait=[prev_wp[m % 2]], inc="ldq%d" % (m % 2))
            for dt in range(8):
                tok = p.op("tensor",
                           (lambda e, m=m, dt=dt, wb=wb, s=(m == M_ACT[0]), t=(m == M_ACT[-1]):
                            e.matmul(psx[dt][:], wb[:, dt * P:(dt + 1) * P],
                                     Hb[:, m * 512:(m + 1) * 512], start=s, stop=t)),
                           wait=([lwp, lwp2, hev] if dt == 0 else [lwp, lwp2]) +
                                ([ev_x[dt]] if m == M_ACT[0] else []),
                           inc="pe" if (dt == 7 or m == M_ACT[-1]) else None, chain=False)
            prev_wp[m % 2] = tok
        xg_toks = []
        for dt in range(8):
            ev = p.op("vector", lambda e, dt=dt: e.tensor_copy(xn_ev[:], psx[dt][:]),
                      wait=[tok, last_px])
            ev_x[dt] = ev
            st = p.dma("sync", xnew_out[dt * P:(dt + 1) * P, t0:t0 + 512], xn_ev[:],
                       wait=[ev], inc="stx%d" % (dt % 2))
            last_px = st
            last_xst[dt % 2] = st
            c1 = p.op("vector", lambda e: e.tensor_reduce(pl_tmp[:], xn_ev[:], AX.X, ALU.add),
                      wait=[mz])
            xg_toks.append(p.op("vector", lambda e, dt=dt: e.tensor_tensor(
                pl_acc[:, dt:dt + 1], pl_acc[:, dt:dt + 1], pl_tmp[:], ALU.add)))
        ev_h[0], ev_h[1] = ev_x[6], ev_x[7]
    pst_tok = p.dma("sync", pooled_out, pl_acc[:], wait=[xg_toks[-1]], inc="stp")
    p.barrier("sync", [pst_tok, last_xst[0], last_xst[1]])
    return p.finalize()


def build_l1ffn_ctl():
    p = Prog("l1ffn_ctl")
    parts_in = p.din("pooled_parts", [NCORES, P, 8])
    wc1T_in = p.din("wc1T_tiled", [P, 8, 256])
    wc2T_in = p.din("wc2T_tiled", [P, 2, 12288])
    hist_in = p.din("history", [P, 192])
    act_in = p.din("active", [P, 192])
    done_in = p.din("done_rep", [P, 1])
    act_out = p.dout("active_next", [P, 192])
    hist_out = p.dout("history_next", [P, 192])
    done_out = p.dout("done_next", [P, 1])

    prt = p.sb("prt", [P, 8])
    pl = p.sb("pl", [P, 16])
    wc1 = p.sb("wc1", [P, 8 * 256], BF16)
    wc2 = p.sb("wc2", [P, 2 * 12288], BF16)
    r_sb = p.sb("r_sb", [P, 4], BF16)
    ck = p.sb("ck", [P, 192])
    hist = p.sb("hist", [P, 192])
    act = p.sb("act", [P, 192])
    done = p.sb("done", [P, 1])
    omd = p.sb("omd", [P, 1])
    lo = p.sb("lo", [P, 2])
    hi = p.sb("hi", [P, 2])
    mid = p.sb("mid", [P, 2])
    ge = p.sb("ge", [P, 192])
    rs = p.sb("rs", [P, 2])
    pred = p.sb("pred", [P, 2])
    omp = p.sb("omp", [P, 2])
    tq = p.sb("tq", [P, 2])
    nov = p.sb("nov", [P, 1])
    ps1 = p.ps("ps1", [P, 8])
    ps2 = p.ps("ps2", [P, 8])

    lws = p.dma("gpsimd", wc1[:].rearrange("p (k c) -> p k c", k=8), wc1T_in, inc="lw1")
    lw2 = p.dma("gpsimd", wc2[:].rearrange("p (k c) -> p k c", k=2), wc2T_in, inc="lw2")
    lh = p.dma("sync", hist[:], hist_in, inc="ldh")
    la = p.dma("sync", act[:], act_in, wait=[lh], inc="ldh")
    ldn = p.dma("sync", done[:], done_in, wait=[la], inc="ldh")

    toks = []
    prev = None
    for j in range(NCORES):
        b = j // 4
        lp = p.dma("sync", prt[:], parts_in[j], wait=[prev], inc="ldp")
        if j % 4 == 0:
            tok = p.op("vector", lambda e, b=b: e.tensor_copy(
                pl[:].rearrange("p (dt b2) -> p dt b2", b2=2)[:, :, b], prt[:]), wait=[lp])
        else:
            tok = p.op("vector", lambda e, b=b: e.tensor_tensor(
                pl[:].rearrange("p (dt b2) -> p dt b2", b2=2)[:, :, b],
                pl[:].rearrange("p (dt b2) -> p dt b2", b2=2)[:, :, b], prt[:], ALU.add),
                wait=[lp])
        prev = tok
    cpl = p.op("vector", lambda e: e.tensor_scalar(pl[:], pl[:], 1.0 / T, None, ALU.mult))
    plb = p.sb("plb", [P, 16], BF16)
    cpl = p.op("vector", lambda e: e.tensor_copy(plb[:], pl[:]), wait=[cpl])
    rtoks = []
    for mt in range(2):
        mms = [(wc1[:, (k * 256) + mt * P:(k * 256) + (mt + 1) * P], plb[:, k * 2:(k + 1) * 2], [lws, cpl])
               for k in range(8)]
        tok = mm_group(p, ps1[:, 0:2], mms, psum_free=rtoks[-1] if rtoks else None)
        rtoks.append(p.op("scalar", lambda e, mt=mt: e.activation(
            r_sb[:, mt * 2:(mt + 1) * 2], ps1[:, 0:2], AF.Relu), wait=[tok]))
    cktoks = []
    for m in range(96):
        mms = [(wc2[:, k * 12288 + m * P:k * 12288 + (m + 1) * P], r_sb[:, k * 2:(k + 1) * 2],
                [lw2, rtoks[-1]]) for k in range(2)]
        tok = mm_group(p, ps2[:, 0:2], mms, psum_free=cktoks[-1] if cktoks else None)
        cktoks.append(p.op("vector", lambda e, m=m: e.tensor_copy(
            ck[:, m * 2:(m + 1) * 2], ps2[:, 0:2]), wait=[tok]))
    ck_done = cktoks[-1]

    ckv = ck[:].rearrange("p (m b2) -> p b2 m", b2=2)
    ones = p.sb("ones", [P, P])
    lohi = p.sb("lohi", [P, 4])
    lohi1 = p.sb("lohi1", [1, 4])
    lh_dram = p.dtmp("lh_dram", [P, 4])
    psr = p.ps("psr", [P, 4])
    mon = p.op("vector", lambda e: e.memset(ones[:], 1.0))

    def part_sum(src_ap, dst_ap, w):
        n = src_ap.shape[-1]
        t1 = p.op("tensor", lambda e: e.matmul(psr[0:1, 0:n], ones[:, 0:1], src_ap, start=True, stop=True),
                  wait=list(w) + [mon, part_sum.free], inc="pe", chain=False)
        t2 = p.op("vector", lambda e: e.tensor_copy(lohi1[0:1, 0:n], psr[0:1, 0:n]), wait=[t1])
        t3 = p.op("tensor", lambda e: e.matmul(psr[:, 0:n], ones[0:1, :], lohi1[0:1, 0:n], start=True, stop=True),
                  wait=[t2], inc="pe", chain=False)
        t4 = p.op("vector", lambda e: e.tensor_copy(dst_ap, psr[:, 0:n]), wait=[t3])
        part_sum.free = t4
        return t4
    part_sum.free = None

    flat512 = p.sb("flat512", [1, 512])
    lh2_dram = p.dtmp("lh2_dram", [1, 4])
    c1 = p.op("vector", lambda e: e.tensor_reduce(lohi[:, 0:2], ckv, AX.X, ALU.min), wait=[ck_done])
    c2 = p.op("vector", lambda e: e.tensor_reduce(lohi[:, 2:4], ckv, AX.X, ALU.max))
    d1 = p.dma("sync", lh_dram, lohi[:], wait=[c2], inc="lhd")
    d2 = p.dma("sync", flat512[0:1, :], lh_dram.rearrange("p c -> (p c)").unsqueeze(0),
               wait=[d1], inc="lhd")
    fv = flat512[0:1, :].rearrange("q (pp c) -> q c pp", c=4)
    c3 = p.op("vector", lambda e: e.tensor_reduce(lohi1[0:1, 0:2], fv[:, 0:2, :], AX.X, ALU.min),
              wait=[d2])
    c4 = p.op("vector", lambda e: e.tensor_reduce(lohi1[0:1, 2:4], fv[:, 2:4, :], AX.X, ALU.max))
    d3 = p.dma("sync", lh2_dram, lohi1[0:1, :], wait=[c4], inc="lhd")
    d4 = p.dma("sync", lohi[:], lh2_dram.broadcast_to([P, 4]), wait=[d3], inc="lhd")
    c5 = p.op("vector", lambda e: e.tensor_copy(lo[:], lohi[:, 0:2]), wait=[d4])
    c6 = p.op("vector", lambda e: e.tensor_copy(hi[:], lohi[:, 2:4]))
    ctok = c6
    for it in range(42):
        cm = p.op("vector", lambda e: e.tensor_tensor(mid[:], lo[:], hi[:], ALU.add), wait=[ctok])
        cm = p.op("vector", lambda e: e.tensor_scalar(mid[:], mid[:], 0.5, None, ALU.mult))
        cg = p.op("vector", lambda e: e.tensor_tensor(
            ge[:].rearrange("p (m b2) -> p b2 m", b2=2), ckv,
            mid[:].unsqueeze(2).broadcast_to([P, 2, 96]), ALU.is_ge))
        cr = p.op("vector", lambda e: e.tensor_reduce(rs[:], ge[:].rearrange("p (m b2) -> p b2 m", b2=2),
                                                      AX.X, ALU.add))
        cgp = part_sum(rs[:, 0:2], rs[:, 0:2], [cr])
        cp = p.op("vector", lambda e: e.tensor_scalar(pred[:], rs[:], 4096.0, None, ALU.is_ge), wait=[cgp])
        co = p.op("vector", lambda e: e.tensor_scalar(omp[:], pred[:], -1.0, 1.0, ALU.mult, ALU.add))
        u1 = p.op("vector", lambda e: e.tensor_tensor(tq[:], mid[:], lo[:], ALU.subtract))
        u2 = p.op("vector", lambda e: e.tensor_tensor(tq[:], tq[:], pred[:], ALU.mult))
        u3 = p.op("vector", lambda e: e.tensor_tensor(lo[:], lo[:], tq[:], ALU.add))
        u4 = p.op("vector", lambda e: e.tensor_tensor(tq[:], mid[:], hi[:], ALU.subtract))
        u5 = p.op("vector", lambda e: e.tensor_tensor(tq[:], tq[:], omp[:], ALU.mult))
        ctok = p.op("vector", lambda e: e.tensor_tensor(hi[:], hi[:], tq[:], ALU.add))
    cmask = p.op("vector", lambda e: e.tensor_tensor(
        ge[:].rearrange("p (m b2) -> p b2 m", b2=2), ckv,
        lo[:].unsqueeze(2).broadcast_to([P, 2, 96]), ALU.is_ge), wait=[ctok])
    comb = ck
    cc1 = p.op("vector", lambda e: e.tensor_tensor(comb[:], hist[:], ge[:], ALU.add), wait=[cmask, la])
    cc2 = p.op("vector", lambda e: e.tensor_single_scalar(comb[:], comb[:], 1.0, ALU.min))
    dif = p.sb("dif", [P, 192])
    cd1 = p.op("vector", lambda e: e.tensor_tensor(dif[:], comb[:], hist[:], ALU.subtract))
    cd2 = p.op("vector", lambda e: e.tensor_reduce(rs[:], dif[:].rearrange("p (m b2) -> p b2 m", b2=2),
                                                   AX.X, ALU.add))
    cd3 = part_sum(rs[:, 0:2], rs[:, 0:2], [cd2])
    cd4 = p.op("vector", lambda e: e.tensor_reduce(nov[:], rs[:], AX.X, ALU.add), wait=[cd3])
    cd5 = p.op("vector", lambda e: e.tensor_scalar(nov[:], nov[:], 1.0 / 8192.0, None, ALU.mult))
    cd6 = p.op("vector", lambda e: e.tensor_scalar(nov[:], nov[:], 0.7, None, ALU.is_lt))
    cdn = p.op("vector", lambda e: e.tensor_tensor(nov[:], done[:], nov[:], ALU.max), wait=[ldn])
    sd = p.dma("sync", done_out, nov[:], wait=[cdn], inc="std")
    co = p.op("vector", lambda e: e.tensor_scalar(omd[:], done[:], -1.0, 1.0, ALU.mult, ALU.add))
    b1 = p.op("vector", lambda e: e.tensor_scalar(act[:], act[:], done[:, 0:1], None, ALU.mult))
    b2 = p.op("vector", lambda e: e.scalar_tensor_tensor(act[:], ge[:], omd[:, 0:1], act[:],
                                                         ALU.mult, ALU.add), wait=[cmask])
    sa = p.dma("sync", act_out, act[:], wait=[b2], inc="sta")
    b3 = p.op("vector", lambda e: e.tensor_scalar(hist[:], hist[:], done[:, 0:1], None, ALU.mult))
    b4 = p.op("vector", lambda e: e.scalar_tensor_tensor(hist[:], comb[:], omd[:, 0:1], hist[:],
                                                         ALU.mult, ALU.add))
    sh2 = p.dma("sync", hist_out, hist[:], wait=[b4], inc="sth")
    p.barrier("sync", [sd, sa, sh2])
    return p.finalize()


def build_final():
    p = Prog("final")
    y1T_in = p.din("y1T_sl", [2048, TS])
    woT_in = p.din("wo1T_tiled", [P, 16, D])
    xn1_in = p.din("xn1_sl", [TS, D])
    x1_in = p.din("x1_sl", [TS, D])
    xc3_in = p.din("xc3_sl", [TS, D])
    xn3_in = p.din("xnew3_sl", [TS, D])
    done_in = p.din("done_rep", [P, 1])
    x2_out = p.dout("x2_sl", [TS, D])

    KT = 16
    y1Tb = p.sb("y1Tb", [P, KT * P], BF16)
    woTb = p.sb("woTb", [P, KT * D], BF16)
    xn1t = p.sb("xn1t", [P, D])
    x1t = p.sb("x1t", [P, D])
    xc3t = p.sb("xc3t", [P, D])
    xn3t = p.sb("xn3t", [P, D])
    s1t = p.sb("s1t", [P, D])
    gt = p.sb("gt", [P, D])
    done_sb = p.sb("done_sb", [P, 1])
    omd_sb = p.sb("omd_sb", [P, 1])
    pst = [p.ps(f"pst{i}", [P, 512]) for i in range(2)]

    w1 = None
    for q in range(2):
        w1 = p.dma("gpsimd", woTb[:].rearrange("p (k c) -> p k c", k=KT)[:, 8 * q:8 * q + 8],
                   woT_in[:, 8 * q:8 * q + 8], wait=[w1], inc="ldw")
    ld = p.dma("sync", done_sb[:], done_in, inc="ldd")
    cdn = p.op("vector", lambda e: e.tensor_scalar(omd_sb[:], done_sb[:], -1.0, 1.0,
                                                   ALU.mult, ALU.add), wait=[ld])
    y1T_v = y1T_in.rearrange("(k p) t -> p k t", p=P)
    ev_last = {0: None, 1: None}
    prev_mm = None
    prev_st = None
    for i in range(TS // P):
        t0 = i * P
        ly = p.dma("gpsimd", y1Tb[:].rearrange("p (k c) -> p k c", k=KT),
                   y1T_v[:, :, t0:t0 + P], wait=[prev_mm], inc="ldy")
        l1 = p.dma("sync", xn1t[:], xn1_in[t0:t0 + P, :], wait=[prev_st], inc="ldt")
        l2 = p.dma("sync", x1t[:], x1_in[t0:t0 + P, :], wait=[l1], inc="ldt")
        l3 = p.dma("sync", xc3t[:], xc3_in[t0:t0 + P, :], wait=[l2], inc="ldt")
        l4 = p.dma("sync", xn3t[:], xn3_in[t0:t0 + P, :], wait=[l3], inc="ldt")
        stoks = []
        for nch in range(2):
            pb = pst[nch]
            mms = [(y1Tb[:, k * P:(k + 1) * P], woTb[:, k * D + nch * 512:k * D + (nch + 1) * 512],
                    [ly, w1]) for k in range(KT)]
            tok = mm_group(p, pb[:], mms, psum_free=ev_last[nch])
            prev_mm = tok
            ev = p.op("vector", lambda e, pb=pb, nch=nch: e.tensor_tensor(
                s1t[:, nch * 512:(nch + 1) * 512], pb[:], xn1t[:, nch * 512:(nch + 1) * 512], ALU.add),
                wait=[tok, l4])
            ev_last[nch] = ev
            stoks.append(ev)
        p.op("vector", lambda e: e.tensor_scalar(xc3t[:], xc3t[:], done_sb[:, 0:1], None, ALU.mult),
             wait=[cdn, l4])
        cxf = p.op("vector", lambda e: e.scalar_tensor_tensor(
            xc3t[:], xn3t[:], omd_sb[:, 0:1], xc3t[:], ALU.mult, ALU.add))
        cg = p.op("scalar", lambda e: e.activation(gt[:], xc3t[:], AF.Silu), wait=[cxf, prev_st])
        p.op("vector", lambda e: e.tensor_tensor(gt[:], gt[:], s1t[:], ALU.mult), wait=[cg] + stoks)
        cx2 = p.op("vector", lambda e: e.tensor_tensor(gt[:], gt[:], x1t[:], ALU.add))
        prev_st = p.dma("sync", x2_out[t0:t0 + P, :], gt[:], wait=[cx2], inc="st")
    p.barrier("sync", [prev_st])
    return p.finalize()


_BUILD_CACHE = {}


def _get(name, builder, *a):
    key = (name,) + a
    if key not in _BUILD_CACHE:
        _BUILD_CACHE[key] = builder(*a)
    return _BUILD_CACHE[key]


def _tile3(w, kt):
    return np.ascontiguousarray(w.reshape(kt, P, -1).transpose(1, 0, 2))


def _pack32(v):
    return np.ascontiguousarray(np.asarray(v, F).reshape(-1, P).T)


def _packF2(v2):
    return np.ascontiguousarray(v2.reshape(2, 96, P).transpose(2, 1, 0).reshape(P, 192))


def kernel(**inputs):
    inp = {k: np.ascontiguousarray(np.asarray(v, dtype=F)) for k, v in inputs.items()}
    x = inp["L0_attn_x"].reshape(NTOK, D)
    R = list(range(NCORES))

    def run(nc, im):
        return run_bass_kernel_spmd(nc, im, core_ids=R).results

    def sl(c):
        return slice(c * TS, (c + 1) * TS)

    r = run(_get("prep", build_prep),
            [{"x_sl": x[sl(c)], "norm_w": inp["L0_attn_norm_w"][None, :]} for c in R])
    xn = np.concatenate([q["xn_sl"] for q in r], 0)
    h = np.concatenate([q["h_sl"] for q in r], 0)
    hT = np.ascontiguousarray(h.T)

    fold_in = stage_fold_inputs(inp["L0_ffn_up_weights"], inp["L0_ffn_down_weights"],
                                inp["L0_ffn_W_exp"], inp["L0_ffn_b_exp"], inp["L0_ffn_W_con"])
    r = run(_get("fold", build_fold), [fold_in] * NCORES)
    wexpfT, wconf = r[0]["wexpfT"], r[0]["wconf"]
    Wf = np.ascontiguousarray(wexpfT[:, :D].T)
    bf = np.ascontiguousarray(wexpfT[:, D])
    l0ffn_shared = {
        "wexpf_tiled": _tile3(Wf, 8), "wconf_tiled": _tile3(wconf, 32),
        "gbias1_packed": _pack32(bf), "gbias2_packed": _pack32(inp["L0_ffn_up_bias"]),
        "cbias2": np.ascontiguousarray(np.stack([inp["L0_ffn_b_con"], inp["L0_ffn_down_bias"]])),
    }

    r = run(_get("l0ffn", build_l0ffn),
            [dict(l0ffn_shared, xnT_sl=np.ascontiguousarray(xn[sl(c)].T)) for c in R])
    g0 = np.concatenate([q["g0_sl"] for q in r], 0)

    im = []
    for c in R:
        rows = slice(c * 8 * 16 * 17, (c + 1) * 8 * 16 * 17)
        waT = np.ascontiguousarray(inp["L0_attn_W_a"][rows].T)
        wvT = np.ascontiguousarray(inp["L0_attn_W_v"][c * P:(c + 1) * P].T)
        im.append({"hT_full": hT, "waT_tiled": _tile3(waT, 8), "wvT_tiled": _tile3(wvT, 8)})
    r = run(_get("l0attn", build_l0attn), im)
    y0 = np.stack([q["y0_sl"] for q in r])
    yfull = np.ascontiguousarray(
        y0.reshape(NCORES, 2, 8, 8, 1024, 16).transpose(0, 2, 5, 1, 3, 4).reshape(D, NTOK))

    woT_t = _tile3(np.ascontiguousarray(inp["L0_attn_W_out"].T), 8)
    im = [{"yT_sl": np.ascontiguousarray(yfull[:, sl(c)]), "woT_tiled": woT_t,
           "xn_sl": xn[sl(c)], "x_sl": x[sl(c)], "g0_sl": g0[sl(c)],
           "ln_w": inp["L1_attn_ln_w"][None, :], "ln_b": inp["L1_attn_ln_b"][None, :]}
          for c in R]
    r = run(_get("l0out", build_l0out), im)
    x1 = np.concatenate([q["x1_sl"] for q in r], 0)
    xn1 = np.concatenate([q["xn1_sl"] for q in r], 0)
    xn1n = np.concatenate([q["xn1n_sl"] for q in r], 0)
    xn1nT = np.ascontiguousarray(xn1n.T)

    im = []
    for c in R:
        rows = np.r_[256 * c:256 * c + 256, 2048 + 256 * c:2048 + 256 * c + 256]
        winT = np.ascontiguousarray(inp["L1_attn_w_in"][rows].T)
        wxT = np.ascontiguousarray(inp["L1_attn_w_xproj"][:, 256 * c:256 * c + 256].T)
        im.append({
            "xn1nT_full": xn1nT, "winT_tiled": _tile3(winT, 8),
            "convw_tiled": np.ascontiguousarray(
                inp["L1_attn_conv_w"][256 * c:256 * c + 256].reshape(2, P, 4).transpose(1, 0, 2)),
            "convb_tiled": np.ascontiguousarray(
                inp["L1_attn_conv_b"][256 * c:256 * c + 256].reshape(2, P).T),
            "wxT_tiled": np.ascontiguousarray(wxT.reshape(2, P, 96).transpose(1, 0, 2)),
        })
    r = run(_get("mamba1", build_mamba1), im)
    u_full = np.concatenate([q["u_sl"] for q in r], 0)
    z_full = np.concatenate([q["zsil_sl"] for q in r], 0)
    xdbl_parts = np.stack([q["xdbl_part"] for q in r])

    im = []
    for c in R:
        s2 = slice(256 * c, 256 * c + 256)
        im.append({
            "xdbl_parts": xdbl_parts,
            "wdtT_sl": np.ascontiguousarray(inp["L1_attn_w_dt"][s2].T),
            "bdt_tiled": np.ascontiguousarray(inp["L1_attn_b_dt"][s2].reshape(2, P).T),
            "alog_tiled": np.ascontiguousarray(
                inp["L1_attn_a_log"][s2].reshape(2, P, 16).transpose(1, 0, 2)),
            "dskip_tiled": np.ascontiguousarray(inp["L1_attn_d_skip"][s2].reshape(2, P).T),
            "u_sl": np.ascontiguousarray(u_full[s2]), "zsil_sl": np.ascontiguousarray(z_full[s2]),
        })
    r = run(_get("mamba2", build_mamba2), im)
    y1full = np.concatenate([q["y1_sl"] for q in r], 0)

    wmT = np.ascontiguousarray(inp["L1_ffn_Wm"].T)
    wpT = np.ascontiguousarray(inp["L1_ffn_Wp"].T)
    wc1T_t = np.ascontiguousarray(inp["L1_ffn_Wc1"].T.reshape(8, P, 256).transpose(1, 0, 2))
    wc2T_t = np.ascontiguousarray(inp["L1_ffn_Wc2"].T.reshape(2, P, 12288).transpose(1, 0, 2))
    base2 = np.zeros((2, 12288), F)
    base2[:, :4096] = 1.0
    active = _packF2(base2)
    history = _packF2(base2)
    zeros_rep = np.zeros((P, 1), F)
    done_for_blend = [zeros_rep, zeros_rep, None]
    done_ctl_in = zeros_rep
    xcT = {c: np.ascontiguousarray(xn1[sl(c)].T) for c in R}
    xnT = {}
    xcuT = {}
    for it in range(3):
        im = []
        for c in R:
            b = c // 4
            d = {"xcT_prev": xcT[c], "wmT_full": wmT, "wpT_full": wpT,
                 "mask_packed": np.ascontiguousarray(active[:, b::2])}
            if it > 0:
                d["xnewT_prev"] = xnT[c]
                d["done_rep"] = done_for_blend[it]
            im.append(d)
        r = run(_get("l1ffn_iter", build_l1ffn_iter, it == 0), im)
        for c in R:
            xnT[c] = r[c]["xnewT_sl"]
            if it > 0:
                xcuT[c] = r[c]["xcT_used"]
                xcT[c] = xcuT[c]
        if it == 2:
            break
        pooled_parts = np.stack([r[j]["pooled_part"] for j in R])
        cin = {"pooled_parts": pooled_parts, "wc1T_tiled": wc1T_t, "wc2T_tiled": wc2T_t,
               "history": history, "active": active, "done_rep": done_ctl_in}
        rc = run(_get("l1ffn_ctl", build_l1ffn_ctl), [cin] * NCORES)
        active = rc[0]["active_next"]
        history = rc[0]["history_next"]
        done_next = np.ascontiguousarray(rc[0]["done_next"])
        done_for_blend[2] = done_next if it == 0 else done_for_blend[2]
        if it == 1:
            done2 = done_next
        done_ctl_in = done_next

    wo1T_t = _tile3(np.ascontiguousarray(inp["L1_attn_w_out"].T), 16)
    im = [{"y1T_sl": np.ascontiguousarray(y1full[:, sl(c)]), "wo1T_tiled": wo1T_t,
           "xn1_sl": xn1[sl(c)], "x1_sl": x1[sl(c)],
           "xc3_sl": np.ascontiguousarray(xcuT[c].T), "xnew3_sl": np.ascontiguousarray(xnT[c].T),
           "done_rep": done2} for c in R]
    r = run(_get("final", build_final), im)
    x2 = np.concatenate([q["x2_sl"] for q in r], 0)
    return np.ascontiguousarray(x2.reshape(B, T, D))
